# version 10
# baseline (speedup 1.0000x reference)
import os
import numpy as np
import ml_dtypes
import concourse.bass as bass
import concourse.mybir as mybir
from concourse.bass_utils import run_bass_kernel_spmd

F32 = mybir.dt.float32
BF16 = mybir.dt.bfloat16
I32 = mybir.dt.int32
AF = mybir.ActivationFunctionType
ALU = mybir.AluOpType
BF = ml_dtypes.bfloat16
AX = mybir.AxisListType.X

D = 2048
S = 2048
NT = D // 128
IFF = 5632
NI = IFF // 128
CW = 3072
NG = 8
DQK = 64
STAGE = int(os.environ.get("KSTAGE", "5"))
KSUB = int(os.environ.get("KSUB", "9"))
KSUB3 = int(os.environ.get("KSUB3", "9"))
KPROBE = int(os.environ.get("KPROBE", "0"))
NCORE = int(os.environ.get("KCORES", "8"))

LN_EPS = 1e-5
RMS_EPS = 1e-6
QMAX = 127.0
QEPS = 1e-8
L1_EPS = 1e-10
RB = 256.0
LOG_HALF = float(np.log(0.5))


NRR = 6
VTS = 96


class Ctx:
    ENG = ("t", "v", "s", "g", "y")

    def __init__(self):
        self.ops = {e: [] for e in self.ENG}
        self.semnames = (["t", "v", "s", "g"]
                         + [f"y{i}" for i in range(NRR)]
                         + [f"g{i}" for i in range(NRR)]
                         + ["o0", "o1"])
        self.cnt = {n: 0 for n in self.semnames}
        self.hw = {}
        self.rr = {"y": 0, "g": 0, "o": 0}
        self.seen = {e: {} for e in self.ENG}

    def emit(self, eng, fn, waits=(), inc=None, n=1):
        real = []
        flat = []
        for w in waits:
            if w is None:
                continue
            if isinstance(w, tuple) and w and isinstance(w[0], tuple):
                flat.extend(w)
            else:
                flat.append(w)
        seen = {}
        for w in flat:
            s, v = w
            if v <= 0:
                continue
            if seen.get(s, 0) < v:
                seen[s] = v
        eseen = self.seen[eng]
        for s, v in sorted(seen.items()):
            if v <= eseen.get(s, 0):
                continue
            eseen[s] = v
            real.append((s, v))
        if inc is None:
            inc, n = eng, 1
        self.cnt[inc] += n
        tok = (inc, self.cnt[inc])
        self.ops[eng].append((fn, tuple(real), inc, n))
        return tok

    def dma(self, eng, fn, waits=(), queue=None, nsem=None):
        q = queue or eng
        nsem = nsem if nsem is not None else (NRR if q in ("y", "g") else 2)
        sem = f"{q}{self.rr[q] % nsem}"
        self.rr[q] += 1
        self_wait = (sem, self.cnt[sem])
        return self.emit(eng, fn, waits=list(waits) + [self_wait],
                         inc=sem, n=16)


def ts(i, n=128):
    return slice(i * n, (i + 1) * n)


def build_program():
    nc = bass.Bass("TRN2", target_bir_lowering=False)

    dx = nc.dram_tensor("x", [S, D], F32, kind="ExternalInput").ap()
    dwaT = nc.dram_tensor("waT", [D, D], F32, kind="ExternalInput").ap()
    dwxT = nc.dram_tensor("wxT", [D, D], F32, kind="ExternalInput").ap()
    dba = nc.dram_tensor("ba2", [128, NT], F32, kind="ExternalInput").ap()
    dbx = nc.dram_tensor("bx2", [128, NT], F32, kind="ExternalInput").ap()
    dlam = nc.dram_tensor("lam", [128, NT], F32, kind="ExternalInput").ap()
    dw1 = nc.dram_tensor("w1", [2 * IFF, D], F32, kind="ExternalInput").ap()
    dw2 = nc.dram_tensor("w2", [D, IFF], F32, kind="ExternalInput").ap()
    dwinT = nc.dram_tensor("winT", [D, CW], F32, kind="ExternalInput").ap()
    dwoutT = nc.dram_tensor("woutT", [D, D], F32, kind="ExternalInput").ap()
    drms = nc.dram_tensor("rms2", [128, NT], F32, kind="ExternalInput").ap()
    dl1w = nc.dram_tensor("l1w", [D, D], F32, kind="ExternalInput").ap()
    dl1b = nc.dram_tensor("l1b2", [128, NT], F32, kind="ExternalInput").ap()
    dident = nc.dram_tensor("identb", [128, 128], BF16, kind="ExternalInput").ap()
    didf = nc.dram_tensor("identf", [128, 128], F32, kind="ExternalInput").ap()
    dprope = nc.dram_tensor("prope", [128, 128], BF16, kind="ExternalInput").ap()
    dmask = nc.dram_tensor("mask", [128, 128], F32, kind="ExternalInput").ap()
    dcos = nc.dram_tensor("cosT", [128, S], F32, kind="ExternalInput").ap()
    dsin = nc.dram_tensor("sinT", [128, S], F32, kind="ExternalInput").ap()
    dones1 = nc.dram_tensor("ones1", [1, 128], F32, kind="ExternalInput").ap()
    donesd = nc.dram_tensor("onesd", [128, 1], BF16, kind="ExternalInput").ap()
    donesk = nc.dram_tensor("onesk", [128, 1], F32, kind="ExternalInput").ap()
    dcvec = nc.dram_tensor("cvec", [128, 8], F32, kind="ExternalInput").ap()

    dout = nc.dram_tensor("out", [S, D], F32, kind="ExternalOutput").ap()
    dh = nc.dram_tensor("h_scr", [IFF, S], BF16).ap()
    dw2q = nc.dram_tensor("w2q_scr", [D, IFF], BF16).ap()

    from contextlib import ExitStack
    es = ExitStack()
    with es:
        XT = es.enter_context(nc.sbuf_tensor([128, NT * S], BF16))
        ZT = es.enter_context(nc.sbuf_tensor([128, NT * S], BF16))
        U = es.enter_context(nc.sbuf_tensor([128, 12288], BF16))
        Fs = es.enter_context(nc.sbuf_tensor([128, 3 * 2048], F32))
        IQ = es.enter_context(nc.sbuf_tensor([128, 2048], I32))
        SM = es.enter_context(nc.sbuf_tensor([128, 64], F32))
        SRA = es.enter_context(nc.sbuf_tensor([16, 128], F32))
        SRB = es.enter_context(nc.sbuf_tensor([16, 128], F32))
        RWS = es.enter_context(nc.sbuf_tensor([1, 2048], F32))
        RWS2 = es.enter_context(nc.sbuf_tensor([1, 2048], F32))
        CID = es.enter_context(nc.sbuf_tensor([128, 128], BF16))
        CIDF = es.enter_context(nc.sbuf_tensor([128, 128], F32))
        CP = es.enter_context(nc.sbuf_tensor([128, 128], BF16))
        CM = es.enter_context(nc.sbuf_tensor([128, 128], F32))
        C1 = es.enter_context(nc.sbuf_tensor([1, 128], F32))
        CD = es.enter_context(nc.sbuf_tensor([128, 1], BF16))
        CK = es.enter_context(nc.sbuf_tensor([128, 1], F32))
        CV = es.enter_context(nc.sbuf_tensor([128, 8], F32))
        VBA2 = es.enter_context(nc.sbuf_tensor([128, NT], F32))
        VBX2 = es.enter_context(nc.sbuf_tensor([128, NT], F32))
        VLAM = es.enter_context(nc.sbuf_tensor([128, NT], F32))
        VC2 = es.enter_context(nc.sbuf_tensor([128, NT], F32))
        VRMS = es.enter_context(nc.sbuf_tensor([128, NT], F32))
        VL1B = es.enter_context(nc.sbuf_tensor([128, NT], F32))
        VS1 = es.enter_context(nc.sbuf_tensor([128, 2 * NI], F32))
        VSC = es.enter_context(nc.sbuf_tensor([128, NI], F32))
        VS2 = es.enter_context(nc.sbuf_tensor([128, NT], F32))
        VSL = es.enter_context(nc.sbuf_tensor([128, 2], F32))

        PA = es.enter_context(nc.psum_tensor([128, 2048], F32))
        PB = es.enter_context(nc.psum_tensor([128, 2048], F32))

        sems = {}
        for nme in Ctx().semnames:
            sems[nme] = es.enter_context(nc.semaphore("sem_" + nme))
        block = es.enter_context(nc.Block())

        C = Ctx()

        def y_dma(dst, src, waits=(), sem=None):
            return C.dma("y", lambda e, d=dst, s=src: e.dma_start(d, s), waits,
                         queue="o" if sem == "do" else "y")

        def g_dma(dst, src, waits=()):
            return C.dma("g", lambda e, d=dst, s=src: e.dma_start(d, s), waits)

        def y_trdma(dst, src, waits=()):
            return C.dma("y", lambda e, d=dst, s=src: e.dma_start_transpose(d, s),
                         waits)

        _ctoks = []
        for dst, src in ((CID[:, :], dident), (CIDF[:, :], didf), (CP[:, :], dprope),
                         (CM[:, :], dmask),
                         (C1[:, :], dones1), (CD[:, :], donesd),
                         (CK[:, :], donesk), (CV[:, :], dcvec), (VBA2[:, :], dba), (VBX2[:, :], dbx),
                         (VLAM[:, :], dlam), (VRMS[:, :], drms), (VL1B[:, :], dl1b)):
            _ctoks.append(y_dma(dst, src))
        tCl = tuple(_ctoks)

        t1 = C.emit("s", lambda e: e.activation(VC2[:, :], VLAM[:, :], AF.Exp,
                    bias=0.0, scale=-1.0), waits=[tCl])
        t2 = C.emit("v", lambda e: e.tensor_scalar_add(VC2[:, :], VC2[:, :], 1.0), waits=[t1])
        t3 = C.emit("s", lambda e: e.activation(VC2[:, :], VC2[:, :], AF.Ln,
                    bias=0.0, scale=1.0), waits=[t2])
        tok_c2 = C.emit("s", lambda e: e.mul(VC2[:, :], VC2[:, :], -4.0), waits=[t3])
        t5 = C.emit("s", lambda e: e.mul(VBA2[:, :], VBA2[:, :], 0.5), waits=[tCl])
        tok_b2 = C.emit("s", lambda e: e.mul(VBX2[:, :], VBX2[:, :], 0.5), waits=[t5])

        stage_tok = [None, None]
        for j in range(16):
            sb = j % 2
            buf = U[:, 4096 * sb:4096 * sb + 2048]
            buf2 = U[:, 4096 * sb + 2048:4096 * sb + 4096]
            tl = g_dma(buf, dx[ts(j), :], waits=[stage_tok[sb]])
            tt = y_trdma(bass.AP(U[:, :].tensor, 4096 * sb + 2048,
                                 [[12288, 128], [128, 16], [1, 128]]),
                         buf[:, :], waits=[tl])
            out_ap = bass.AP(XT[:, :].tensor, j * 128,
                             [[NT * S, 128], [S, NT], [1, 128]])
            tt2 = y_dma(out_ap, buf2[:, :], waits=[tt])
            stage_tok[sb] = tt2
        tok_x = stage_tok[1]

        def ln_feature_major(src_big, dst_big, eps_col, tok_ready, u_guard=None):
            prev_mm = u_guard
            for g in range(NT):
                zg = src_big[:, ts(g, S)]
                sq = U[:, ts(g % 2, 2048)]
                tsq = C.emit("s", lambda e, a=sq, b=zg: e.activation(
                    a, b, AF.Square, bias=0.0, scale=1.0),
                    waits=[tok_ready, prev_mm])
                for c in range(4):
                    C.emit("t", lambda e, g=g, c=c, zg=zg: e.matmul(
                        PA[0:1, ts(c, 512)], CD[:, :], zg[:, ts(c, 512)],
                        start=(g == 0), stop=(g == NT - 1)),
                        waits=[tok_ready, tCl])
                    prev_mm = C.emit("t", lambda e, g=g, c=c, sq=sq: e.matmul(
                        PB[0:1, ts(c, 512)], CD[:, :], sq[:, ts(c, 512)],
                        start=(g == 0), stop=(g == NT - 1)), waits=[tsq])
            r1 = C.emit("s", lambda e: e.activation(RWS[0:1, :], PA[0:1, 0:2048],
                        AF.Square, bias=0.0, scale=1.0), waits=[prev_mm])
            r2 = C.emit("v", lambda e: e.scalar_tensor_tensor(RWS[0:1, :],
                        PB[0:1, 0:2048], 1.0, RWS[0:1, :], ALU.mult, ALU.subtract),
                        waits=[r1])
            r3 = C.emit("s", lambda e: e.activation(RWS[0:1, :], RWS[0:1, :], AF.Ln,
                        bias=CV[0:1, eps_col:eps_col + 1], scale=1.0), waits=[r2])
            r4 = C.emit("s", lambda e: e.activation(RWS[0:1, :], RWS[0:1, :], AF.Exp,
                        bias=0.0, scale=CV[0:1, 4:5]), waits=[r3])
            b2 = C.emit("s", lambda e: e.copy(RWS2[0:1, :], PA[0:1, 0:2048]), waits=[r4])
            last = None
            for c in range(4):
                C.emit("t", lambda e, c=c: e.matmul(PA[:, ts(c, 512)], C1[:, :],
                       RWS[0:1, ts(c, 512)], start=True, stop=True), waits=[b2])
                last = C.emit("t", lambda e, c=c: e.matmul(PB[:, ts(c, 512)], C1[:, :],
                              RWS2[0:1, ts(c, 512)], start=True, stop=True), waits=[b2])
            v1 = C.emit("s", lambda e: e.copy(Fs[:, 0:2048], PA[:, 0:2048]), waits=[last])
            v2 = C.emit("s", lambda e: e.copy(Fs[:, 2048:4096], PB[:, 0:2048]), waits=[v1])
            out_tok = None
            for g in range(NT):
                zg = src_big[:, ts(g, S)]
                dg = dst_big[:, ts(g, S)]
                a1 = C.emit("v", lambda e, zg=zg: e.tensor_sub(
                    Fs[:, 4096:6144], zg, Fs[:, 2048:4096]), waits=[v2, out_tok])
                out_tok = C.emit("v", lambda e, dg=dg: e.tensor_mul(
                    dg, Fs[:, 4096:6144], Fs[:, 0:2048]), waits=[a1])
            return out_tok

        def max_stat_bcast(tok_acc):
            red = None
            for c in range(16):
                b = (c % 4) * 512
                tt = C.emit("t", lambda e, b=b, c=c: e.transpose(
                    PA[:, b:b + 128],
                    Fs[:, 4096 + c * 128:4096 + (c + 1) * 128], CIDF[:, :]),
                    waits=[tok_acc, red, tCl])
                red = C.emit("v", lambda e, b=b, c=c: e.reduce_max(
                    SM[:, c:c + 1], PA[:, b:b + 128], AX), waits=[tt])
            q1 = C.emit("v", lambda e: e.tensor_scalar(SM[:, 16:32], SM[:, 0:16],
                        1.0 / QMAX, QEPS, ALU.mult, ALU.max), waits=[red])
            q2 = C.emit("v", lambda e: e.reciprocal(SM[:, 32:48], SM[:, 16:32]), waits=[q1])
            q3a = C.emit("t", lambda e: e.transpose(PB[0:16, 0:128], SM[:, 16:32],
                         CIDF[:, :]), waits=[q2])
            q3b = C.emit("t", lambda e: e.transpose(PB[0:16, 512:640], SM[:, 32:48],
                         CIDF[:, :]), waits=[q2])
            q4a = C.emit("s", lambda e: e.copy(SRA[:, :], PB[0:16, 0:128]), waits=[q3a])
            q4b = C.emit("s", lambda e: e.copy(SRB[:, :], PB[0:16, 512:640]), waits=[q3b])
            c1 = y_dma(RWS[0:1, :], SRB[:, :], waits=[q4b])
            c2 = y_dma(RWS2[0:1, :], SRA[:, :], waits=[q4a])
            last = None
            for c in range(4):
                C.emit("t", lambda e, c=c: e.matmul(PA[:, ts(c, 512)], C1[:, :],
                       RWS[0:1, ts(c, 512)], start=True, stop=True), waits=[c1, c2])
                last = C.emit("t", lambda e, c=c: e.matmul(PB[:, ts(c, 512)], C1[:, :],
                              RWS2[0:1, ts(c, 512)], start=True, stop=True), waits=[c1, c2])
            b1 = C.emit("s", lambda e: e.copy(Fs[:, 0:2048], PA[:, 0:2048]), waits=[last])
            b2 = C.emit("s", lambda e: e.copy(Fs[:, 2048:4096], PB[:, 0:2048]), waits=[b1])
            return b2

        tok_final = tok_x

        HW2 = IFF // 2
        W2A, W2B = 22528, 22528 + HW2
        w2s = dict(pw=None, ptr=None, piv2=None, wq=None)
        w2st = []
        RUN_W2 = STAGE >= 2 and KSUB >= 3

        def emit_w2quant(m):
            ldA = g_dma(ZT[:, W2A:W2A + HW2], dw2[ts(m), 0:HW2],
                        waits=[w2s["ptr"]])
            ldB = g_dma(ZT[:, W2B:W2B + HW2], dw2[ts(m), HW2:IFF],
                        waits=[w2s["ptr"]])
            rdA = C.emit("v", lambda e: e.reduce_max(SM[:, 48:49],
                         ZT[:, W2A:W2A + HW2],
                         AX, apply_absolute_value=True), waits=[ldA, w2s["piv2"]])
            rdB = C.emit("v", lambda e: e.reduce_max(SM[:, 49:50],
                         ZT[:, W2B:W2B + HW2],
                         AX, apply_absolute_value=True), waits=[ldB, rdA])
            mx = C.emit("v", lambda e: e.tensor_max(SM[:, 48:49],
                        SM[:, 48:49], SM[:, 49:50]), waits=[rdB])
            sc = C.emit("v", lambda e, m=m: e.tensor_scalar(
                VS2[:, m:m + 1], SM[:, 48:49], 1.0 / QMAX, QEPS,
                ALU.mult, ALU.max), waits=[mx])
            iv = C.emit("v", lambda e, m=m: e.reciprocal(SM[:, 50:51],
                        VS2[:, m:m + 1]), waits=[sc, w2s["wq"]])
            w2s["piv2"] = iv
            half_toks = []
            for cc in range(4):
                lo = W2A + cc * 1408
                rq = C.emit("s", lambda e, lo=lo: e.activation(
                    IQ[:, 0:1408], ZT[:, lo:lo + 1408], AF.Copy, bias=RB,
                    scale=SM[:, 50:51]),
                    waits=[iv] + ([half_toks[-1]] if half_toks else [w2s["wq"]]))
                wq = C.emit("v", lambda e, lo=lo: e.tensor_scalar_add(
                    ZT[:, lo:lo + 1408], IQ[:, 0:1408], -256.0), waits=[rq])
                half_toks.append(wq)
            w2s["wq"] = half_toks[-1]
            stA = y_dma(dw2q[ts(m), 0:HW2], ZT[:, W2A:W2A + HW2],
                        waits=[half_toks[1], w2s["ptr"]])
            ptr = y_dma(dw2q[ts(m), HW2:IFF], ZT[:, W2B:W2B + HW2],
                        waits=[half_toks[3], stA])
            w2s["ptr"] = ptr
            w2st.append(ptr)

        if RUN_W2:
            for m in range(4):
                emit_w2quant(m)

        if STAGE >= 1:
            prev_ra = None
            prev_rb = None
            prev_z = None
            prev_s3 = None
            prev_v3 = None
            prev_v5 = None
            wtok = [stage_tok[0], stage_tok[1]]
            for g in range(NT):
                sb = g % 2
                base = 4096 * sb
                wa_st = U[:, base:base + 2048]
                wx_st = U[:, base + 2048:base + 4096]
                in_a = bass.AP(dwaT.tensor, g * 128, [[D, 128], [128 * D, 16], [1, 128]])
                in_x = bass.AP(dwxT.tensor, g * 128, [[D, 128], [128 * D, 16], [1, 128]])
                oa = bass.AP(U[:, :].tensor, base, [[12288, 128], [128, 16], [1, 128]])
                ox = bass.AP(U[:, :].tensor, base + 2048, [[12288, 128], [128, 16], [1, 128]])
                ta = g_dma(oa, in_a, waits=[wtok[sb]])
                txw = g_dma(ox, in_x, waits=[wtok[sb]])
                mm = None
                for c in range(4):
                    for k in range(NT):
                        mm = C.emit("t", lambda e, c=c, k=k, wa_st=wa_st: e.matmul(
                            PA[:, ts(c, 512)], wa_st[:, ts(k)],
                            XT[:, k * S + 512 * c:k * S + 512 * c + 512],
                            start=(k == 0), stop=(k == NT - 1)),
                            waits=[ta, tok_x, prev_ra])
                mm2 = None
                for c in range(4):
                    for k in range(NT):
                        mm2 = C.emit("t", lambda e, c=c, k=k, wx_st=wx_st: e.matmul(
                            PB[:, ts(c, 512)], wx_st[:, ts(k)],
                            XT[:, k * S + 512 * c:k * S + 512 * c + 512],
                            start=(k == 0), stop=(k == NT - 1)),
                            waits=[txw, prev_rb])
                wtok[sb] = mm2
                rt = U[:, 8192:10240]
                it = U[:, 10240:12288]
                prev_ra = C.emit("s", lambda e, g=g, rt=rt: e.activation(
                    rt, PA[:, 0:2048], AF.Tanh, bias=VBA2[:, g:g + 1], scale=0.5),
                    waits=[mm, tok_b2, prev_s3])
                prev_rb = C.emit("s", lambda e, g=g, it=it: e.activation(
                    it, PB[:, 0:2048], AF.Tanh, bias=VBX2[:, g:g + 1], scale=0.5),
                    waits=[mm2, prev_v3])
                s3 = C.emit("s", lambda e, g=g, rt=rt: e.activation(
                    Fs[:, 0:2048], rt, AF.Exp, bias=VC2[:, g:g + 1],
                    scale=VC2[:, g:g + 1]), waits=[prev_ra, tok_c2, prev_v5])
                prev_s3 = s3
                v1 = C.emit("v", lambda e: e.tensor_mul(Fs[:, 2048:4096],
                            Fs[:, 0:2048], Fs[:, 0:2048]), waits=[s3])
                v2 = C.emit("v", lambda e: e.tensor_scalar(Fs[:, 2048:4096],
                            Fs[:, 2048:4096], -1.0, 1.0, ALU.mult, ALU.add), waits=[v1])
                s4 = C.emit("s", lambda e: e.activation(Fs[:, 2048:4096],
                            Fs[:, 2048:4096], AF.Ln, bias=0.0, scale=1.0), waits=[v2])
                s5 = C.emit("s", lambda e: e.activation(Fs[:, 2048:4096],
                            Fs[:, 2048:4096], AF.Exp, bias=CV[:, 0:1], scale=0.5), waits=[s4])
                v3 = C.emit("v", lambda e, g=g, it=it: e.scalar_tensor_tensor(
                    Fs[:, 4096:6144], it, 1.0, XT[:, ts(g, S)], ALU.add, ALU.mult),
                    waits=[prev_rb, tok_x])
                prev_v3 = v3
                v4 = C.emit("v", lambda e: e.tensor_mul(Fs[:, 2048:4096],
                            Fs[:, 2048:4096], Fs[:, 4096:6144]), waits=[v3, s5])
                v5 = C.emit("v", lambda e: e.tensor_tensor_scan(
                    Fs[:, 4096:6144], Fs[:, 0:2048], Fs[:, 2048:4096], 0.0,
                    ALU.mult, ALU.add), waits=[v4])
                prev_v5 = v5
                if RUN_W2 and g < 12:
                    emit_w2quant(g + 4)
                zguard = w2st[-1] if (RUN_W2 and g in (11, 12, 13)) else None
                prev_z = C.emit("v", lambda e, g=g: e.tensor_add(
                    ZT[:, ts(g, S)], Fs[:, 4096:6144], XT[:, ts(g, S)]),
                    waits=[v5, zguard])
            tok_x1 = ln_feature_major(ZT, XT, 2, prev_z, u_guard=wtok[1])
            tok_final = tok_x1

        if STAGE >= 2:
            acc = None
            sprev = None
            for g in range(NT):
                sprev = C.emit("s", lambda e, g=g: e.activation(U[:, 0:2048],
                               XT[:, ts(g, S)], AF.Abs, bias=0.0, scale=1.0),
                               waits=[tok_x1, acc])
                if g == 0:
                    acc = C.emit("v", lambda e: e.tensor_copy(Fs[:, 4096:6144],
                                 U[:, 0:2048]), waits=[sprev])
                else:
                    acc = C.emit("v", lambda e: e.tensor_max(Fs[:, 4096:6144],
                                 Fs[:, 4096:6144], U[:, 0:2048]), waits=[sprev])
            tok_bc = max_stat_bcast(acc)
            if KSUB == 0:
                tok_final = tok_bc
            tq = None
            prev_iqr = w2s["wq"]
            for g in range(NT):
                w1_ = C.emit("v", lambda e, g=g: e.tensor_mul(
                    Fs[:, 4096:6144], XT[:, ts(g, S)], Fs[:, 0:2048]),
                    waits=[tok_bc, tq])
                w2_ = C.emit("s", lambda e: e.activation(IQ[:, :], Fs[:, 4096:6144],
                             AF.Copy, bias=RB, scale=1.0), waits=[w1_, prev_iqr])
                w3_ = C.emit("v", lambda e: e.tensor_scalar_add(Fs[:, 4096:6144],
                             IQ[:, :], -256.0), waits=[w2_])
                prev_iqr = w3_
                tq = C.emit("v", lambda e, g=g: e.tensor_mul(
                    ZT[:, ts(g, S)], Fs[:, 4096:6144], Fs[:, 2048:4096]), waits=[w3_])
            tok_xq = tq
            if KSUB == 1:
                tok_final = tok_xq

            RUN_GU = KSUB >= 2
            hacc = None
            prev_e1 = None
            prev_e2 = None
            prev_e3 = None
            prev_rq = {"g": None, "u": None}
            prev_wq = w2s["wq"]
            h_done = None
            st_free = None
            prev_iv = w2s["piv2"]
            prev_trt = {"g": None, "u": None}
            e1_hist = [None, None]
            gs = dict(prev_rq=prev_rq, prev_wq=prev_wq, prev_iv=prev_iv,
                      prev_trt=prev_trt, st_free=None)

            def emit_w1quant(i):
                wgl = g_dma(U[:, 0:2048], dw1[ts(i), :],
                            waits=[gs["prev_rq"]["g"], gs["prev_trt"]["g"],
                                   tok_xq if i == 0 else None])
                wul = g_dma(U[:, 2048:4096], dw1[ts(NI + i), :],
                            waits=[gs["prev_rq"]["u"], gs["prev_trt"]["u"]])
                trt = {}
                for nm, off, ld in (("g", 0, wgl), ("u", 2048, wul)):
                    wn = U[:, off:off + 2048]
                    col = i if nm == "g" else NI + i
                    rd = C.emit("v", lambda e, wn=wn: e.reduce_max(
                        SM[:, 48:49], wn, AX, apply_absolute_value=True),
                        waits=[ld, gs["prev_iv"]])
                    sc = C.emit("v", lambda e, col=col: e.tensor_scalar(
                        VS1[:, col:col + 1], SM[:, 48:49], 1.0 / QMAX, QEPS,
                        ALU.mult, ALU.max), waits=[rd])
                    iv = C.emit("v", lambda e, col=col: e.reciprocal(
                        SM[:, 50:51], VS1[:, col:col + 1]),
                        waits=[sc, gs["prev_rq"][nm]])
                    gs["prev_iv"] = iv
                    rq = C.emit("s", lambda e, wn=wn: e.activation(IQ[:, :], wn,
                                AF.Copy, bias=RB, scale=SM[:, 50:51]),
                                waits=[iv, gs["prev_wq"]])
                    gs["prev_rq"][nm] = rq
                    wq = C.emit("v", lambda e, wn=wn: e.tensor_scalar_add(
                        wn, IQ[:, :], -256.0), waits=[rq])
                    gs["prev_wq"] = wq
                    so_ = 4096 + (0 if nm == "g" else 2048)
                    ot = bass.AP(U[:, :].tensor, so_, [[12288, 128], [128, 16], [1, 128]])
                    trt[nm] = y_trdma(ot, wn, waits=[wq, gs["st_free"]])
                    gs["prev_trt"][nm] = trt[nm]
                cs = C.emit("v", lambda e, i=i: e.tensor_mul(VSC[:, i:i + 1],
                            VS1[:, i:i + 1], VS1[:, NI + i:NI + i + 1]),
                            waits=[gs["prev_rq"]["u"], gs["prev_rq"]["g"]])
                cs2 = C.emit("v", lambda e, i=i: e.tensor_scalar_mul(
                    VSC[:, i:i + 1], VSC[:, i:i + 1], 0.5), waits=[cs])
                tcol = 54 + (i % 2)
                tsc = C.emit("v", lambda e, i=i, tcol=tcol: e.tensor_scalar_mul(
                    SM[:, tcol:tcol + 1], VS1[:, i:i + 1], 0.5),
                    waits=[cs2, e1_hist[i % 2]])
                trt["tsc"] = tsc
                trt["cs2"] = cs2
                return trt

            trt_cur = emit_w1quant(0) if RUN_GU else None
            for i in range(NI if RUN_GU else 0):
                trt = trt_cur
                mmg = None
                for c in range(4):
                    for k in range(NT):
                        mmg = C.emit("t", lambda e, c=c, k=k: e.matmul(
                            PA[:, ts(c, 512)],
                            U[:, 4096 + k * 128:4096 + k * 128 + 128],
                            ZT[:, k * S + 512 * c:k * S + 512 * c + 512],
                            start=(k == 0), stop=(k == NT - 1)),
                            waits=[trt["g"], tok_xq, prev_e2])
                mmu = None
                for c in range(4):
                    for k in range(NT):
                        mmu = C.emit("t", lambda e, c=c, k=k: e.matmul(
                            PB[:, ts(c, 512)],
                            U[:, 6144 + k * 128:6144 + k * 128 + 128],
                            ZT[:, k * S + 512 * c:k * S + 512 * c + 512],
                            start=(k == 0), stop=(k == NT - 1)),
                            waits=[trt["u"], prev_e3])
                gs["st_free"] = mmu
                if i + 1 < NI:
                    trt_cur = emit_w1quant(i + 1)
                tcol = 54 + (i % 2)
                e1 = C.emit("s", lambda e, tcol=tcol: e.activation(
                            U[:, 8192:10240], PA[:, 0:2048], AF.Tanh, bias=0.0,
                            scale=SM[:, tcol:tcol + 1]),
                            waits=[mmg, trt["tsc"], h_done])
                e1_hist[i % 2] = e1
                prev_e1 = e1
                e2 = C.emit("v", lambda e: e.scalar_tensor_tensor(Fs[:, 0:2048],
                            U[:, 8192:10240], 1.0, PA[:, 0:2048], ALU.add, ALU.mult),
                            waits=[e1])
                prev_e2 = e2
                e3 = C.emit("v", lambda e: e.tensor_mul(Fs[:, 0:2048], Fs[:, 0:2048],
                            PB[:, 0:2048]), waits=[e2, mmu])
                prev_e3 = e3
                e4 = C.emit("v", lambda e, i=i: e.tensor_scalar_mul(
                    U[:, 10240:12288], Fs[:, 0:2048], VSC[:, i:i + 1]),
                    waits=[e3, trt["cs2"], h_done])
                sab = C.emit("s", lambda e: e.activation(Fs[:, 2048:4096],
                             U[:, 10240:12288], AF.Abs, bias=0.0, scale=1.0), waits=[e4])
                if i == 0:
                    hacc = C.emit("v", lambda e: e.tensor_copy(Fs[:, 4096:6144],
                                  Fs[:, 2048:4096]), waits=[sab])
                else:
                    hacc = C.emit("v", lambda e: e.tensor_max(Fs[:, 4096:6144],
                                  Fs[:, 4096:6144], Fs[:, 2048:4096]), waits=[sab])
                h_done = y_dma(dh[ts(i), :], U[:, 10240:12288], waits=[e4])
            prev_wq = gs["prev_wq"]
            tok_hbc = max_stat_bcast(hacc) if RUN_GU else None
            if KSUB == 2:
                tok_final = tok_hbc

            qts0 = []
            prev_hq = tok_xq
            if KSUB >= 4:
                piq0 = prev_wq
                for i in range(NI):
                    hz = ZT[:, i * 512:(i + 1) * 512]
                    hl = g_dma(hz, dh[ts(i), 0:512],
                               waits=[prev_hq, tok_hbc, h_done])
                    q1 = C.emit("v", lambda e, hz=hz: e.tensor_mul(
                        Fs[:, 4096:4608], hz, Fs[:, 0:512]),
                        waits=[hl, qts0[-1] if qts0 else None])
                    q2 = C.emit("s", lambda e: e.activation(IQ[:, 1536:2048],
                                Fs[:, 4096:4608], AF.Copy, bias=RB, scale=1.0),
                                waits=[q1, piq0])
                    q3 = C.emit("v", lambda e, hz=hz: e.tensor_scalar_add(
                        hz, IQ[:, 1536:2048], -256.0), waits=[q2])
                    piq0 = q3
                    qts0.append(q3)
                prev_hq = qts0[-1]

            prev_ev = None
            prev_wst = [(gs["prev_trt"]["g"], gs["prev_trt"]["u"], tok_hbc),
                        (h_done, tok_hbc)]
            last_mm_prev_chunk = None
            bank_rd = [None] * 4
            prev_d2 = None
            for c in range(4 if KSUB >= 4 else 0):
                if c == 0:
                    qts = qts0
                else:
                    qts = []
                    piq = None
                    for i in range(NI):
                        hz = ZT[:, i * 512:(i + 1) * 512]
                        hl = g_dma(hz, dh[ts(i), c * 512:c * 512 + 512],
                                   waits=[prev_hq, tok_hbc, h_done,
                                          last_mm_prev_chunk])
                        q1 = C.emit("v", lambda e, hz=hz, c=c: e.tensor_mul(
                            Fs[:, 4096:4608], hz, Fs[:, c * 512:c * 512 + 512]),
                            waits=[hl, qts[-1] if qts else None])
                        q2 = C.emit("s", lambda e: e.activation(IQ[:, 1536:2048],
                                    Fs[:, 4096:4608], AF.Copy, bias=RB, scale=1.0),
                                    waits=[q1, piq])
                        q3 = C.emit("v", lambda e, hz=hz: e.tensor_scalar_add(
                            hz, IQ[:, 1536:2048], -256.0), waits=[q2])
                        piq = q3
                        qts.append(q3)
                    prev_hq = qts[-1]
                for m in range(NT):
                    sb = m % 2
                    wl = y_trdma(bass.AP(U[:, :].tensor, sb * 6144,
                                         [[12288, 128], [128, NI], [1, 128]]),
                                 dw2q[ts(m), :],
                                 waits=[prev_wst[sb], w2st[m] if w2st else None])
                    if sb == 0:
                        last_wl0 = wl
                    else:
                        last_wl = wl
                    mm = None
                    for k in range(NI):
                        mm = C.emit("t", lambda e, m=m, k=k, sb=sb: e.matmul(
                            PA[:, ts(m % 4, 512)],
                            U[:, sb * 6144 + k * 128:sb * 6144 + k * 128 + 128],
                            ZT[:, k * 512:(k + 1) * 512],
                            start=(k == 0), stop=(k == NI - 1)),
                            waits=[wl, qts[NI - 1], bank_rd[m % 4]])
                    prev_wst[sb] = mm
                    last_mm_prev_chunk = mm
                    d1 = C.emit("s", lambda e, m=m: e.activation(Fs[:, 4608:5120],
                                PA[:, ts(m % 4, 512)], AF.Copy, bias=0.0,
                                scale=VS2[:, m:m + 1]), waits=[mm, prev_d2])
                    bank_rd[m % 4] = d1
                    d2 = C.emit("v", lambda e, c=c: e.tensor_mul(Fs[:, 4608:5120],
                                Fs[:, 4608:5120],
                                Fs[:, 2048 + c * 512:2048 + c * 512 + 512]),
                                waits=[d1])
                    prev_d2 = d2
                    d3 = C.emit("v", lambda e, m=m, c=c: e.tensor_add(
                        Fs[:, 5120:5632], Fs[:, 4608:5120],
                        XT[:, m * S + c * 512:m * S + c * 512 + 512]),
                        waits=[d2, prev_ev])
                    prev_ev = C.emit("s", lambda e, m=m, c=c: e.copy(
                        XT[:, m * S + c * 512:m * S + c * 512 + 512],
                        Fs[:, 5120:5632]), waits=[d3])
            if KSUB >= 4:
                tok_x2 = ln_feature_major(XT, XT, 2, prev_ev, u_guard=prev_wst[1])
                tok_final = tok_x2
                guard2 = (tok_x2, prev_wst[0], prev_wst[1], last_wl, last_wl0)


        if STAGE >= 3 and KSUB3 >= 1:
            prev_mm = guard2
            for g in range(NT):
                sq = U[:, ts(g % 2, 2048)]
                tsq = C.emit("s", lambda e, a=sq, g=g: e.activation(
                    a, XT[:, ts(g, S)], AF.Square, bias=0.0, scale=1.0),
                    waits=[tok_x2, prev_mm])
                for c in range(4):
                    prev_mm = C.emit("t", lambda e, g=g, c=c, sq=sq: e.matmul(
                        PA[0:1, ts(c, 512)], CD[:, :], sq[:, ts(c, 512)],
                        start=(g == 0), stop=(g == NT - 1)), waits=[tsq])
            r3 = C.emit("s", lambda e: e.activation(RWS[0:1, :], PA[0:1, 0:2048],
                        AF.Ln, bias=CV[0:1, 3:4], scale=1.0), waits=[prev_mm])
            r4 = C.emit("s", lambda e: e.activation(RWS[0:1, :], RWS[0:1, :], AF.Exp,
                        bias=0.0, scale=CV[0:1, 4:5]), waits=[r3])
            last = None
            for c in range(4):
                last = C.emit("t", lambda e, c=c: e.matmul(PA[:, ts(c, 512)], C1[:, :],
                              RWS[0:1, ts(c, 512)], start=True, stop=True), waits=[r4])
            tok_rrms = C.emit("s", lambda e: e.copy(Fs[:, 0:2048], PA[:, 0:2048]),
                              waits=[last])
            prev_stage = (prev_mm, tok_rrms) + guard2
            tok_cos = y_dma(Fs[:, 2048:4096], dcos, waits=[tok_x2])
            tok_sin = y_dma(Fs[:, 4096:6144], dsin, waits=[tok_x2])
            vrms_ap = bass.AP(VRMS[:, :].tensor, 0, [[NT, 128], [1, NT], [0, 128]])
            stage3 = bass.AP(U[:, :].tensor, 0, [[12288, 128], [128, 16], [1, 128]])

            prev_ostrip = None
            prev_vt = None
            prev_exp = [None] * 4
            prev_oev = None
            prev_rope = None
            oev_bank = [None] * 4
            bc_prev = None
            for g in range(NG if KSUB3 >= 4 else 1):
                qtiles = []
                prev_ev_sub = None
                for sub in range(3):
                    if sub < 2:
                        col0 = g * 256 + sub * 128
                        lds = [g_dma(stage3,
                               bass.AP(dwinT.tensor, col0,
                                       [[CW, 128], [128 * CW, 16], [1, 128]]),
                               waits=[prev_stage])]
                    else:
                        lds = [g_dma(bass.AP(U[:, :].tensor, 0,
                                     [[12288, 128], [128, 16], [1, 64]]),
                               bass.AP(dwinT.tensor, 2048 + g * 64,
                                       [[CW, 128], [128 * CW, 16], [1, 64]]),
                               waits=[prev_stage]),
                               g_dma(bass.AP(U[:, :].tensor, 64,
                                     [[12288, 128], [128, 16], [1, 64]]),
                               bass.AP(dwinT.tensor, 2560 + g * 64,
                                       [[CW, 128], [128 * CW, 16], [1, 64]]),
                               waits=[prev_stage])]
                    sc = C.emit("v", lambda e: e.tensor_tensor(
                        bass.AP(U[:, :].tensor, 0, [[12288, 128], [128, 16], [1, 128]]),
                        bass.AP(U[:, :].tensor, 0, [[12288, 128], [128, 16], [1, 128]]),
                        vrms_ap, ALU.mult), waits=lds + [tCl, prev_stage])
                    mm = None
                    for c in range(4):
                        for k in range(NT):
                            mm = C.emit("t", lambda e, c=c, k=k: e.matmul(
                                PA[:, ts(c, 512)], U[:, ts(k)],
                                XT[:, k * S + 512 * c:k * S + 512 * c + 512],
                                start=(k == 0), stop=(k == NT - 1)),
                                waits=[sc, tok_x2, prev_rope, prev_ev_sub])
                    prev_stage = mm
                    dst = U[:, 2048 + sub * 2048:4096 + sub * 2048]
                    ev = C.emit("v", lambda e, dst=dst: e.tensor_mul(
                        dst, PA[:, 0:2048], Fs[:, 0:2048]), waits=[mm, tok_rrms])
                    prev_ev_sub = ev
                    qtiles.append((dst, ev))
                rope_toks = [prev_ev_sub]
                for ri in range(3 if KSUB3 >= 2 else 0):
                    dst, ev = qtiles[ri]
                    nrow = 128 if ri < 2 else 64
                    pm = None
                    for c in range(4):
                        pm = C.emit("t", lambda e, c=c, dst=dst, nrow=nrow: e.matmul(
                            PA[0:nrow, ts(c, 512)], CP[0:nrow, 0:nrow],
                            dst[0:nrow, ts(c, 512)], start=True, stop=True),
                            waits=[ev, tCl, prev_rope, rope_toks[-1]])
                        a1 = C.emit("v", lambda e, c=c, dst=dst, nrow=nrow:
                                    e.tensor_mul(PB[0:nrow, ts(c, 512)],
                                    dst[0:nrow, ts(c, 512)],
                                    Fs[0:nrow, 2048 + c * 512:2048 + c * 512 + 512]),
                                    waits=[ev, tok_cos, prev_oev])
                        a2 = C.emit("v", lambda e, c=c, nrow=nrow: e.tensor_mul(
                            Fs[0:nrow, 0:512] if False else U[0:nrow, 8192 + c * 512:8192 + c * 512 + 512],
                            PA[0:nrow, ts(c, 512)],
                            Fs[0:nrow, 4096 + c * 512:4096 + c * 512 + 512]),
                            waits=[pm, tok_sin])
                        a3 = C.emit("v", lambda e, c=c, dst=dst, nrow=nrow:
                                    e.tensor_add(dst[0:nrow, ts(c, 512)],
                                    PB[0:nrow, ts(c, 512)],
                                    U[0:nrow, 8192 + c * 512:8192 + c * 512 + 512]),
                                    waits=[a1, a2])
                        rope_toks.append(a3)
                    prev_rope = rope_toks[-1]
                kvt, _ = qtiles[2]
                vt0 = 10240
                if KSUB3 >= 3:
                    ttr = y_trdma(bass.AP(U[:, :].tensor, 8192,
                                          [[12288, 128], [128, 16], [1, 128]]),
                                  kvt[:, :], waits=[qtiles[2][1], prev_rope, prev_vt])
                    vts = []
                    for c in range(16):
                        sv = C.emit("s", lambda e, c=c: e.copy(
                            U[:, vt0 + c * VTS:vt0 + c * VTS + 64],
                            U[:, 8192 + c * 128 + 64:8192 + c * 128 + 128]), waits=[ttr])
                        vts.append(sv)
                    on = C.emit("v", lambda e: e.memset(
                        bass.AP(U[:, :].tensor, vt0 + 64, [[12288, 128], [VTS, 16]]), 1.0),
                        waits=[vts[-1], prev_vt])
                    tok_vt = on
                    prev_vt = on
                    kdup = g_dma(kvt[64:128, :], kvt[0:64, :], waits=[ttr, prev_rope])
                for sp in range(4 if KSUB3 >= 3 else 0):
                    q0 = sp * 512
                    nk = sp * 4 + 4
                    mmq = {}
                    av_last = {}
                    for sk in range(nk):
                        for h in range(4):
                            qt = U[:, 2048 + (h // 2) * 2048:4096 + (h // 2) * 2048]
                            hb = 64 * (h % 2)
                            mv = qt[hb:hb + 64, q0:q0 + 512]
                            mmq[h] = C.emit("t", lambda e, h=h, sk=sk, mv=mv,
                                            kvt=kvt, hb=hb:
                                            e.matmul(PA[:, ts(h, 512)],
                                            kvt[hb:hb + 64, ts(sk)], mv,
                                            start=True, stop=True),
                                            waits=[prev_rope, tok_vt, kdup,
                                                   prev_exp[h]])
                        msk = None
                        if sk >= sp * 4:
                            off = sk * 128 - q0
                            pmask = bass.AP(PA[:, :].tensor, off,
                                            [[2048, 128], [512, 4], [1, 128]])
                            cmr = bass.AP(CM[:, :].tensor, 0,
                                          [[128, 128], [0, 4], [1, 128]])
                            msk = C.emit("v", lambda e, pmask=pmask, cmr=cmr:
                                e.tensor_add(pmask, pmask, cmr),
                                waits=[mmq[0], mmq[1], mmq[2], mmq[3], tCl])
                        etok = C.emit("s", lambda e: e.activation(
                            U[:, 8192:10240],
                            PA[:, 0:2048], AF.Exp, bias=CV[:, 5:6],
                            scale=CV[:, 6:7]),
                            waits=[mmq[0], mmq[1], mmq[2], mmq[3], msk,
                                   oev_bank[0], oev_bank[1],
                                   oev_bank[2], oev_bank[3]])
                        if sk > sp * 4:
                            off = sk * 128 - q0
                            zm = bass.AP(U[:, :].tensor, 8192,
                                         [[12288, 128], [512, 4], [1, off]])
                            etok = C.emit("v", lambda e, zm=zm: e.memset(zm, 0.0),
                                          waits=[etok])
                        for h in range(4):
                            prev_exp[h] = etok
                            av_last[h] = C.emit("t", lambda e, h=h, sk=sk, nk=nk:
                                e.matmul(
                                PB[0:65, ts(h, 512)],
                                U[:, vt0 + sk * VTS:vt0 + sk * VTS + 65],
                                U[:, 8192 + h * 512:8192 + h * 512 + 512],
                                start=(sk == 0), stop=(sk == nk - 1)),
                                waits=[etok] +
                                      ([oev_bank[h]] if sk == 0 else []))
                    otoks = []
                    rr = C.emit("v", lambda e: e.reciprocal(
                        RWS[0:1, 0:2048], PB[64:65, 0:2048]),
                        waits=[av_last[0], av_last[1], av_last[2], av_last[3],
                               bc_prev])
                    for h in range(4):
                        head = 4 * g + h
                        bc = C.emit("t", lambda e, h=h: e.matmul(
                                    PB[64:128, ts(h, 512)],
                                    C1[0:1, 0:64], RWS[0:1, ts(h, 512)],
                                    start=True, stop=True), waits=[rr])
                        bc_prev = bc
                        oslot = U[0:64, 8192 + h * 512:8192 + h * 512 + 512]
                        oe = C.emit("s", lambda e, h=h, oslot=oslot: e.copy(
                            oslot, PB[0:64, ts(h, 512)]), waits=[av_last[h]])
                        zslot = ZT[64 * (head % 2):64 * (head % 2) + 64,
                                   (head // 2) * S + q0:(head // 2) * S + q0 + 512]
                        ov = C.emit("v", lambda e, h=h, zslot=zslot, oslot=oslot:
                                    e.tensor_mul(zslot, oslot,
                                                 PB[64:128, ts(h, 512)]),
                                    waits=[oe, bc])
                        oev_bank[h] = (oe, ov)
                        otoks.extend([oe, ov])
                    prev_oev = tuple(otoks)
            prev_ev3 = None
            ev3_bank = [None, None]
            cp_last = [None, None]
            wst3 = [None, None]
            for m in range(NT if KSUB3 >= 5 else 0):
                sb3 = m % 2
                stage3m = bass.AP(U[:, :].tensor, sb3 * 2048,
                                  [[12288, 128], [128, 16], [1, 128]])
                wl = g_dma(stage3m, bass.AP(dwoutT.tensor, m * 128,
                           [[D, 128], [128 * D, 16], [1, 128]]),
                           waits=[prev_stage, prev_oev, wst3[sb3]])
                PM = PA if sb3 == 0 else PB
                mm = None
                for c in range(4):
                    for k in range(NT):
                        mm = C.emit("t", lambda e, c=c, k=k, PM=PM, sb3=sb3:
                            e.matmul(
                            PM[:, ts(c, 512)],
                            U[:, sb3 * 2048 + k * 128:sb3 * 2048 + k * 128 + 128],
                            ZT[:, k * S + 512 * c:k * S + 512 * c + 512],
                            start=(k == 0), stop=(k == NT - 1)),
                            waits=[wl, prev_oev, ev3_bank[sb3]])
                prev_stage = mm
                wst3[sb3] = mm
                last_v = None
                for c in range(4):
                    v1_ = C.emit("v", lambda e, m=m, c=c, PM=PM, sb3=sb3:
                        e.scalar_tensor_tensor(
                        Fs[:, sb3 * 2048 + c * 512:sb3 * 2048 + c * 512 + 512],
                        XT[:, m * S + c * 512:m * S + c * 512 + 512], 2.0,
                        PM[:, ts(c, 512)], ALU.mult, ALU.add),
                        waits=[mm, cp_last[sb3] if c == 0 else None])
                    prev_ev3 = C.emit("s", lambda e, m=m, c=c, sb3=sb3: e.copy(
                        XT[:, m * S + c * 512:m * S + c * 512 + 512],
                        Fs[:, sb3 * 2048 + c * 512:sb3 * 2048 + c * 512 + 512]),
                        waits=[v1_])
                    last_v = v1_
                ev3_bank[sb3] = last_v
                cp_last[sb3] = prev_ev3
            if KSUB3 >= 5:
                tok_x3 = ln_feature_major(XT, XT, 2, prev_ev3, u_guard=prev_stage)
                tok_final = tok_x3

        if STAGE >= 4:
            pacc = None
            pld = None
            for m in range(NT):
                ld = g_dma(U[:, 0:2048], dl1w[ts(m), :], waits=[pld, tok_x3])
                ab = C.emit("s", lambda e: e.activation(Fs[:, 0:2048], U[:, 0:2048],
                            AF.Abs, bias=0.0, scale=1.0), waits=[ld, pacc])
                pld = ab
                rd = C.emit("v", lambda e: e.reduce_sum(SM[:, 52:53], Fs[:, 0:2048],
                            AX), waits=[ab])
                if m == 0:
                    pacc = C.emit("v", lambda e: e.tensor_copy(SM[:, 53:54],
                                  SM[:, 52:53]), waits=[rd])
                else:
                    pacc = C.emit("v", lambda e: e.tensor_add(SM[:, 53:54],
                                  SM[:, 53:54], SM[:, 52:53]), waits=[rd])
            gs = C.emit("t", lambda e: e.matmul(PA[0:1, 0:1], CK[:, :],
                        SM[:, 53:54], start=True, stop=True), waits=[pacc, tCl])
            g1_ = C.emit("s", lambda e: e.mul(RWS[0:1, 0:1], PA[0:1, 0:1],
                         1.0 / (D * D)), waits=[gs])
            g2_ = C.emit("v", lambda e: e.tensor_scalar_add(RWS[0:1, 1:2],
                         RWS[0:1, 0:1], L1_EPS), waits=[g1_])
            g3_ = C.emit("v", lambda e: e.reciprocal(RWS[0:1, 2:3], RWS[0:1, 1:2]),
                         waits=[g2_])
            bc1 = C.emit("t", lambda e: e.matmul(PB[:, 0:1], C1[0:1, :],
                         RWS[0:1, 2:3], start=True, stop=True), waits=[g3_])
            bc2 = C.emit("t", lambda e: e.matmul(PB[:, 512:513], C1[0:1, :],
                         RWS[0:1, 0:1], start=True, stop=True), waits=[g3_])
            sv1 = C.emit("s", lambda e: e.copy(VSL[:, 0:1], PB[:, 0:1]), waits=[bc1])
            sv2 = C.emit("s", lambda e: e.copy(VSL[:, 1:2], PB[:, 512:513]), waits=[bc2])
            prev_rq4 = [None, None]
            prev_tr4 = [None, None]
            prev_t1 = None
            prev_z4 = None
            prev_mm4 = [None, None]
            psv_bank = [None, None]
            prev_cp4 = [None, None]
            for m in range(NT):
                sb4 = m % 2
                nat = U[:, sb4 * 2048:sb4 * 2048 + 2048]
                ld = g_dma(nat, dl1w[ts(m), :],
                           waits=[prev_rq4[sb4], prev_tr4[sb4], sv1, sv2])
                rq = C.emit("s", lambda e, nat=nat: e.activation(IQ[:, :], nat,
                            AF.Copy, bias=RB, scale=VSL[:, 0:1]),
                            waits=[ld, prev_t1])
                t1_ = C.emit("v", lambda e: e.tensor_scalar(Fs[:, 4096:6144],
                             IQ[:, :], -(RB), -1.0, ALU.add, ALU.max), waits=[rq])
                prev_t1 = t1_
                t2_ = C.emit("v", lambda e, nat=nat: e.tensor_scalar(
                    nat, Fs[:, 4096:6144], 1.0, 1.0, ALU.min, ALU.mult),
                    waits=[t1_])
                prev_rq4[sb4] = t2_
                so4 = 4096 + sb4 * 2048
                ot = bass.AP(U[:, :].tensor, so4, [[12288, 128], [128, 16], [1, 128]])
                tr = y_trdma(ot, nat, waits=[t2_, prev_mm4[sb4]])
                prev_tr4[sb4] = tr
                PM = PA if sb4 == 0 else PB
                mm = None
                for c in range(4):
                    for k in range(NT):
                        mm = C.emit("t", lambda e, c=c, k=k, PM=PM, so4=so4:
                            e.matmul(
                            PM[:, ts(c, 512)],
                            U[:, so4 + k * 128:so4 + k * 128 + 128],
                            XT[:, k * S + 512 * c:k * S + 512 * c + 512],
                            start=(k == 0), stop=(k == NT - 1)),
                            waits=[tr, psv_bank[sb4]])
                prev_mm4[sb4] = mm
                for c in range(4):
                    sv = C.emit("s", lambda e, c=c, PM=PM, sb4=sb4: e.activation(
                        Fs[:, sb4 * 2048 + c * 512:sb4 * 2048 + c * 512 + 512],
                        PM[:, ts(c, 512)], AF.Copy, bias=0.0, scale=VSL[:, 1:2]),
                        waits=[mm, prev_cp4[sb4] if c == 0 else None])
                    prev_z4 = C.emit("v", lambda e, m=m, c=c, sb4=sb4:
                        e.scalar_tensor_tensor(
                        ZT[:, m * S + c * 512:m * S + c * 512 + 512],
                        Fs[:, sb4 * 2048 + c * 512:sb4 * 2048 + c * 512 + 512],
                        VL1B[:, m:m + 1],
                        XT[:, m * S + c * 512:m * S + c * 512 + 512],
                        ALU.add, ALU.add), waits=[sv])
                    psv_bank[sb4] = sv
                prev_cp4[sb4] = prev_z4
            tok_x4 = ln_feature_major(ZT, XT, 2, prev_z4,
                                      u_guard=(prev_tr4[0], prev_tr4[1]))
            tok_final = tok_x4

        out_done = []
        prev_cv = [None, None]
        for j in range(16 if not KPROBE else 0):
            sb = j % 2
            g1 = U[:, 4096 * sb:4096 * sb + 2048]
            g2 = U[:, 4096 * sb + 2048:4096 * sb + 4096]
            it_ap = bass.AP(XT[:, :].tensor, j * 128,
                            [[NT * S, 128], [S, NT], [1, 128]])
            td = y_dma(g1[:, :], it_ap, waits=[tok_final, prev_cv[sb]])
            tt = y_trdma(bass.AP(U[:, :].tensor, 4096 * sb + 2048,
                                 [[12288, 128], [128, 16], [1, 128]]),
                         g1[:, :], waits=[td])
            cv = C.emit("s", lambda e, g2=g2, sb=sb: e.copy(
                Fs[:, ts(sb, 2048)], g2[:, :]),
                waits=[tt] + ([out_done[j - 2]] if j >= 2 else []))
            prev_cv[sb] = cv
            st = y_dma(dout[ts(j), :], Fs[:, ts(sb, 2048)], waits=[cv], sem="do")
            out_done.append(st)

        final_o = (C.cnt["o0"], C.cnt["o1"])

        def mk(ename):
            def body(eng):
                for fn, waits, inc, n in C.ops[ename]:
                    for (snm, v) in waits:
                        eng.wait_ge(sems[snm], v)
                    ins = fn(eng)
                    if n:
                        ins.then_inc(sems[inc], n)
                if ename == "y":
                    eng.wait_ge(sems["o0"], final_o[0])
                    eng.wait_ge(sems["o1"], final_o[1])
            return body

        block.tensor(mk("t"))
        block.vector(mk("v"))
        block.scalar(mk("s"))
        block.gpsimd(mk("g"))
        block.sync(mk("y"))

    return nc


def _rope_tables():
    j = np.arange(0, DQK, 2, dtype=np.float32)
    inv = (np.float32(1.0) / np.power(np.float32(10000.0),
           (j / np.float32(DQK)).astype(np.float32))).astype(np.float32)
    ang = (np.arange(S, dtype=np.float32)[:, None] * inv[None, :]).astype(np.float32)
    c = np.cos(ang).astype(np.float32)
    s = np.sin(ang).astype(np.float32)
    rows = np.arange(128) % 32
    return np.ascontiguousarray(c[:, rows].T), np.ascontiguousarray(s[:, rows].T)


def _vec_tiles(v):
    return np.ascontiguousarray(np.asarray(v, dtype=np.float32).reshape(NT, 128).T)


def make_inputs(L0_attn_x, L0_attn_Wa, L0_attn_Wx, L0_attn_ba, L0_attn_bx,
                L0_attn_Lam, L0_ffn_w_gate_up, L0_ffn_w_down, L1_attn_w_in,
                L1_attn_w_out, L1_attn_rms_w, L1_ffn_weight, L1_ffn_bias):
    cosT, sinT = _rope_tables()
    P = np.zeros((128, 128), dtype=np.float32)
    for h in range(2):
        for jj in range(32):
            P[h * 64 + jj + 32, h * 64 + jj] = -1.0
            P[h * 64 + jj, h * 64 + jj + 32] = 1.0
    mask = np.where(np.arange(128)[:, None] > np.arange(128)[None, :],
                    np.float32(-1e5), np.float32(0.0)).astype(np.float32)
    shared = dict(
        waT=np.ascontiguousarray(np.asarray(L0_attn_Wa, dtype=np.float32).T),
        wxT=np.ascontiguousarray(np.asarray(L0_attn_Wx, dtype=np.float32).T),
        ba2=_vec_tiles(L0_attn_ba), bx2=_vec_tiles(L0_attn_bx),
        lam=_vec_tiles(L0_attn_Lam),
        w1=np.ascontiguousarray(L0_ffn_w_gate_up, dtype=np.float32),
        w2=np.ascontiguousarray(L0_ffn_w_down, dtype=np.float32),
        winT=np.ascontiguousarray(np.asarray(L1_attn_w_in, dtype=np.float32).T),
        woutT=np.ascontiguousarray(np.asarray(L1_attn_w_out, dtype=np.float32).T),
        rms2=_vec_tiles(L1_attn_rms_w),
        l1w=np.ascontiguousarray(L1_ffn_weight, dtype=np.float32),
        l1b2=_vec_tiles(L1_ffn_bias),
        identb=np.eye(128, dtype=BF), identf=np.eye(128, dtype=np.float32),
        prope=P.astype(BF), mask=mask, cosT=cosT, sinT=sinT,
        ones1=np.ones((1, 128), dtype=np.float32),
        onesd=np.full((128, 1), 1.0 / D, dtype=BF),
        onesk=np.ones((128, 1), dtype=np.float32),
        cvec=np.tile(np.array([LOG_HALF, RB, LN_EPS, RMS_EPS, -0.5, -20.0,
                               0.125, 0.0], dtype=np.float32), (128, 1)),
    )
    xs = np.asarray(L0_attn_x, dtype=np.float32)
    return [dict(shared, x=np.ascontiguousarray(xs[c])) for c in range(NCORE)]


def kernel(**inputs):
    nc = build_program()
    in_maps = make_inputs(**inputs)
    res = run_bass_kernel_spmd(nc, in_maps, core_ids=list(range(NCORE)))
    return np.stack([np.asarray(r["out"], dtype=np.float32) for r in res.results])


# version 11
# speedup vs baseline: 1.0376x; 1.0376x over previous
import os
import numpy as np
import ml_dtypes
import concourse.bass as bass
import concourse.mybir as mybir
from concourse.bass_utils import run_bass_kernel_spmd

F32 = mybir.dt.float32
BF16 = mybir.dt.bfloat16
I32 = mybir.dt.int32
AF = mybir.ActivationFunctionType
ALU = mybir.AluOpType
BF = ml_dtypes.bfloat16
AX = mybir.AxisListType.X

D = 2048
S = 2048
NT = D // 128
IFF = 5632
NI = IFF // 128
CW = 3072
NG = 8
DQK = 64
STAGE = int(os.environ.get("KSTAGE", "5"))
KSUB = int(os.environ.get("KSUB", "9"))
KSUB3 = int(os.environ.get("KSUB3", "9"))
KPROBE = int(os.environ.get("KPROBE", "0"))
NCORE = int(os.environ.get("KCORES", "8"))

LN_EPS = 1e-5
RMS_EPS = 1e-6
QMAX = 127.0
QEPS = 1e-8
L1_EPS = 1e-10
RB = 256.0
LOG_HALF = float(np.log(0.5))


NRR = 6
VTS = 96


class Ctx:
    ENG = ("t", "v", "s", "g", "y")

    def __init__(self):
        self.ops = {e: [] for e in self.ENG}
        self.semnames = (["t", "v", "s", "g"]
                         + [f"y{i}" for i in range(NRR)]
                         + [f"g{i}" for i in range(NRR)]
                         + ["o0", "o1"])
        self.cnt = {n: 0 for n in self.semnames}
        self.hw = {}
        self.rr = {"y": 0, "g": 0, "o": 0}
        self.seen = {e: {} for e in self.ENG}

    def emit(self, eng, fn, waits=(), inc=None, n=1):
        real = []
        flat = []
        for w in waits:
            if w is None:
                continue
            if isinstance(w, tuple) and w and isinstance(w[0], tuple):
                flat.extend(w)
            else:
                flat.append(w)
        seen = {}
        for w in flat:
            s, v = w
            if v <= 0:
                continue
            if seen.get(s, 0) < v:
                seen[s] = v
        eseen = self.seen[eng]
        for s, v in sorted(seen.items()):
            if v <= eseen.get(s, 0):
                continue
            eseen[s] = v
            real.append((s, v))
        if inc is None:
            inc, n = eng, 1
        self.cnt[inc] += n
        tok = (inc, self.cnt[inc])
        self.ops[eng].append((fn, tuple(real), inc, n))
        return tok

    def dma(self, eng, fn, waits=(), queue=None, nsem=None):
        q = queue or eng
        nsem = nsem if nsem is not None else (NRR if q in ("y", "g") else 2)
        sem = f"{q}{self.rr[q] % nsem}"
        self.rr[q] += 1
        self_wait = (sem, self.cnt[sem])
        return self.emit(eng, fn, waits=list(waits) + [self_wait],
                         inc=sem, n=16)


def ts(i, n=128):
    return slice(i * n, (i + 1) * n)


def build_program():
    nc = bass.Bass("TRN2", target_bir_lowering=False)

    dx = nc.dram_tensor("x", [S, D], F32, kind="ExternalInput").ap()
    dwaT = nc.dram_tensor("waT", [D, D], F32, kind="ExternalInput").ap()
    dwxT = nc.dram_tensor("wxT", [D, D], F32, kind="ExternalInput").ap()
    dba = nc.dram_tensor("ba2", [128, NT], F32, kind="ExternalInput").ap()
    dbx = nc.dram_tensor("bx2", [128, NT], F32, kind="ExternalInput").ap()
    dlam = nc.dram_tensor("lam", [128, NT], F32, kind="ExternalInput").ap()
    dw1 = nc.dram_tensor("w1", [2 * IFF, D], F32, kind="ExternalInput").ap()
    dw2 = nc.dram_tensor("w2", [D, IFF], F32, kind="ExternalInput").ap()
    dwinT = nc.dram_tensor("winT", [D, CW], F32, kind="ExternalInput").ap()
    dwoutT = nc.dram_tensor("woutT", [D, D], F32, kind="ExternalInput").ap()
    drms = nc.dram_tensor("rms2", [128, NT], F32, kind="ExternalInput").ap()
    dl1w = nc.dram_tensor("l1w", [D, D], F32, kind="ExternalInput").ap()
    dl1b = nc.dram_tensor("l1b2", [128, NT], F32, kind="ExternalInput").ap()
    dident = nc.dram_tensor("identb", [128, 128], BF16, kind="ExternalInput").ap()
    didf = nc.dram_tensor("identf", [128, 128], F32, kind="ExternalInput").ap()
    dprope = nc.dram_tensor("prope", [128, 128], BF16, kind="ExternalInput").ap()
    dmask = nc.dram_tensor("mask", [128, 128], F32, kind="ExternalInput").ap()
    dcos = nc.dram_tensor("cosT", [128, S], F32, kind="ExternalInput").ap()
    dsin = nc.dram_tensor("sinT", [128, S], F32, kind="ExternalInput").ap()
    dones1 = nc.dram_tensor("ones1", [1, 128], F32, kind="ExternalInput").ap()
    donesd = nc.dram_tensor("onesd", [128, 1], BF16, kind="ExternalInput").ap()
    donesk = nc.dram_tensor("onesk", [128, 1], F32, kind="ExternalInput").ap()
    dcvec = nc.dram_tensor("cvec", [128, 8], F32, kind="ExternalInput").ap()

    dout = nc.dram_tensor("out", [S, D], F32, kind="ExternalOutput").ap()
    dh = nc.dram_tensor("h_scr", [IFF, S], BF16).ap()
    dw2q = nc.dram_tensor("w2q_scr", [D, IFF], BF16).ap()

    from contextlib import ExitStack
    es = ExitStack()
    with es:
        XT = es.enter_context(nc.sbuf_tensor([128, NT * S], BF16))
        ZT = es.enter_context(nc.sbuf_tensor([128, NT * S], BF16))
        U = es.enter_context(nc.sbuf_tensor([128, 12288], BF16))
        Fs = es.enter_context(nc.sbuf_tensor([128, 3 * 2048], F32))
        IQ = es.enter_context(nc.sbuf_tensor([128, 2048], I32))
        SM = es.enter_context(nc.sbuf_tensor([128, 64], F32))
        SRA = es.enter_context(nc.sbuf_tensor([16, 128], F32))
        SRB = es.enter_context(nc.sbuf_tensor([16, 128], F32))
        RWS = es.enter_context(nc.sbuf_tensor([1, 2048], F32))
        RWS2 = es.enter_context(nc.sbuf_tensor([1, 2048], F32))
        CID = es.enter_context(nc.sbuf_tensor([128, 128], BF16))
        CIDF = es.enter_context(nc.sbuf_tensor([128, 128], F32))
        CP = es.enter_context(nc.sbuf_tensor([128, 128], BF16))
        CM = es.enter_context(nc.sbuf_tensor([128, 128], F32))
        C1 = es.enter_context(nc.sbuf_tensor([1, 128], F32))
        CD = es.enter_context(nc.sbuf_tensor([128, 1], BF16))
        CK = es.enter_context(nc.sbuf_tensor([128, 1], F32))
        CV = es.enter_context(nc.sbuf_tensor([128, 8], F32))
        VBA2 = es.enter_context(nc.sbuf_tensor([128, NT], F32))
        VBX2 = es.enter_context(nc.sbuf_tensor([128, NT], F32))
        VLAM = es.enter_context(nc.sbuf_tensor([128, NT], F32))
        VC2 = es.enter_context(nc.sbuf_tensor([128, NT], F32))
        VRMS = es.enter_context(nc.sbuf_tensor([128, NT], F32))
        VL1B = es.enter_context(nc.sbuf_tensor([128, NT], F32))
        VS1 = es.enter_context(nc.sbuf_tensor([128, 2 * NI], F32))
        VSC = es.enter_context(nc.sbuf_tensor([128, NI], F32))
        VS2 = es.enter_context(nc.sbuf_tensor([128, NT], F32))
        VSL = es.enter_context(nc.sbuf_tensor([128, 2], F32))

        PA = es.enter_context(nc.psum_tensor([128, 2048], F32))
        PB = es.enter_context(nc.psum_tensor([128, 2048], F32))

        sems = {}
        for nme in Ctx().semnames:
            sems[nme] = es.enter_context(nc.semaphore("sem_" + nme))
        block = es.enter_context(nc.Block())

        C = Ctx()

        def y_dma(dst, src, waits=(), sem=None):
            return C.dma("y", lambda e, d=dst, s=src: e.dma_start(d, s), waits,
                         queue="o" if sem == "do" else "y")

        def g_dma(dst, src, waits=()):
            return C.dma("g", lambda e, d=dst, s=src: e.dma_start(d, s), waits)

        def y_trdma(dst, src, waits=()):
            return C.dma("y", lambda e, d=dst, s=src: e.dma_start_transpose(d, s),
                         waits)

        _ctoks = []
        for dst, src in ((CID[:, :], dident), (CIDF[:, :], didf), (CP[:, :], dprope),
                         (CM[:, :], dmask),
                         (C1[:, :], dones1), (CD[:, :], donesd),
                         (CK[:, :], donesk), (CV[:, :], dcvec), (VBA2[:, :], dba), (VBX2[:, :], dbx),
                         (VLAM[:, :], dlam), (VRMS[:, :], drms), (VL1B[:, :], dl1b)):
            _ctoks.append(y_dma(dst, src))
        tCl = tuple(_ctoks)

        t1 = C.emit("s", lambda e: e.activation(VC2[:, :], VLAM[:, :], AF.Exp,
                    bias=0.0, scale=-1.0), waits=[tCl])
        t2 = C.emit("v", lambda e: e.tensor_scalar_add(VC2[:, :], VC2[:, :], 1.0), waits=[t1])
        t3 = C.emit("s", lambda e: e.activation(VC2[:, :], VC2[:, :], AF.Ln,
                    bias=0.0, scale=1.0), waits=[t2])
        tok_c2 = C.emit("s", lambda e: e.mul(VC2[:, :], VC2[:, :], -4.0), waits=[t3])
        t5 = C.emit("s", lambda e: e.mul(VBA2[:, :], VBA2[:, :], 0.5), waits=[tCl])
        tok_b2 = C.emit("s", lambda e: e.mul(VBX2[:, :], VBX2[:, :], 0.5), waits=[t5])

        stage_tok = [None, None]
        for j in range(16):
            sb = j % 2
            buf = U[:, 4096 * sb:4096 * sb + 2048]
            buf2 = U[:, 4096 * sb + 2048:4096 * sb + 4096]
            tl = g_dma(buf, dx[ts(j), :], waits=[stage_tok[sb]])
            tt = y_trdma(bass.AP(U[:, :].tensor, 4096 * sb + 2048,
                                 [[12288, 128], [128, 16], [1, 128]]),
                         buf[:, :], waits=[tl])
            out_ap = bass.AP(XT[:, :].tensor, j * 128,
                             [[NT * S, 128], [S, NT], [1, 128]])
            tt2 = y_dma(out_ap, buf2[:, :], waits=[tt])
            stage_tok[sb] = tt2
        tok_x = stage_tok[1]

        def ln_feature_major(src_big, dst_big, eps_col, tok_ready, u_guard=None):
            prev_mm = u_guard
            for g in range(NT):
                zg = src_big[:, ts(g, S)]
                sq = U[:, ts(g % 2, 2048)]
                tsq = C.emit("s", lambda e, a=sq, b=zg: e.activation(
                    a, b, AF.Square, bias=0.0, scale=1.0),
                    waits=[tok_ready, prev_mm])
                for c in range(4):
                    C.emit("t", lambda e, g=g, c=c, zg=zg: e.matmul(
                        PA[0:1, ts(c, 512)], CD[:, :], zg[:, ts(c, 512)],
                        start=(g == 0), stop=(g == NT - 1)),
                        waits=[tok_ready, tCl])
                    prev_mm = C.emit("t", lambda e, g=g, c=c, sq=sq: e.matmul(
                        PB[0:1, ts(c, 512)], CD[:, :], sq[:, ts(c, 512)],
                        start=(g == 0), stop=(g == NT - 1)), waits=[tsq])
            r1 = C.emit("s", lambda e: e.activation(RWS[0:1, :], PA[0:1, 0:2048],
                        AF.Square, bias=0.0, scale=1.0), waits=[prev_mm])
            r2 = C.emit("v", lambda e: e.scalar_tensor_tensor(RWS[0:1, :],
                        PB[0:1, 0:2048], 1.0, RWS[0:1, :], ALU.mult, ALU.subtract),
                        waits=[r1])
            r3 = C.emit("s", lambda e: e.activation(RWS[0:1, :], RWS[0:1, :], AF.Ln,
                        bias=CV[0:1, eps_col:eps_col + 1], scale=1.0), waits=[r2])
            r4 = C.emit("s", lambda e: e.activation(RWS[0:1, :], RWS[0:1, :], AF.Exp,
                        bias=0.0, scale=CV[0:1, 4:5]), waits=[r3])
            b2 = C.emit("s", lambda e: e.copy(RWS2[0:1, :], PA[0:1, 0:2048]), waits=[r4])
            last = None
            for c in range(4):
                C.emit("t", lambda e, c=c: e.matmul(PA[:, ts(c, 512)], C1[:, :],
                       RWS[0:1, ts(c, 512)], start=True, stop=True), waits=[b2])
                last = C.emit("t", lambda e, c=c: e.matmul(PB[:, ts(c, 512)], C1[:, :],
                              RWS2[0:1, ts(c, 512)], start=True, stop=True), waits=[b2])
            v1 = C.emit("s", lambda e: e.copy(Fs[:, 0:2048], PA[:, 0:2048]), waits=[last])
            v2 = C.emit("s", lambda e: e.copy(Fs[:, 2048:4096], PB[:, 0:2048]), waits=[v1])
            out_tok = None
            for g in range(NT):
                zg = src_big[:, ts(g, S)]
                dg = dst_big[:, ts(g, S)]
                a1 = C.emit("v", lambda e, zg=zg: e.tensor_sub(
                    Fs[:, 4096:6144], zg, Fs[:, 2048:4096]), waits=[v2, out_tok])
                out_tok = C.emit("v", lambda e, dg=dg: e.tensor_mul(
                    dg, Fs[:, 4096:6144], Fs[:, 0:2048]), waits=[a1])
            return out_tok

        def max_stat_bcast(tok_acc):
            red = None
            for c in range(16):
                b = (c % 4) * 512
                tt = C.emit("t", lambda e, b=b, c=c: e.transpose(
                    PA[:, b:b + 128],
                    Fs[:, 4096 + c * 128:4096 + (c + 1) * 128], CIDF[:, :]),
                    waits=[tok_acc, red, tCl])
                red = C.emit("v", lambda e, b=b, c=c: e.reduce_max(
                    SM[:, c:c + 1], PA[:, b:b + 128], AX), waits=[tt])
            q1 = C.emit("v", lambda e: e.tensor_scalar(SM[:, 16:32], SM[:, 0:16],
                        1.0 / QMAX, QEPS, ALU.mult, ALU.max), waits=[red])
            q2 = C.emit("v", lambda e: e.reciprocal(SM[:, 32:48], SM[:, 16:32]), waits=[q1])
            q3a = C.emit("t", lambda e: e.transpose(PB[0:16, 0:128], SM[:, 16:32],
                         CIDF[:, :]), waits=[q2])
            q3b = C.emit("t", lambda e: e.transpose(PB[0:16, 512:640], SM[:, 32:48],
                         CIDF[:, :]), waits=[q2])
            q4a = C.emit("s", lambda e: e.copy(SRA[:, :], PB[0:16, 0:128]), waits=[q3a])
            q4b = C.emit("s", lambda e: e.copy(SRB[:, :], PB[0:16, 512:640]), waits=[q3b])
            c1 = y_dma(RWS[0:1, :], SRB[:, :], waits=[q4b])
            c2 = y_dma(RWS2[0:1, :], SRA[:, :], waits=[q4a])
            last = None
            for c in range(4):
                C.emit("t", lambda e, c=c: e.matmul(PA[:, ts(c, 512)], C1[:, :],
                       RWS[0:1, ts(c, 512)], start=True, stop=True), waits=[c1, c2])
                last = C.emit("t", lambda e, c=c: e.matmul(PB[:, ts(c, 512)], C1[:, :],
                              RWS2[0:1, ts(c, 512)], start=True, stop=True), waits=[c1, c2])
            b1 = C.emit("s", lambda e: e.copy(Fs[:, 0:2048], PA[:, 0:2048]), waits=[last])
            b2 = C.emit("s", lambda e: e.copy(Fs[:, 2048:4096], PB[:, 0:2048]), waits=[b1])
            return b2

        tok_final = tok_x

        HW2 = IFF // 2
        W2A, W2B = 22528, 22528 + HW2
        w2s = dict(pw=None, ptr=None, piv2=None, wq=None)
        w2st = []
        RUN_W2 = STAGE >= 2 and KSUB >= 3

        def emit_w2quant(m):
            ldA = g_dma(ZT[:, W2A:W2A + HW2], dw2[ts(m), 0:HW2],
                        waits=[w2s["ptr"]])
            ldB = g_dma(ZT[:, W2B:W2B + HW2], dw2[ts(m), HW2:IFF],
                        waits=[w2s["ptr"]])
            rdA = C.emit("v", lambda e: e.reduce_max(SM[:, 48:49],
                         ZT[:, W2A:W2A + HW2],
                         AX, apply_absolute_value=True), waits=[ldA, w2s["piv2"]])
            rdB = C.emit("v", lambda e: e.reduce_max(SM[:, 49:50],
                         ZT[:, W2B:W2B + HW2],
                         AX, apply_absolute_value=True), waits=[ldB, rdA])
            mx = C.emit("v", lambda e: e.tensor_max(SM[:, 48:49],
                        SM[:, 48:49], SM[:, 49:50]), waits=[rdB])
            sc = C.emit("v", lambda e, m=m: e.tensor_scalar(
                VS2[:, m:m + 1], SM[:, 48:49], 1.0 / QMAX, QEPS,
                ALU.mult, ALU.max), waits=[mx])
            iv = C.emit("v", lambda e, m=m: e.reciprocal(SM[:, 50:51],
                        VS2[:, m:m + 1]), waits=[sc, w2s["wq"]])
            w2s["piv2"] = iv
            half_toks = []
            for cc in range(4):
                lo = W2A + cc * 1408
                rq = C.emit("s", lambda e, lo=lo: e.activation(
                    IQ[:, 0:1408], ZT[:, lo:lo + 1408], AF.Copy, bias=RB,
                    scale=SM[:, 50:51]),
                    waits=[iv] + ([half_toks[-1]] if half_toks else [w2s["wq"]]))
                wq = C.emit("v", lambda e, lo=lo: e.tensor_scalar_add(
                    ZT[:, lo:lo + 1408], IQ[:, 0:1408], -256.0), waits=[rq])
                half_toks.append(wq)
            w2s["wq"] = half_toks[-1]
            stA = y_dma(dw2q[ts(m), 0:HW2], ZT[:, W2A:W2A + HW2],
                        waits=[half_toks[1], w2s["ptr"]])
            ptr = y_dma(dw2q[ts(m), HW2:IFF], ZT[:, W2B:W2B + HW2],
                        waits=[half_toks[3], stA])
            w2s["ptr"] = ptr
            w2st.append(ptr)

        if RUN_W2:
            for m in range(4):
                emit_w2quant(m)

        if STAGE >= 1:
            prev_ra = None
            prev_rb = None
            prev_z = None
            prev_s3 = None
            prev_v3 = None
            prev_v5 = None
            wtok = [stage_tok[0], stage_tok[1]]
            for g in range(NT):
                sb = g % 2
                base = 4096 * sb
                wa_st = U[:, base:base + 2048]
                wx_st = U[:, base + 2048:base + 4096]
                in_a = bass.AP(dwaT.tensor, g * 128, [[D, 128], [128 * D, 16], [1, 128]])
                in_x = bass.AP(dwxT.tensor, g * 128, [[D, 128], [128 * D, 16], [1, 128]])
                oa = bass.AP(U[:, :].tensor, base, [[12288, 128], [128, 16], [1, 128]])
                ox = bass.AP(U[:, :].tensor, base + 2048, [[12288, 128], [128, 16], [1, 128]])
                ta = g_dma(oa, in_a, waits=[wtok[sb]])
                txw = g_dma(ox, in_x, waits=[wtok[sb]])
                mm = None
                for c in range(4):
                    for k in range(NT):
                        mm = C.emit("t", lambda e, c=c, k=k, wa_st=wa_st: e.matmul(
                            PA[:, ts(c, 512)], wa_st[:, ts(k)],
                            XT[:, k * S + 512 * c:k * S + 512 * c + 512],
                            start=(k == 0), stop=(k == NT - 1)),
                            waits=[ta, tok_x, prev_ra])
                mm2 = None
                for c in range(4):
                    for k in range(NT):
                        mm2 = C.emit("t", lambda e, c=c, k=k, wx_st=wx_st: e.matmul(
                            PB[:, ts(c, 512)], wx_st[:, ts(k)],
                            XT[:, k * S + 512 * c:k * S + 512 * c + 512],
                            start=(k == 0), stop=(k == NT - 1)),
                            waits=[txw, prev_rb])
                wtok[sb] = mm2
                rt = U[:, 8192:10240]
                it = U[:, 10240:12288]
                prev_ra = C.emit("s", lambda e, g=g, rt=rt: e.activation(
                    rt, PA[:, 0:2048], AF.Tanh, bias=VBA2[:, g:g + 1], scale=0.5),
                    waits=[mm, tok_b2, prev_s3])
                prev_rb = C.emit("s", lambda e, g=g, it=it: e.activation(
                    it, PB[:, 0:2048], AF.Tanh, bias=VBX2[:, g:g + 1], scale=0.5),
                    waits=[mm2, prev_v3])
                s3 = C.emit("s", lambda e, g=g, rt=rt: e.activation(
                    Fs[:, 0:2048], rt, AF.Exp, bias=VC2[:, g:g + 1],
                    scale=VC2[:, g:g + 1]), waits=[prev_ra, tok_c2, prev_v5])
                prev_s3 = s3
                v1 = C.emit("v", lambda e: e.tensor_mul(Fs[:, 2048:4096],
                            Fs[:, 0:2048], Fs[:, 0:2048]), waits=[s3])
                v2 = C.emit("v", lambda e: e.tensor_scalar(Fs[:, 2048:4096],
                            Fs[:, 2048:4096], -1.0, 1.0, ALU.mult, ALU.add), waits=[v1])
                s4 = C.emit("s", lambda e: e.activation(Fs[:, 2048:4096],
                            Fs[:, 2048:4096], AF.Ln, bias=0.0, scale=1.0), waits=[v2])
                s5 = C.emit("s", lambda e: e.activation(Fs[:, 2048:4096],
                            Fs[:, 2048:4096], AF.Exp, bias=CV[:, 0:1], scale=0.5), waits=[s4])
                v3 = C.emit("v", lambda e, g=g, it=it: e.scalar_tensor_tensor(
                    Fs[:, 4096:6144], it, 1.0, XT[:, ts(g, S)], ALU.add, ALU.mult),
                    waits=[prev_rb, tok_x])
                prev_v3 = v3
                v4 = C.emit("v", lambda e: e.tensor_mul(Fs[:, 2048:4096],
                            Fs[:, 2048:4096], Fs[:, 4096:6144]), waits=[v3, s5])
                v5 = C.emit("v", lambda e: e.tensor_tensor_scan(
                    Fs[:, 4096:6144], Fs[:, 0:2048], Fs[:, 2048:4096], 0.0,
                    ALU.mult, ALU.add), waits=[v4])
                prev_v5 = v5
                if RUN_W2 and g < 12:
                    emit_w2quant(g + 4)
                zguard = w2st[-1] if (RUN_W2 and g in (11, 12, 13)) else None
                prev_z = C.emit("v", lambda e, g=g: e.tensor_add(
                    ZT[:, ts(g, S)], Fs[:, 4096:6144], XT[:, ts(g, S)]),
                    waits=[v5, zguard])
            tok_x1 = ln_feature_major(ZT, XT, 2, prev_z, u_guard=wtok[1])
            tok_final = tok_x1

        if STAGE >= 2:
            acc = None
            sprev = None
            for g in range(NT):
                sprev = C.emit("s", lambda e, g=g: e.activation(U[:, 0:2048],
                               XT[:, ts(g, S)], AF.Abs, bias=0.0, scale=1.0),
                               waits=[tok_x1, acc])
                if g == 0:
                    acc = C.emit("v", lambda e: e.tensor_copy(Fs[:, 4096:6144],
                                 U[:, 0:2048]), waits=[sprev])
                else:
                    acc = C.emit("v", lambda e: e.tensor_max(Fs[:, 4096:6144],
                                 Fs[:, 4096:6144], U[:, 0:2048]), waits=[sprev])
            tok_bc = max_stat_bcast(acc)
            if KSUB == 0:
                tok_final = tok_bc
            tq = None
            prev_iqr = w2s["wq"]
            for g in range(NT):
                w1_ = C.emit("v", lambda e, g=g: e.tensor_mul(
                    Fs[:, 4096:6144], XT[:, ts(g, S)], Fs[:, 0:2048]),
                    waits=[tok_bc, tq])
                w2_ = C.emit("s", lambda e: e.activation(IQ[:, :], Fs[:, 4096:6144],
                             AF.Copy, bias=RB, scale=1.0), waits=[w1_, prev_iqr])
                w3_ = C.emit("v", lambda e: e.tensor_scalar_add(Fs[:, 4096:6144],
                             IQ[:, :], -256.0), waits=[w2_])
                prev_iqr = w3_
                tq = C.emit("v", lambda e, g=g: e.tensor_mul(
                    ZT[:, ts(g, S)], Fs[:, 4096:6144], Fs[:, 2048:4096]), waits=[w3_])
            tok_xq = tq
            if KSUB == 1:
                tok_final = tok_xq

            RUN_GU = KSUB >= 2
            hacc = None
            prev_e1 = None
            prev_e2 = None
            prev_e3 = None
            prev_rq = {"g": None, "u": None}
            prev_wq = w2s["wq"]
            h_done = None
            st_free = None
            prev_iv = w2s["piv2"]
            prev_trt = {"g": None, "u": None}
            e1_hist = [None, None]
            gs = dict(prev_rq=prev_rq, prev_wq=prev_wq, prev_iv=prev_iv,
                      prev_trt=prev_trt, st_free=None)

            def emit_w1quant(i):
                wgl = g_dma(U[:, 0:2048], dw1[ts(i), :],
                            waits=[gs["prev_rq"]["g"], gs["prev_trt"]["g"],
                                   tok_xq if i == 0 else None])
                wul = g_dma(U[:, 2048:4096], dw1[ts(NI + i), :],
                            waits=[gs["prev_rq"]["u"], gs["prev_trt"]["u"]])
                trt = {}
                for nm, off, ld in (("g", 0, wgl), ("u", 2048, wul)):
                    wn = U[:, off:off + 2048]
                    col = i if nm == "g" else NI + i
                    rd = C.emit("v", lambda e, wn=wn: e.reduce_max(
                        SM[:, 48:49], wn, AX, apply_absolute_value=True),
                        waits=[ld, gs["prev_iv"]])
                    sc = C.emit("v", lambda e, col=col: e.tensor_scalar(
                        VS1[:, col:col + 1], SM[:, 48:49], 1.0 / QMAX, QEPS,
                        ALU.mult, ALU.max), waits=[rd])
                    iv = C.emit("v", lambda e, col=col: e.reciprocal(
                        SM[:, 50:51], VS1[:, col:col + 1]),
                        waits=[sc, gs["prev_rq"][nm]])
                    gs["prev_iv"] = iv
                    rq = C.emit("s", lambda e, wn=wn: e.activation(IQ[:, :], wn,
                                AF.Copy, bias=RB, scale=SM[:, 50:51]),
                                waits=[iv, gs["prev_wq"]])
                    gs["prev_rq"][nm] = rq
                    wq = C.emit("v", lambda e, wn=wn: e.tensor_scalar_add(
                        wn, IQ[:, :], -256.0), waits=[rq])
                    gs["prev_wq"] = wq
                    so_ = 4096 + (0 if nm == "g" else 2048)
                    ot = bass.AP(U[:, :].tensor, so_, [[12288, 128], [128, 16], [1, 128]])
                    trt[nm] = y_trdma(ot, wn, waits=[wq, gs["st_free"]])
                    gs["prev_trt"][nm] = trt[nm]
                cs = C.emit("v", lambda e, i=i: e.tensor_mul(VSC[:, i:i + 1],
                            VS1[:, i:i + 1], VS1[:, NI + i:NI + i + 1]),
                            waits=[gs["prev_rq"]["u"], gs["prev_rq"]["g"]])
                cs2 = C.emit("v", lambda e, i=i: e.tensor_scalar_mul(
                    VSC[:, i:i + 1], VSC[:, i:i + 1], 0.5), waits=[cs])
                tcol = 54 + (i % 2)
                tsc = C.emit("v", lambda e, i=i, tcol=tcol: e.tensor_scalar_mul(
                    SM[:, tcol:tcol + 1], VS1[:, i:i + 1], 0.5),
                    waits=[cs2, e1_hist[i % 2]])
                trt["tsc"] = tsc
                trt["cs2"] = cs2
                return trt

            trt_cur = emit_w1quant(0) if RUN_GU else None
            for i in range(NI if RUN_GU else 0):
                trt = trt_cur
                mmg = None
                for c in range(4):
                    for k in range(NT):
                        mmg = C.emit("t", lambda e, c=c, k=k: e.matmul(
                            PA[:, ts(c, 512)],
                            U[:, 4096 + k * 128:4096 + k * 128 + 128],
                            ZT[:, k * S + 512 * c:k * S + 512 * c + 512],
                            start=(k == 0), stop=(k == NT - 1)),
                            waits=[trt["g"], tok_xq, prev_e2])
                mmu = None
                for c in range(4):
                    for k in range(NT):
                        mmu = C.emit("t", lambda e, c=c, k=k: e.matmul(
                            PB[:, ts(c, 512)],
                            U[:, 6144 + k * 128:6144 + k * 128 + 128],
                            ZT[:, k * S + 512 * c:k * S + 512 * c + 512],
                            start=(k == 0), stop=(k == NT - 1)),
                            waits=[trt["u"], prev_e3])
                gs["st_free"] = mmu
                if i + 1 < NI:
                    trt_cur = emit_w1quant(i + 1)
                tcol = 54 + (i % 2)
                e1 = C.emit("s", lambda e, tcol=tcol: e.activation(
                            U[:, 8192:10240], PA[:, 0:2048], AF.Tanh, bias=0.0,
                            scale=SM[:, tcol:tcol + 1]),
                            waits=[mmg, trt["tsc"], h_done])
                e1_hist[i % 2] = e1
                prev_e1 = e1
                e2 = C.emit("v", lambda e: e.scalar_tensor_tensor(Fs[:, 0:2048],
                            U[:, 8192:10240], 1.0, PA[:, 0:2048], ALU.add, ALU.mult),
                            waits=[e1])
                prev_e2 = e2
                e3 = C.emit("v", lambda e: e.tensor_mul(Fs[:, 0:2048], Fs[:, 0:2048],
                            PB[:, 0:2048]), waits=[e2, mmu])
                prev_e3 = e3
                e4 = C.emit("v", lambda e, i=i: e.tensor_scalar_mul(
                    U[:, 10240:12288], Fs[:, 0:2048], VSC[:, i:i + 1]),
                    waits=[e3, trt["cs2"], h_done])
                sab = C.emit("s", lambda e: e.activation(Fs[:, 2048:4096],
                             U[:, 10240:12288], AF.Abs, bias=0.0, scale=1.0), waits=[e4])
                if i == 0:
                    hacc = C.emit("v", lambda e: e.tensor_copy(Fs[:, 4096:6144],
                                  Fs[:, 2048:4096]), waits=[sab])
                else:
                    hacc = C.emit("v", lambda e: e.tensor_max(Fs[:, 4096:6144],
                                  Fs[:, 4096:6144], Fs[:, 2048:4096]), waits=[sab])
                h_done = y_dma(dh[ts(i), :], U[:, 10240:12288], waits=[e4])
            prev_wq = gs["prev_wq"]
            tok_hbc = max_stat_bcast(hacc) if RUN_GU else None
            if KSUB == 2:
                tok_final = tok_hbc

            qts0 = []
            prev_hq = tok_xq
            if KSUB >= 4:
                piq0 = prev_wq
                for i in range(NI):
                    hz = ZT[:, i * 512:(i + 1) * 512]
                    hl = g_dma(hz, dh[ts(i), 0:512],
                               waits=[prev_hq, tok_hbc, h_done])
                    q1 = C.emit("v", lambda e, hz=hz: e.tensor_mul(
                        Fs[:, 4096:4608], hz, Fs[:, 0:512]),
                        waits=[hl, qts0[-1] if qts0 else None])
                    q2 = C.emit("s", lambda e: e.activation(IQ[:, 1536:2048],
                                Fs[:, 4096:4608], AF.Copy, bias=RB, scale=1.0),
                                waits=[q1, piq0])
                    q3 = C.emit("v", lambda e, hz=hz: e.tensor_scalar_add(
                        hz, IQ[:, 1536:2048], -256.0), waits=[q2])
                    piq0 = q3
                    qts0.append(q3)
                prev_hq = qts0[-1]

            prev_ev = None
            prev_wst = [(gs["prev_trt"]["g"], gs["prev_trt"]["u"], tok_hbc),
                        (h_done, tok_hbc)]
            last_mm_prev_chunk = None
            bank_rd = [None] * 4
            prev_d2 = None
            for c in range(4 if KSUB >= 4 else 0):
                if c == 0:
                    qts = qts0
                else:
                    qts = []
                    piq = None
                    for i in range(NI):
                        hz = ZT[:, i * 512:(i + 1) * 512]
                        hl = g_dma(hz, dh[ts(i), c * 512:c * 512 + 512],
                                   waits=[prev_hq, tok_hbc, h_done,
                                          last_mm_prev_chunk])
                        q1 = C.emit("v", lambda e, hz=hz, c=c: e.tensor_mul(
                            Fs[:, 4096:4608], hz, Fs[:, c * 512:c * 512 + 512]),
                            waits=[hl, qts[-1] if qts else None])
                        q2 = C.emit("s", lambda e: e.activation(IQ[:, 1536:2048],
                                    Fs[:, 4096:4608], AF.Copy, bias=RB, scale=1.0),
                                    waits=[q1, piq])
                        q3 = C.emit("v", lambda e, hz=hz: e.tensor_scalar_add(
                            hz, IQ[:, 1536:2048], -256.0), waits=[q2])
                        piq = q3
                        qts.append(q3)
                    prev_hq = qts[-1]
                for m in range(NT):
                    sb = m % 2
                    wl = y_trdma(bass.AP(U[:, :].tensor, sb * 6144,
                                         [[12288, 128], [128, NI], [1, 128]]),
                                 dw2q[ts(m), :],
                                 waits=[prev_wst[sb], w2st[m] if w2st else None])
                    if sb == 0:
                        last_wl0 = wl
                    else:
                        last_wl = wl
                    mm = None
                    for k in range(NI):
                        mm = C.emit("t", lambda e, m=m, k=k, sb=sb: e.matmul(
                            PA[:, ts(m % 4, 512)],
                            U[:, sb * 6144 + k * 128:sb * 6144 + k * 128 + 128],
                            ZT[:, k * 512:(k + 1) * 512],
                            start=(k == 0), stop=(k == NI - 1)),
                            waits=[wl, qts[NI - 1], bank_rd[m % 4]])
                    prev_wst[sb] = mm
                    last_mm_prev_chunk = mm
                    d1 = C.emit("s", lambda e, m=m: e.activation(Fs[:, 4608:5120],
                                PA[:, ts(m % 4, 512)], AF.Copy, bias=0.0,
                                scale=VS2[:, m:m + 1]), waits=[mm, prev_d2])
                    bank_rd[m % 4] = d1
                    d2 = C.emit("v", lambda e, c=c: e.tensor_mul(Fs[:, 4608:5120],
                                Fs[:, 4608:5120],
                                Fs[:, 2048 + c * 512:2048 + c * 512 + 512]),
                                waits=[d1])
                    prev_d2 = d2
                    d3 = C.emit("v", lambda e, m=m, c=c: e.tensor_add(
                        Fs[:, 5120:5632], Fs[:, 4608:5120],
                        XT[:, m * S + c * 512:m * S + c * 512 + 512]),
                        waits=[d2, prev_ev])
                    prev_ev = C.emit("s", lambda e, m=m, c=c: e.copy(
                        XT[:, m * S + c * 512:m * S + c * 512 + 512],
                        Fs[:, 5120:5632]), waits=[d3])
            if KSUB >= 4:
                tok_x2 = ln_feature_major(XT, XT, 2, prev_ev, u_guard=prev_wst[1])
                tok_final = tok_x2
                guard2 = (tok_x2, prev_wst[0], prev_wst[1], last_wl, last_wl0)


        if STAGE >= 3 and KSUB3 >= 1:
            prev_mm = guard2
            for g in range(NT):
                sq = U[:, ts(g % 2, 2048)]
                tsq = C.emit("s", lambda e, a=sq, g=g: e.activation(
                    a, XT[:, ts(g, S)], AF.Square, bias=0.0, scale=1.0),
                    waits=[tok_x2, prev_mm])
                for c in range(4):
                    prev_mm = C.emit("t", lambda e, g=g, c=c, sq=sq: e.matmul(
                        PA[0:1, ts(c, 512)], CD[:, :], sq[:, ts(c, 512)],
                        start=(g == 0), stop=(g == NT - 1)), waits=[tsq])
            r3 = C.emit("s", lambda e: e.activation(RWS[0:1, :], PA[0:1, 0:2048],
                        AF.Ln, bias=CV[0:1, 3:4], scale=1.0), waits=[prev_mm])
            r4 = C.emit("s", lambda e: e.activation(RWS[0:1, :], RWS[0:1, :], AF.Exp,
                        bias=0.0, scale=CV[0:1, 4:5]), waits=[r3])
            last = None
            for c in range(4):
                last = C.emit("t", lambda e, c=c: e.matmul(PA[:, ts(c, 512)], C1[:, :],
                              RWS[0:1, ts(c, 512)], start=True, stop=True), waits=[r4])
            tok_rrms = C.emit("s", lambda e: e.copy(Fs[:, 0:2048], PA[:, 0:2048]),
                              waits=[last])
            prev_stage = (prev_mm, tok_rrms) + guard2
            tok_cos = y_dma(Fs[:, 2048:4096], dcos, waits=[tok_x2])
            tok_sin = y_dma(Fs[:, 4096:6144], dsin, waits=[tok_x2])
            vrms_ap = bass.AP(VRMS[:, :].tensor, 0, [[NT, 128], [1, NT], [0, 128]])
            stage3 = bass.AP(U[:, :].tensor, 0, [[12288, 128], [128, 16], [1, 128]])

            prev_ostrip = None
            prev_vt = None
            prev_exp = [None] * 4
            prev_oev = None
            prev_rope = None
            oev_bank = [None] * 4
            bc_prev = None
            blk_free = [prev_stage] * NT
            for g in range(NG if KSUB3 >= 4 else 1):
                qtiles = []
                prev_ev_sub = None
                for sub in range(3):
                    sck = []
                    for k in range(NT):
                        dstb = U[:, k * 128:(k + 1) * 128]
                        if sub < 2:
                            col0 = g * 256 + sub * 128
                            lds = [g_dma(dstb,
                                   bass.AP(dwinT.tensor, col0 + k * 128 * CW,
                                           [[CW, 128], [1, 128]]),
                                   waits=[blk_free[k]])]
                        else:
                            lds = [g_dma(U[:, k * 128:k * 128 + 64],
                                   bass.AP(dwinT.tensor, 2048 + g * 64 + k * 128 * CW,
                                           [[CW, 128], [1, 64]]),
                                   waits=[blk_free[k]]),
                                   g_dma(U[:, k * 128 + 64:(k + 1) * 128],
                                   bass.AP(dwinT.tensor, 2560 + g * 64 + k * 128 * CW,
                                           [[CW, 128], [1, 64]]),
                                   waits=[blk_free[k]])]
                        vrb = bass.AP(VRMS[:, :].tensor, k, [[NT, 128], [0, 128]])
                        sck.append(C.emit("v", lambda e, dstb=dstb, vrb=vrb:
                            e.tensor_tensor(dstb, dstb, vrb, ALU.mult),
                            waits=lds + [tCl]))
                    mm = None
                    for k in range(NT):
                        for c in range(4):
                            mm = C.emit("t", lambda e, c=c, k=k: e.matmul(
                                PA[:, ts(c, 512)], U[:, ts(k)],
                                XT[:, k * S + 512 * c:k * S + 512 * c + 512],
                                start=(k == 0), stop=(k == NT - 1)),
                                waits=[sck[k], tok_x2, prev_rope, prev_ev_sub])
                        blk_free[k] = mm
                    prev_stage = mm
                    dst = U[:, 2048 + sub * 2048:4096 + sub * 2048]
                    ev = C.emit("v", lambda e, dst=dst: e.tensor_mul(
                        dst, PA[:, 0:2048], Fs[:, 0:2048]), waits=[mm, tok_rrms])
                    prev_ev_sub = ev
                    qtiles.append((dst, ev))
                rope_toks = [prev_ev_sub]
                for ri in range(3 if KSUB3 >= 2 else 0):
                    dst, ev = qtiles[ri]
                    nrow = 128 if ri < 2 else 64
                    pm = None
                    for c in range(4):
                        pm = C.emit("t", lambda e, c=c, dst=dst, nrow=nrow: e.matmul(
                            PA[0:nrow, ts(c, 512)], CP[0:nrow, 0:nrow],
                            dst[0:nrow, ts(c, 512)], start=True, stop=True),
                            waits=[ev, tCl, prev_rope, rope_toks[-1]])
                        a1 = C.emit("v", lambda e, c=c, dst=dst, nrow=nrow:
                                    e.tensor_mul(PB[0:nrow, ts(c, 512)],
                                    dst[0:nrow, ts(c, 512)],
                                    Fs[0:nrow, 2048 + c * 512:2048 + c * 512 + 512]),
                                    waits=[ev, tok_cos, prev_oev])
                        a2 = C.emit("v", lambda e, c=c, nrow=nrow: e.tensor_mul(
                            Fs[0:nrow, 0:512] if False else U[0:nrow, 8192 + c * 512:8192 + c * 512 + 512],
                            PA[0:nrow, ts(c, 512)],
                            Fs[0:nrow, 4096 + c * 512:4096 + c * 512 + 512]),
                            waits=[pm, tok_sin])
                        a3 = C.emit("v", lambda e, c=c, dst=dst, nrow=nrow:
                                    e.tensor_add(dst[0:nrow, ts(c, 512)],
                                    PB[0:nrow, ts(c, 512)],
                                    U[0:nrow, 8192 + c * 512:8192 + c * 512 + 512]),
                                    waits=[a1, a2])
                        rope_toks.append(a3)
                    prev_rope = rope_toks[-1]
                kvt, _ = qtiles[2]
                vt0 = 10240
                if KSUB3 >= 3:
                    ttr = y_trdma(bass.AP(U[:, :].tensor, 8192,
                                          [[12288, 128], [128, 16], [1, 128]]),
                                  kvt[:, :], waits=[qtiles[2][1], prev_rope, prev_vt])
                    vts = []
                    for c in range(16):
                        sv = C.emit("s", lambda e, c=c: e.copy(
                            U[:, vt0 + c * VTS:vt0 + c * VTS + 64],
                            U[:, 8192 + c * 128 + 64:8192 + c * 128 + 128]), waits=[ttr])
                        vts.append(sv)
                    on = C.emit("v", lambda e: e.memset(
                        bass.AP(U[:, :].tensor, vt0 + 64, [[12288, 128], [VTS, 16]]), 1.0),
                        waits=[vts[-1], prev_vt])
                    tok_vt = on
                    prev_vt = on
                    kdup = g_dma(kvt[64:128, :], kvt[0:64, :], waits=[ttr, prev_rope])
                for sp in range(4 if KSUB3 >= 3 else 0):
                    q0 = sp * 512
                    nk = sp * 4 + 4
                    mmq = {}
                    av_last = {}
                    for sk in range(nk):
                        for h in range(4):
                            qt = U[:, 2048 + (h // 2) * 2048:4096 + (h // 2) * 2048]
                            hb = 64 * (h % 2)
                            mv = qt[hb:hb + 64, q0:q0 + 512]
                            mmq[h] = C.emit("t", lambda e, h=h, sk=sk, mv=mv,
                                            kvt=kvt, hb=hb:
                                            e.matmul(PA[:, ts(h, 512)],
                                            kvt[hb:hb + 64, ts(sk)], mv,
                                            start=True, stop=True),
                                            waits=[prev_rope, tok_vt, kdup,
                                                   prev_exp[h]])
                        msk = None
                        if sk >= sp * 4:
                            off = sk * 128 - q0
                            for h in range(4):
                                msk = C.emit("v", lambda e, h=h, off=off: e.tensor_add(
                                    PA[:, h * 512 + off:h * 512 + off + 128],
                                    PA[:, h * 512 + off:h * 512 + off + 128],
                                    CM[:, :]), waits=[mmq[h], tCl])
                        for h in range(4):
                            prev_exp[h] = C.emit("s", lambda e, h=h: e.activation(
                                U[:, 8192 + h * 512:8192 + h * 512 + 512],
                                PA[:, ts(h, 512)], AF.Exp, bias=CV[:, 5:6],
                                scale=CV[:, 6:7]),
                                waits=[mmq[h], msk, oev_bank[h]])
                            if sk > sp * 4:
                                off = sk * 128 - q0
                                prev_exp[h] = C.emit("v", lambda e, h=h, off=off:
                                    e.memset(U[:, 8192 + h * 512:
                                               8192 + h * 512 + off], 0.0),
                                    waits=[prev_exp[h]])
                        for h in range(4):
                            av_last[h] = C.emit("t", lambda e, h=h, sk=sk, nk=nk:
                                e.matmul(
                                PB[0:65, ts(h, 512)],
                                U[:, vt0 + sk * VTS:vt0 + sk * VTS + 65],
                                U[:, 8192 + h * 512:8192 + h * 512 + 512],
                                start=(sk == 0), stop=(sk == nk - 1)),
                                waits=[prev_exp[h]] +
                                      ([oev_bank[h]] if sk == 0 else []))
                    otoks = []
                    for h in range(4):
                        head = 4 * g + h
                        rr = C.emit("v", lambda e, h=h: e.reciprocal(
                            RWS[0:1, 0:512], PB[64:65, ts(h, 512)]),
                            waits=[av_last[h], bc_prev])
                        bc = C.emit("t", lambda e, h=h: e.matmul(
                                    PB[64:128, ts(h, 512)],
                                    C1[0:1, 0:64], RWS[0:1, 0:512],
                                    start=True, stop=True), waits=[rr])
                        bc_prev = bc
                        oslot = U[0:64, 8192 + h * 512:8192 + h * 512 + 512]
                        oe = C.emit("s", lambda e, h=h, oslot=oslot: e.copy(
                            oslot, PB[0:64, ts(h, 512)]), waits=[av_last[h]])
                        zslot = ZT[64 * (head % 2):64 * (head % 2) + 64,
                                   (head // 2) * S + q0:(head // 2) * S + q0 + 512]
                        ov = C.emit("v", lambda e, h=h, zslot=zslot, oslot=oslot:
                                    e.tensor_mul(zslot, oslot,
                                                 PB[64:128, ts(h, 512)]),
                                    waits=[oe, bc])
                        oev_bank[h] = (oe, ov)
                        otoks.extend([oe, ov])
                    prev_oev = tuple(otoks)
            prev_ev3 = None
            ev3_bank = [None, None]
            cp_last = [None, None]
            wst3 = [None, None]
            for m in range(NT if KSUB3 >= 5 else 0):
                sb3 = m % 2
                stage3m = bass.AP(U[:, :].tensor, sb3 * 2048,
                                  [[12288, 128], [128, 16], [1, 128]])
                wl = g_dma(stage3m, bass.AP(dwoutT.tensor, m * 128,
                           [[D, 128], [128 * D, 16], [1, 128]]),
                           waits=[prev_stage, prev_oev, wst3[sb3]])
                PM = PA if sb3 == 0 else PB
                mm = None
                for c in range(4):
                    for k in range(NT):
                        mm = C.emit("t", lambda e, c=c, k=k, PM=PM, sb3=sb3:
                            e.matmul(
                            PM[:, ts(c, 512)],
                            U[:, sb3 * 2048 + k * 128:sb3 * 2048 + k * 128 + 128],
                            ZT[:, k * S + 512 * c:k * S + 512 * c + 512],
                            start=(k == 0), stop=(k == NT - 1)),
                            waits=[wl, prev_oev, ev3_bank[sb3]])
                prev_stage = mm
                wst3[sb3] = mm
                last_v = None
                for c in range(4):
                    v1_ = C.emit("v", lambda e, m=m, c=c, PM=PM, sb3=sb3:
                        e.scalar_tensor_tensor(
                        Fs[:, sb3 * 2048 + c * 512:sb3 * 2048 + c * 512 + 512],
                        XT[:, m * S + c * 512:m * S + c * 512 + 512], 2.0,
                        PM[:, ts(c, 512)], ALU.mult, ALU.add),
                        waits=[mm, cp_last[sb3] if c == 0 else None])
                    prev_ev3 = C.emit("s", lambda e, m=m, c=c, sb3=sb3: e.copy(
                        XT[:, m * S + c * 512:m * S + c * 512 + 512],
                        Fs[:, sb3 * 2048 + c * 512:sb3 * 2048 + c * 512 + 512]),
                        waits=[v1_])
                    last_v = v1_
                ev3_bank[sb3] = last_v
                cp_last[sb3] = prev_ev3
            if KSUB3 >= 5:
                tok_x3 = ln_feature_major(XT, XT, 2, prev_ev3, u_guard=prev_stage)
                tok_final = tok_x3

        if STAGE >= 4:
            pacc = None
            pld = None
            for m in range(NT):
                ld = g_dma(U[:, 0:2048], dl1w[ts(m), :], waits=[pld, tok_x3])
                ab = C.emit("s", lambda e: e.activation(Fs[:, 0:2048], U[:, 0:2048],
                            AF.Abs, bias=0.0, scale=1.0), waits=[ld, pacc])
                pld = ab
                rd = C.emit("v", lambda e: e.reduce_sum(SM[:, 52:53], Fs[:, 0:2048],
                            AX), waits=[ab])
                if m == 0:
                    pacc = C.emit("v", lambda e: e.tensor_copy(SM[:, 53:54],
                                  SM[:, 52:53]), waits=[rd])
                else:
                    pacc = C.emit("v", lambda e: e.tensor_add(SM[:, 53:54],
                                  SM[:, 53:54], SM[:, 52:53]), waits=[rd])
            gs = C.emit("t", lambda e: e.matmul(PA[0:1, 0:1], CK[:, :],
                        SM[:, 53:54], start=True, stop=True), waits=[pacc, tCl])
            g1_ = C.emit("s", lambda e: e.mul(RWS[0:1, 0:1], PA[0:1, 0:1],
                         1.0 / (D * D)), waits=[gs])
            g2_ = C.emit("v", lambda e: e.tensor_scalar_add(RWS[0:1, 1:2],
                         RWS[0:1, 0:1], L1_EPS), waits=[g1_])
            g3_ = C.emit("v", lambda e: e.reciprocal(RWS[0:1, 2:3], RWS[0:1, 1:2]),
                         waits=[g2_])
            bc1 = C.emit("t", lambda e: e.matmul(PB[:, 0:1], C1[0:1, :],
                         RWS[0:1, 2:3], start=True, stop=True), waits=[g3_])
            bc2 = C.emit("t", lambda e: e.matmul(PB[:, 512:513], C1[0:1, :],
                         RWS[0:1, 0:1], start=True, stop=True), waits=[g3_])
            sv1 = C.emit("s", lambda e: e.copy(VSL[:, 0:1], PB[:, 0:1]), waits=[bc1])
            sv2 = C.emit("s", lambda e: e.copy(VSL[:, 1:2], PB[:, 512:513]), waits=[bc2])
            prev_rq4 = [None, None]
            prev_tr4 = [None, None]
            prev_t1 = None
            prev_z4 = None
            prev_mm4 = [None, None]
            psv_bank = [None, None]
            prev_cp4 = [None, None]
            for m in range(NT):
                sb4 = m % 2
                nat = U[:, sb4 * 2048:sb4 * 2048 + 2048]
                ld = g_dma(nat, dl1w[ts(m), :],
                           waits=[prev_rq4[sb4], prev_tr4[sb4], sv1, sv2])
                rq = C.emit("s", lambda e, nat=nat: e.activation(IQ[:, :], nat,
                            AF.Copy, bias=RB, scale=VSL[:, 0:1]),
                            waits=[ld, prev_t1])
                t1_ = C.emit("v", lambda e: e.tensor_scalar(Fs[:, 4096:6144],
                             IQ[:, :], -(RB), -1.0, ALU.add, ALU.max), waits=[rq])
                prev_t1 = t1_
                t2_ = C.emit("v", lambda e, nat=nat: e.tensor_scalar(
                    nat, Fs[:, 4096:6144], 1.0, 1.0, ALU.min, ALU.mult),
                    waits=[t1_])
                prev_rq4[sb4] = t2_
                so4 = 4096 + sb4 * 2048
                ot = bass.AP(U[:, :].tensor, so4, [[12288, 128], [128, 16], [1, 128]])
                tr = y_trdma(ot, nat, waits=[t2_, prev_mm4[sb4]])
                prev_tr4[sb4] = tr
                PM = PA if sb4 == 0 else PB
                mm = None
                for c in range(4):
                    for k in range(NT):
                        mm = C.emit("t", lambda e, c=c, k=k, PM=PM, so4=so4:
                            e.matmul(
                            PM[:, ts(c, 512)],
                            U[:, so4 + k * 128:so4 + k * 128 + 128],
                            XT[:, k * S + 512 * c:k * S + 512 * c + 512],
                            start=(k == 0), stop=(k == NT - 1)),
                            waits=[tr, psv_bank[sb4]])
                prev_mm4[sb4] = mm
                for c in range(4):
                    sv = C.emit("s", lambda e, c=c, PM=PM, sb4=sb4: e.activation(
                        Fs[:, sb4 * 2048 + c * 512:sb4 * 2048 + c * 512 + 512],
                        PM[:, ts(c, 512)], AF.Copy, bias=0.0, scale=VSL[:, 1:2]),
                        waits=[mm, prev_cp4[sb4] if c == 0 else None])
                    prev_z4 = C.emit("v", lambda e, m=m, c=c, sb4=sb4:
                        e.scalar_tensor_tensor(
                        ZT[:, m * S + c * 512:m * S + c * 512 + 512],
                        Fs[:, sb4 * 2048 + c * 512:sb4 * 2048 + c * 512 + 512],
                        VL1B[:, m:m + 1],
                        XT[:, m * S + c * 512:m * S + c * 512 + 512],
                        ALU.add, ALU.add), waits=[sv])
                    psv_bank[sb4] = sv
                prev_cp4[sb4] = prev_z4
            tok_x4 = ln_feature_major(ZT, XT, 2, prev_z4,
                                      u_guard=(prev_tr4[0], prev_tr4[1]))
            tok_final = tok_x4

        out_done = []
        prev_cv = [None, None]
        for j in range(16 if not KPROBE else 0):
            sb = j % 2
            g1 = U[:, 4096 * sb:4096 * sb + 2048]
            g2 = U[:, 4096 * sb + 2048:4096 * sb + 4096]
            it_ap = bass.AP(XT[:, :].tensor, j * 128,
                            [[NT * S, 128], [S, NT], [1, 128]])
            td = y_dma(g1[:, :], it_ap, waits=[tok_final, prev_cv[sb]])
            tt = y_trdma(bass.AP(U[:, :].tensor, 4096 * sb + 2048,
                                 [[12288, 128], [128, 16], [1, 128]]),
                         g1[:, :], waits=[td])
            cv = C.emit("s", lambda e, g2=g2, sb=sb: e.copy(
                Fs[:, ts(sb, 2048)], g2[:, :]),
                waits=[tt] + ([out_done[j - 2]] if j >= 2 else []))
            prev_cv[sb] = cv
            st = y_dma(dout[ts(j), :], Fs[:, ts(sb, 2048)], waits=[cv], sem="do")
            out_done.append(st)

        final_o = (C.cnt["o0"], C.cnt["o1"])

        def mk(ename):
            def body(eng):
                for fn, waits, inc, n in C.ops[ename]:
                    for (snm, v) in waits:
                        eng.wait_ge(sems[snm], v)
                    ins = fn(eng)
                    if n:
                        ins.then_inc(sems[inc], n)
                if ename == "y":
                    eng.wait_ge(sems["o0"], final_o[0])
                    eng.wait_ge(sems["o1"], final_o[1])
            return body

        block.tensor(mk("t"))
        block.vector(mk("v"))
        block.scalar(mk("s"))
        block.gpsimd(mk("g"))
        block.sync(mk("y"))

    return nc


def _rope_tables():
    j = np.arange(0, DQK, 2, dtype=np.float32)
    inv = (np.float32(1.0) / np.power(np.float32(10000.0),
           (j / np.float32(DQK)).astype(np.float32))).astype(np.float32)
    ang = (np.arange(S, dtype=np.float32)[:, None] * inv[None, :]).astype(np.float32)
    c = np.cos(ang).astype(np.float32)
    s = np.sin(ang).astype(np.float32)
    rows = np.arange(128) % 32
    return np.ascontiguousarray(c[:, rows].T), np.ascontiguousarray(s[:, rows].T)


def _vec_tiles(v):
    return np.ascontiguousarray(np.asarray(v, dtype=np.float32).reshape(NT, 128).T)


def make_inputs(L0_attn_x, L0_attn_Wa, L0_attn_Wx, L0_attn_ba, L0_attn_bx,
                L0_attn_Lam, L0_ffn_w_gate_up, L0_ffn_w_down, L1_attn_w_in,
                L1_attn_w_out, L1_attn_rms_w, L1_ffn_weight, L1_ffn_bias):
    cosT, sinT = _rope_tables()
    P = np.zeros((128, 128), dtype=np.float32)
    for h in range(2):
        for jj in range(32):
            P[h * 64 + jj + 32, h * 64 + jj] = -1.0
            P[h * 64 + jj, h * 64 + jj + 32] = 1.0
    mask = np.where(np.arange(128)[:, None] > np.arange(128)[None, :],
                    np.float32(-1e5), np.float32(0.0)).astype(np.float32)
    shared = dict(
        waT=np.ascontiguousarray(np.asarray(L0_attn_Wa, dtype=np.float32).T),
        wxT=np.ascontiguousarray(np.asarray(L0_attn_Wx, dtype=np.float32).T),
        ba2=_vec_tiles(L0_attn_ba), bx2=_vec_tiles(L0_attn_bx),
        lam=_vec_tiles(L0_attn_Lam),
        w1=np.ascontiguousarray(L0_ffn_w_gate_up, dtype=np.float32),
        w2=np.ascontiguousarray(L0_ffn_w_down, dtype=np.float32),
        winT=np.ascontiguousarray(np.asarray(L1_attn_w_in, dtype=np.float32).T),
        woutT=np.ascontiguousarray(np.asarray(L1_attn_w_out, dtype=np.float32).T),
        rms2=_vec_tiles(L1_attn_rms_w),
        l1w=np.ascontiguousarray(L1_ffn_weight, dtype=np.float32),
        l1b2=_vec_tiles(L1_ffn_bias),
        identb=np.eye(128, dtype=BF), identf=np.eye(128, dtype=np.float32),
        prope=P.astype(BF), mask=mask, cosT=cosT, sinT=sinT,
        ones1=np.ones((1, 128), dtype=np.float32),
        onesd=np.full((128, 1), 1.0 / D, dtype=BF),
        onesk=np.ones((128, 1), dtype=np.float32),
        cvec=np.tile(np.array([LOG_HALF, RB, LN_EPS, RMS_EPS, -0.5, -20.0,
                               0.125, 0.0], dtype=np.float32), (128, 1)),
    )
    xs = np.asarray(L0_attn_x, dtype=np.float32)
    return [dict(shared, x=np.ascontiguousarray(xs[c])) for c in range(NCORE)]


def kernel(**inputs):
    nc = build_program()
    in_maps = make_inputs(**inputs)
    res = run_bass_kernel_spmd(nc, in_maps, core_ids=list(range(NCORE)))
    return np.stack([np.asarray(r["out"], dtype=np.float32) for r in res.results])


# version 12
# speedup vs baseline: 1.0400x; 1.0022x over previous
import os
import numpy as np
import ml_dtypes
import concourse.bass as bass
import concourse.mybir as mybir
from concourse.bass_utils import run_bass_kernel_spmd

F32 = mybir.dt.float32
BF16 = mybir.dt.bfloat16
I32 = mybir.dt.int32
AF = mybir.ActivationFunctionType
ALU = mybir.AluOpType
BF = ml_dtypes.bfloat16
AX = mybir.AxisListType.X

D = 2048
S = 2048
NT = D // 128
IFF = 5632
NI = IFF // 128
CW = 3072
NG = 8
DQK = 64
STAGE = int(os.environ.get("KSTAGE", "5"))
KSUB = int(os.environ.get("KSUB", "9"))
KSUB3 = int(os.environ.get("KSUB3", "9"))
KPROBE = int(os.environ.get("KPROBE", "0"))
NCORE = int(os.environ.get("KCORES", "8"))

LN_EPS = 1e-5
RMS_EPS = 1e-6
QMAX = 127.0
QEPS = 1e-8
L1_EPS = 1e-10
RB = 256.0
LOG_HALF = float(np.log(0.5))


NRR = 6
VTS = 96


class Ctx:
    ENG = ("t", "v", "s", "g", "y")

    def __init__(self):
        self.ops = {e: [] for e in self.ENG}
        self.semnames = (["t", "v", "s", "g"]
                         + [f"y{i}" for i in range(NRR)]
                         + [f"g{i}" for i in range(NRR)]
                         + ["o0", "o1"])
        self.cnt = {n: 0 for n in self.semnames}
        self.hw = {}
        self.rr = {"y": 0, "g": 0, "o": 0}
        self.seen = {e: {} for e in self.ENG}

    def emit(self, eng, fn, waits=(), inc=None, n=1):
        real = []
        flat = []
        for w in waits:
            if w is None:
                continue
            if isinstance(w, tuple) and w and isinstance(w[0], tuple):
                flat.extend(w)
            else:
                flat.append(w)
        seen = {}
        for w in flat:
            s, v = w
            if v <= 0:
                continue
            if seen.get(s, 0) < v:
                seen[s] = v
        eseen = self.seen[eng]
        for s, v in sorted(seen.items()):
            if v <= eseen.get(s, 0):
                continue
            eseen[s] = v
            real.append((s, v))
        if inc is None:
            inc, n = eng, 1
        self.cnt[inc] += n
        tok = (inc, self.cnt[inc])
        self.ops[eng].append((fn, tuple(real), inc, n))
        return tok

    def dma(self, eng, fn, waits=(), queue=None, nsem=None):
        q = queue or eng
        nsem = nsem if nsem is not None else (NRR if q in ("y", "g") else 2)
        sem = f"{q}{self.rr[q] % nsem}"
        self.rr[q] += 1
        self_wait = (sem, self.cnt[sem])
        return self.emit(eng, fn, waits=list(waits) + [self_wait],
                         inc=sem, n=16)


def ts(i, n=128):
    return slice(i * n, (i + 1) * n)


def build_program():
    nc = bass.Bass("TRN2", target_bir_lowering=False)

    dx = nc.dram_tensor("x", [S, D], F32, kind="ExternalInput").ap()
    dwaT = nc.dram_tensor("waT", [D, D], F32, kind="ExternalInput").ap()
    dwxT = nc.dram_tensor("wxT", [D, D], F32, kind="ExternalInput").ap()
    dba = nc.dram_tensor("ba2", [128, NT], F32, kind="ExternalInput").ap()
    dbx = nc.dram_tensor("bx2", [128, NT], F32, kind="ExternalInput").ap()
    dlam = nc.dram_tensor("lam", [128, NT], F32, kind="ExternalInput").ap()
    dw1 = nc.dram_tensor("w1", [2 * IFF, D], F32, kind="ExternalInput").ap()
    dw2 = nc.dram_tensor("w2", [D, IFF], F32, kind="ExternalInput").ap()
    dwinT = nc.dram_tensor("winT", [D, CW], F32, kind="ExternalInput").ap()
    dwoutT = nc.dram_tensor("woutT", [D, D], F32, kind="ExternalInput").ap()
    drms = nc.dram_tensor("rms2", [128, NT], F32, kind="ExternalInput").ap()
    dl1w = nc.dram_tensor("l1w", [D, D], F32, kind="ExternalInput").ap()
    dl1b = nc.dram_tensor("l1b2", [128, NT], F32, kind="ExternalInput").ap()
    dident = nc.dram_tensor("identb", [128, 128], BF16, kind="ExternalInput").ap()
    didf = nc.dram_tensor("identf", [128, 128], F32, kind="ExternalInput").ap()
    dprope = nc.dram_tensor("prope", [128, 128], BF16, kind="ExternalInput").ap()
    dmask = nc.dram_tensor("mask", [128, 128], F32, kind="ExternalInput").ap()
    dcos = nc.dram_tensor("cosT", [128, S], F32, kind="ExternalInput").ap()
    dsin = nc.dram_tensor("sinT", [128, S], F32, kind="ExternalInput").ap()
    dones1 = nc.dram_tensor("ones1", [1, 128], F32, kind="ExternalInput").ap()
    donesd = nc.dram_tensor("onesd", [128, 1], BF16, kind="ExternalInput").ap()
    donesk = nc.dram_tensor("onesk", [128, 1], F32, kind="ExternalInput").ap()
    dcvec = nc.dram_tensor("cvec", [128, 8], F32, kind="ExternalInput").ap()

    dout = nc.dram_tensor("out", [S, D], F32, kind="ExternalOutput").ap()
    dh = nc.dram_tensor("h_scr", [IFF, S], BF16).ap()
    dw2q = nc.dram_tensor("w2q_scr", [D, IFF], BF16).ap()

    from contextlib import ExitStack
    es = ExitStack()
    with es:
        XT = es.enter_context(nc.sbuf_tensor([128, NT * S], BF16))
        ZT = es.enter_context(nc.sbuf_tensor([128, NT * S], BF16))
        U = es.enter_context(nc.sbuf_tensor([128, 12288], BF16))
        Fs = es.enter_context(nc.sbuf_tensor([128, 3 * 2048], F32))
        IQ = es.enter_context(nc.sbuf_tensor([128, 2048], I32))
        SM = es.enter_context(nc.sbuf_tensor([128, 64], F32))
        SRA = es.enter_context(nc.sbuf_tensor([16, 128], F32))
        SRB = es.enter_context(nc.sbuf_tensor([16, 128], F32))
        RWS = es.enter_context(nc.sbuf_tensor([1, 2048], F32))
        RWS2 = es.enter_context(nc.sbuf_tensor([1, 2048], F32))
        CID = es.enter_context(nc.sbuf_tensor([128, 128], BF16))
        CIDF = es.enter_context(nc.sbuf_tensor([128, 128], F32))
        CP = es.enter_context(nc.sbuf_tensor([128, 128], BF16))
        CM = es.enter_context(nc.sbuf_tensor([128, 128], F32))
        C1 = es.enter_context(nc.sbuf_tensor([1, 128], F32))
        CD = es.enter_context(nc.sbuf_tensor([128, 1], BF16))
        CK = es.enter_context(nc.sbuf_tensor([128, 1], F32))
        CV = es.enter_context(nc.sbuf_tensor([128, 8], F32))
        VBA2 = es.enter_context(nc.sbuf_tensor([128, NT], F32))
        VBX2 = es.enter_context(nc.sbuf_tensor([128, NT], F32))
        VLAM = es.enter_context(nc.sbuf_tensor([128, NT], F32))
        VC2 = es.enter_context(nc.sbuf_tensor([128, NT], F32))
        VRMS = es.enter_context(nc.sbuf_tensor([128, NT], F32))
        VL1B = es.enter_context(nc.sbuf_tensor([128, NT], F32))
        VS1 = es.enter_context(nc.sbuf_tensor([128, 2 * NI], F32))
        VSC = es.enter_context(nc.sbuf_tensor([128, NI], F32))
        VS2 = es.enter_context(nc.sbuf_tensor([128, NT], F32))
        VSL = es.enter_context(nc.sbuf_tensor([128, 2], F32))

        PA = es.enter_context(nc.psum_tensor([128, 2048], F32))
        PB = es.enter_context(nc.psum_tensor([128, 2048], F32))

        sems = {}
        for nme in Ctx().semnames:
            sems[nme] = es.enter_context(nc.semaphore("sem_" + nme))
        block = es.enter_context(nc.Block())

        C = Ctx()

        def y_dma(dst, src, waits=(), sem=None):
            return C.dma("y", lambda e, d=dst, s=src: e.dma_start(d, s), waits,
                         queue="o" if sem == "do" else "y")

        def g_dma(dst, src, waits=()):
            return C.dma("g", lambda e, d=dst, s=src: e.dma_start(d, s), waits)

        def y_trdma(dst, src, waits=()):
            return C.dma("y", lambda e, d=dst, s=src: e.dma_start_transpose(d, s),
                         waits)

        _ctoks = []
        for dst, src in ((CID[:, :], dident), (CIDF[:, :], didf), (CP[:, :], dprope),
                         (CM[:, :], dmask),
                         (C1[:, :], dones1), (CD[:, :], donesd),
                         (CK[:, :], donesk), (CV[:, :], dcvec), (VBA2[:, :], dba), (VBX2[:, :], dbx),
                         (VLAM[:, :], dlam), (VRMS[:, :], drms), (VL1B[:, :], dl1b)):
            _ctoks.append(y_dma(dst, src))
        tCl = tuple(_ctoks)

        t1 = C.emit("s", lambda e: e.activation(VC2[:, :], VLAM[:, :], AF.Exp,
                    bias=0.0, scale=-1.0), waits=[tCl])
        t2 = C.emit("v", lambda e: e.tensor_scalar_add(VC2[:, :], VC2[:, :], 1.0), waits=[t1])
        t3 = C.emit("s", lambda e: e.activation(VC2[:, :], VC2[:, :], AF.Ln,
                    bias=0.0, scale=1.0), waits=[t2])
        tok_c2 = C.emit("s", lambda e: e.mul(VC2[:, :], VC2[:, :], -4.0), waits=[t3])
        t5 = C.emit("s", lambda e: e.mul(VBA2[:, :], VBA2[:, :], 0.5), waits=[tCl])
        tok_b2 = C.emit("s", lambda e: e.mul(VBX2[:, :], VBX2[:, :], 0.5), waits=[t5])

        stage_tok = [None, None]
        for j in range(16):
            sb = j % 2
            buf = U[:, 4096 * sb:4096 * sb + 2048]
            buf2 = U[:, 4096 * sb + 2048:4096 * sb + 4096]
            tl = g_dma(buf, dx[ts(j), :], waits=[stage_tok[sb]])
            tt = y_trdma(bass.AP(U[:, :].tensor, 4096 * sb + 2048,
                                 [[12288, 128], [128, 16], [1, 128]]),
                         buf[:, :], waits=[tl])
            out_ap = bass.AP(XT[:, :].tensor, j * 128,
                             [[NT * S, 128], [S, NT], [1, 128]])
            tt2 = y_dma(out_ap, buf2[:, :], waits=[tt])
            stage_tok[sb] = tt2
        tok_x = stage_tok[1]

        def ln_feature_major(src_big, dst_big, eps_col, tok_ready, u_guard=None):
            prev_mm = u_guard
            for g in range(NT):
                zg = src_big[:, ts(g, S)]
                sq = U[:, ts(g % 2, 2048)]
                tsq = C.emit("s", lambda e, a=sq, b=zg: e.activation(
                    a, b, AF.Square, bias=0.0, scale=1.0),
                    waits=[tok_ready, prev_mm])
                for c in range(4):
                    C.emit("t", lambda e, g=g, c=c, zg=zg: e.matmul(
                        PA[0:1, ts(c, 512)], CD[:, :], zg[:, ts(c, 512)],
                        start=(g == 0), stop=(g == NT - 1)),
                        waits=[tok_ready, tCl])
                    prev_mm = C.emit("t", lambda e, g=g, c=c, sq=sq: e.matmul(
                        PB[0:1, ts(c, 512)], CD[:, :], sq[:, ts(c, 512)],
                        start=(g == 0), stop=(g == NT - 1)), waits=[tsq])
            r1 = C.emit("s", lambda e: e.activation(RWS[0:1, :], PA[0:1, 0:2048],
                        AF.Square, bias=0.0, scale=1.0), waits=[prev_mm])
            r2 = C.emit("v", lambda e: e.scalar_tensor_tensor(RWS[0:1, :],
                        PB[0:1, 0:2048], 1.0, RWS[0:1, :], ALU.mult, ALU.subtract),
                        waits=[r1])
            r3 = C.emit("s", lambda e: e.activation(RWS[0:1, :], RWS[0:1, :], AF.Ln,
                        bias=CV[0:1, eps_col:eps_col + 1], scale=1.0), waits=[r2])
            r4 = C.emit("s", lambda e: e.activation(RWS[0:1, :], RWS[0:1, :], AF.Exp,
                        bias=0.0, scale=CV[0:1, 4:5]), waits=[r3])
            b2 = C.emit("s", lambda e: e.copy(RWS2[0:1, :], PA[0:1, 0:2048]), waits=[r4])
            last = None
            for c in range(4):
                C.emit("t", lambda e, c=c: e.matmul(PA[:, ts(c, 512)], C1[:, :],
                       RWS[0:1, ts(c, 512)], start=True, stop=True), waits=[b2])
                last = C.emit("t", lambda e, c=c: e.matmul(PB[:, ts(c, 512)], C1[:, :],
                              RWS2[0:1, ts(c, 512)], start=True, stop=True), waits=[b2])
            v1 = C.emit("s", lambda e: e.copy(Fs[:, 0:2048], PA[:, 0:2048]), waits=[last])
            v2 = C.emit("s", lambda e: e.copy(Fs[:, 2048:4096], PB[:, 0:2048]), waits=[v1])
            out_tok = None
            for g in range(NT):
                zg = src_big[:, ts(g, S)]
                dg = dst_big[:, ts(g, S)]
                a1 = C.emit("v", lambda e, zg=zg: e.tensor_sub(
                    Fs[:, 4096:6144], zg, Fs[:, 2048:4096]), waits=[v2, out_tok])
                out_tok = C.emit("v", lambda e, dg=dg: e.tensor_mul(
                    dg, Fs[:, 4096:6144], Fs[:, 0:2048]), waits=[a1])
            return out_tok

        def max_stat_bcast(tok_acc):
            red = None
            for c in range(16):
                b = (c % 4) * 512
                tt = C.emit("t", lambda e, b=b, c=c: e.transpose(
                    PA[:, b:b + 128],
                    Fs[:, 4096 + c * 128:4096 + (c + 1) * 128], CIDF[:, :]),
                    waits=[tok_acc, red, tCl])
                red = C.emit("v", lambda e, b=b, c=c: e.reduce_max(
                    SM[:, c:c + 1], PA[:, b:b + 128], AX), waits=[tt])
            q1 = C.emit("v", lambda e: e.tensor_scalar(SM[:, 16:32], SM[:, 0:16],
                        1.0 / QMAX, QEPS, ALU.mult, ALU.max), waits=[red])
            q2 = C.emit("v", lambda e: e.reciprocal(SM[:, 32:48], SM[:, 16:32]), waits=[q1])
            q3a = C.emit("t", lambda e: e.transpose(PB[0:16, 0:128], SM[:, 16:32],
                         CIDF[:, :]), waits=[q2])
            q3b = C.emit("t", lambda e: e.transpose(PB[0:16, 512:640], SM[:, 32:48],
                         CIDF[:, :]), waits=[q2])
            q4a = C.emit("s", lambda e: e.copy(SRA[:, :], PB[0:16, 0:128]), waits=[q3a])
            q4b = C.emit("s", lambda e: e.copy(SRB[:, :], PB[0:16, 512:640]), waits=[q3b])
            c1 = y_dma(RWS[0:1, :], SRB[:, :], waits=[q4b])
            c2 = y_dma(RWS2[0:1, :], SRA[:, :], waits=[q4a])
            last = None
            for c in range(4):
                C.emit("t", lambda e, c=c: e.matmul(PA[:, ts(c, 512)], C1[:, :],
                       RWS[0:1, ts(c, 512)], start=True, stop=True), waits=[c1, c2])
                last = C.emit("t", lambda e, c=c: e.matmul(PB[:, ts(c, 512)], C1[:, :],
                              RWS2[0:1, ts(c, 512)], start=True, stop=True), waits=[c1, c2])
            b1 = C.emit("s", lambda e: e.copy(Fs[:, 0:2048], PA[:, 0:2048]), waits=[last])
            b2 = C.emit("s", lambda e: e.copy(Fs[:, 2048:4096], PB[:, 0:2048]), waits=[b1])
            return b2

        tok_final = tok_x

        HW2 = IFF // 2
        W2A, W2B = 22528, 22528 + HW2
        w2s = dict(pw=None, ptr=None, piv2=None, wq=None)
        w2st = []
        RUN_W2 = STAGE >= 2 and KSUB >= 3

        def emit_w2quant(m):
            ldA = g_dma(ZT[:, W2A:W2A + HW2], dw2[ts(m), 0:HW2],
                        waits=[w2s["ptr"]])
            ldB = g_dma(ZT[:, W2B:W2B + HW2], dw2[ts(m), HW2:IFF],
                        waits=[w2s["ptr"]])
            rdA = C.emit("v", lambda e: e.reduce_max(SM[:, 48:49],
                         ZT[:, W2A:W2A + HW2],
                         AX, apply_absolute_value=True), waits=[ldA, w2s["piv2"]])
            rdB = C.emit("v", lambda e: e.reduce_max(SM[:, 49:50],
                         ZT[:, W2B:W2B + HW2],
                         AX, apply_absolute_value=True), waits=[ldB, rdA])
            mx = C.emit("v", lambda e: e.tensor_max(SM[:, 48:49],
                        SM[:, 48:49], SM[:, 49:50]), waits=[rdB])
            sc = C.emit("v", lambda e, m=m: e.tensor_scalar(
                VS2[:, m:m + 1], SM[:, 48:49], 1.0 / QMAX, QEPS,
                ALU.mult, ALU.max), waits=[mx])
            iv = C.emit("v", lambda e, m=m: e.reciprocal(SM[:, 50:51],
                        VS2[:, m:m + 1]), waits=[sc, w2s["wq"]])
            w2s["piv2"] = iv
            half_toks = []
            for cc in range(4):
                lo = W2A + cc * 1408
                rq = C.emit("s", lambda e, lo=lo: e.activation(
                    IQ[:, 0:1408], ZT[:, lo:lo + 1408], AF.Copy, bias=RB,
                    scale=SM[:, 50:51]),
                    waits=[iv] + ([half_toks[-1]] if half_toks else [w2s["wq"]]))
                wq = C.emit("v", lambda e, lo=lo: e.tensor_scalar_add(
                    ZT[:, lo:lo + 1408], IQ[:, 0:1408], -256.0), waits=[rq])
                half_toks.append(wq)
            w2s["wq"] = half_toks[-1]
            stA = y_dma(dw2q[ts(m), 0:HW2], ZT[:, W2A:W2A + HW2],
                        waits=[half_toks[1], w2s["ptr"]])
            ptr = y_dma(dw2q[ts(m), HW2:IFF], ZT[:, W2B:W2B + HW2],
                        waits=[half_toks[3], stA])
            w2s["ptr"] = ptr
            w2st.append(ptr)

        if RUN_W2:
            for m in range(4):
                emit_w2quant(m)

        if STAGE >= 1:
            prev_ra = None
            prev_rb = None
            prev_z = None
            prev_s3 = None
            prev_v3 = None
            prev_v5 = None
            wtok = [stage_tok[0], stage_tok[1]]
            for g in range(NT):
                sb = g % 2
                base = 4096 * sb
                wa_st = U[:, base:base + 2048]
                wx_st = U[:, base + 2048:base + 4096]
                in_a = bass.AP(dwaT.tensor, g * 128, [[D, 128], [128 * D, 16], [1, 128]])
                in_x = bass.AP(dwxT.tensor, g * 128, [[D, 128], [128 * D, 16], [1, 128]])
                oa = bass.AP(U[:, :].tensor, base, [[12288, 128], [128, 16], [1, 128]])
                ox = bass.AP(U[:, :].tensor, base + 2048, [[12288, 128], [128, 16], [1, 128]])
                ta = g_dma(oa, in_a, waits=[wtok[sb]])
                txw = g_dma(ox, in_x, waits=[wtok[sb]])
                mm = None
                for c in range(4):
                    for k in range(NT):
                        mm = C.emit("t", lambda e, c=c, k=k, wa_st=wa_st: e.matmul(
                            PA[:, ts(c, 512)], wa_st[:, ts(k)],
                            XT[:, k * S + 512 * c:k * S + 512 * c + 512],
                            start=(k == 0), stop=(k == NT - 1)),
                            waits=[ta, tok_x, prev_ra])
                mm2 = None
                for c in range(4):
                    for k in range(NT):
                        mm2 = C.emit("t", lambda e, c=c, k=k, wx_st=wx_st: e.matmul(
                            PB[:, ts(c, 512)], wx_st[:, ts(k)],
                            XT[:, k * S + 512 * c:k * S + 512 * c + 512],
                            start=(k == 0), stop=(k == NT - 1)),
                            waits=[txw, prev_rb])
                wtok[sb] = mm2
                rt = U[:, 8192:10240]
                it = U[:, 10240:12288]
                prev_ra = C.emit("s", lambda e, g=g, rt=rt: e.activation(
                    rt, PA[:, 0:2048], AF.Tanh, bias=VBA2[:, g:g + 1], scale=0.5),
                    waits=[mm, tok_b2, prev_s3])
                prev_rb = C.emit("s", lambda e, g=g, it=it: e.activation(
                    it, PB[:, 0:2048], AF.Tanh, bias=VBX2[:, g:g + 1], scale=0.5),
                    waits=[mm2, prev_v3])
                s3 = C.emit("s", lambda e, g=g, rt=rt: e.activation(
                    Fs[:, 0:2048], rt, AF.Exp, bias=VC2[:, g:g + 1],
                    scale=VC2[:, g:g + 1]), waits=[prev_ra, tok_c2, prev_v5])
                prev_s3 = s3
                v1 = C.emit("v", lambda e: e.tensor_mul(Fs[:, 2048:4096],
                            Fs[:, 0:2048], Fs[:, 0:2048]), waits=[s3])
                v2 = C.emit("v", lambda e: e.tensor_scalar(Fs[:, 2048:4096],
                            Fs[:, 2048:4096], -1.0, 1.0, ALU.mult, ALU.add), waits=[v1])
                s4 = C.emit("s", lambda e: e.activation(Fs[:, 2048:4096],
                            Fs[:, 2048:4096], AF.Ln, bias=0.0, scale=1.0), waits=[v2])
                s5 = C.emit("s", lambda e: e.activation(Fs[:, 2048:4096],
                            Fs[:, 2048:4096], AF.Exp, bias=CV[:, 0:1], scale=0.5), waits=[s4])
                v3 = C.emit("v", lambda e, g=g, it=it: e.scalar_tensor_tensor(
                    Fs[:, 4096:6144], it, 1.0, XT[:, ts(g, S)], ALU.add, ALU.mult),
                    waits=[prev_rb, tok_x])
                prev_v3 = v3
                v4 = C.emit("v", lambda e: e.tensor_mul(Fs[:, 2048:4096],
                            Fs[:, 2048:4096], Fs[:, 4096:6144]), waits=[v3, s5])
                v5 = C.emit("v", lambda e: e.tensor_tensor_scan(
                    Fs[:, 4096:6144], Fs[:, 0:2048], Fs[:, 2048:4096], 0.0,
                    ALU.mult, ALU.add), waits=[v4])
                prev_v5 = v5
                if RUN_W2 and g < 12:
                    emit_w2quant(g + 4)
                zguard = w2st[-1] if (RUN_W2 and g in (11, 12, 13)) else None
                prev_z = C.emit("v", lambda e, g=g: e.tensor_add(
                    ZT[:, ts(g, S)], Fs[:, 4096:6144], XT[:, ts(g, S)]),
                    waits=[v5, zguard])
            tok_x1 = ln_feature_major(ZT, XT, 2, prev_z, u_guard=wtok[1])
            tok_final = tok_x1

        if STAGE >= 2:
            acc = None
            sprev = None
            for g in range(NT):
                sprev = C.emit("s", lambda e, g=g: e.activation(U[:, 0:2048],
                               XT[:, ts(g, S)], AF.Abs, bias=0.0, scale=1.0),
                               waits=[tok_x1, acc])
                if g == 0:
                    acc = C.emit("v", lambda e: e.tensor_copy(Fs[:, 4096:6144],
                                 U[:, 0:2048]), waits=[sprev])
                else:
                    acc = C.emit("v", lambda e: e.tensor_max(Fs[:, 4096:6144],
                                 Fs[:, 4096:6144], U[:, 0:2048]), waits=[sprev])
            tok_bc = max_stat_bcast(acc)
            if KSUB == 0:
                tok_final = tok_bc
            tq = None
            prev_iqr = w2s["wq"]
            for g in range(NT):
                w1_ = C.emit("v", lambda e, g=g: e.tensor_mul(
                    Fs[:, 4096:6144], XT[:, ts(g, S)], Fs[:, 0:2048]),
                    waits=[tok_bc, tq])
                w2_ = C.emit("s", lambda e: e.activation(IQ[:, :], Fs[:, 4096:6144],
                             AF.Copy, bias=RB, scale=1.0), waits=[w1_, prev_iqr])
                w3_ = C.emit("v", lambda e: e.tensor_scalar_add(Fs[:, 4096:6144],
                             IQ[:, :], -256.0), waits=[w2_])
                prev_iqr = w3_
                tq = C.emit("v", lambda e, g=g: e.tensor_mul(
                    ZT[:, ts(g, S)], Fs[:, 4096:6144], Fs[:, 2048:4096]), waits=[w3_])
            tok_xq = tq
            if KSUB == 1:
                tok_final = tok_xq

            RUN_GU = KSUB >= 2
            hacc = None
            prev_e1 = None
            prev_e2 = None
            prev_e3 = None
            prev_rq = {"g": None, "u": None}
            prev_wq = w2s["wq"]
            h_done = None
            st_free = None
            prev_iv = w2s["piv2"]
            prev_trt = {"g": None, "u": None}
            e1_hist = [None, None]
            gs = dict(prev_rq=prev_rq, prev_wq=prev_wq, prev_iv=prev_iv,
                      prev_trt=prev_trt, st_free=None)

            def emit_w1quant(i):
                wgl = g_dma(U[:, 0:2048], dw1[ts(i), :],
                            waits=[gs["prev_rq"]["g"], gs["prev_trt"]["g"],
                                   tok_xq if i == 0 else None])
                wul = g_dma(U[:, 2048:4096], dw1[ts(NI + i), :],
                            waits=[gs["prev_rq"]["u"], gs["prev_trt"]["u"]])
                trt = {}
                for nm, off, ld in (("g", 0, wgl), ("u", 2048, wul)):
                    wn = U[:, off:off + 2048]
                    col = i if nm == "g" else NI + i
                    rd = C.emit("v", lambda e, wn=wn: e.reduce_max(
                        SM[:, 48:49], wn, AX, apply_absolute_value=True),
                        waits=[ld, gs["prev_iv"]])
                    sc = C.emit("v", lambda e, col=col: e.tensor_scalar(
                        VS1[:, col:col + 1], SM[:, 48:49], 1.0 / QMAX, QEPS,
                        ALU.mult, ALU.max), waits=[rd])
                    iv = C.emit("v", lambda e, col=col: e.reciprocal(
                        SM[:, 50:51], VS1[:, col:col + 1]),
                        waits=[sc, gs["prev_rq"][nm]])
                    gs["prev_iv"] = iv
                    rq = C.emit("s", lambda e, wn=wn: e.activation(IQ[:, :], wn,
                                AF.Copy, bias=RB, scale=SM[:, 50:51]),
                                waits=[iv, gs["prev_wq"]])
                    gs["prev_rq"][nm] = rq
                    wq = C.emit("v", lambda e, wn=wn: e.tensor_scalar_add(
                        wn, IQ[:, :], -256.0), waits=[rq])
                    gs["prev_wq"] = wq
                    so_ = 4096 + (0 if nm == "g" else 2048)
                    ot = bass.AP(U[:, :].tensor, so_, [[12288, 128], [128, 16], [1, 128]])
                    trt[nm] = y_trdma(ot, wn, waits=[wq, gs["st_free"]])
                    gs["prev_trt"][nm] = trt[nm]
                cs = C.emit("v", lambda e, i=i: e.tensor_mul(VSC[:, i:i + 1],
                            VS1[:, i:i + 1], VS1[:, NI + i:NI + i + 1]),
                            waits=[gs["prev_rq"]["u"], gs["prev_rq"]["g"]])
                cs2 = C.emit("v", lambda e, i=i: e.tensor_scalar_mul(
                    VSC[:, i:i + 1], VSC[:, i:i + 1], 0.5), waits=[cs])
                tcol = 54 + (i % 2)
                tsc = C.emit("v", lambda e, i=i, tcol=tcol: e.tensor_scalar_mul(
                    SM[:, tcol:tcol + 1], VS1[:, i:i + 1], 0.5),
                    waits=[cs2, e1_hist[i % 2]])
                trt["tsc"] = tsc
                trt["cs2"] = cs2
                return trt

            trt_cur = emit_w1quant(0) if RUN_GU else None
            for i in range(NI if RUN_GU else 0):
                trt = trt_cur
                mmg = None
                for c in range(4):
                    for k in range(NT):
                        mmg = C.emit("t", lambda e, c=c, k=k: e.matmul(
                            PA[:, ts(c, 512)],
                            U[:, 4096 + k * 128:4096 + k * 128 + 128],
                            ZT[:, k * S + 512 * c:k * S + 512 * c + 512],
                            start=(k == 0), stop=(k == NT - 1)),
                            waits=[trt["g"], tok_xq, prev_e2])
                mmu = None
                for c in range(4):
                    for k in range(NT):
                        mmu = C.emit("t", lambda e, c=c, k=k: e.matmul(
                            PB[:, ts(c, 512)],
                            U[:, 6144 + k * 128:6144 + k * 128 + 128],
                            ZT[:, k * S + 512 * c:k * S + 512 * c + 512],
                            start=(k == 0), stop=(k == NT - 1)),
                            waits=[trt["u"], prev_e3])
                gs["st_free"] = mmu
                if i + 1 < NI:
                    trt_cur = emit_w1quant(i + 1)
                tcol = 54 + (i % 2)
                e1 = C.emit("s", lambda e, tcol=tcol: e.activation(
                            U[:, 8192:10240], PA[:, 0:2048], AF.Tanh, bias=0.0,
                            scale=SM[:, tcol:tcol + 1]),
                            waits=[mmg, trt["tsc"], h_done])
                e1_hist[i % 2] = e1
                prev_e1 = e1
                e2 = C.emit("v", lambda e: e.scalar_tensor_tensor(Fs[:, 0:2048],
                            U[:, 8192:10240], 1.0, PA[:, 0:2048], ALU.add, ALU.mult),
                            waits=[e1])
                prev_e2 = e2
                e3 = C.emit("v", lambda e: e.tensor_mul(Fs[:, 0:2048], Fs[:, 0:2048],
                            PB[:, 0:2048]), waits=[e2, mmu])
                prev_e3 = e3
                e4 = C.emit("v", lambda e, i=i: e.tensor_scalar_mul(
                    U[:, 10240:12288], Fs[:, 0:2048], VSC[:, i:i + 1]),
                    waits=[e3, trt["cs2"], h_done])
                sab = C.emit("s", lambda e: e.activation(Fs[:, 2048:4096],
                             U[:, 10240:12288], AF.Abs, bias=0.0, scale=1.0), waits=[e4])
                if i == 0:
                    hacc = C.emit("v", lambda e: e.tensor_copy(Fs[:, 4096:6144],
                                  Fs[:, 2048:4096]), waits=[sab])
                else:
                    hacc = C.emit("v", lambda e: e.tensor_max(Fs[:, 4096:6144],
                                  Fs[:, 4096:6144], Fs[:, 2048:4096]), waits=[sab])
                h_done = y_dma(dh[ts(i), :], U[:, 10240:12288], waits=[e4])
            prev_wq = gs["prev_wq"]
            tok_hbc = max_stat_bcast(hacc) if RUN_GU else None
            if KSUB == 2:
                tok_final = tok_hbc

            qts0 = []
            prev_hq = tok_xq
            q2h = [None, None]
            q3h = [prev_wq, prev_wq]
            HFS = (4096, 5632)
            HIQ = (1536, 1024)
            if KSUB >= 4:
                for i in range(NI):
                    sl = i % 2
                    fsl, iql = HFS[sl], HIQ[sl]
                    hz = ZT[:, i * 512:(i + 1) * 512]
                    hl = g_dma(hz, dh[ts(i), 0:512],
                               waits=[tok_hbc, h_done])
                    q1 = C.emit("v", lambda e, hz=hz, fsl=fsl: e.tensor_mul(
                        Fs[:, fsl:fsl + 512], hz, Fs[:, 0:512]),
                        waits=[hl, q2h[sl]])
                    q2 = C.emit("s", lambda e, fsl=fsl, iql=iql: e.activation(
                        IQ[:, iql:iql + 512],
                        Fs[:, fsl:fsl + 512], AF.Copy, bias=RB, scale=1.0),
                        waits=[q1, q3h[sl]])
                    q2h[sl] = q2
                    q3 = C.emit("v", lambda e, hz=hz, iql=iql: e.tensor_scalar_add(
                        hz, IQ[:, iql:iql + 512], -256.0), waits=[q2])
                    q3h[sl] = q3
                    qts0.append(q3)
                prev_hq = qts0[-1]

            l1pld = w2st[-1] if w2st else None
            l1acc = None
            l1stats = None
            prev_ev = None
            prev_wst = [(gs["prev_trt"]["g"], gs["prev_trt"]["u"], tok_hbc),
                        (h_done, tok_hbc)]
            last_mm_prev_chunk = None
            bank_rd = [None] * 4
            prev_d2 = None
            for c in range(4 if KSUB >= 4 else 0):
                if c == 0:
                    qts = qts0
                else:
                    qts = []
                    for i in range(NI):
                        sl = i % 2
                        fsl, iql = HFS[sl], HIQ[sl]
                        hz = ZT[:, i * 512:(i + 1) * 512]
                        hl = g_dma(hz, dh[ts(i), c * 512:c * 512 + 512],
                                   waits=[h_done, last_mm_prev_chunk])
                        q1 = C.emit("v", lambda e, hz=hz, c=c, fsl=fsl:
                            e.tensor_mul(
                            Fs[:, fsl:fsl + 512], hz,
                            Fs[:, c * 512:c * 512 + 512]),
                            waits=[hl, q2h[sl]])
                        q2 = C.emit("s", lambda e, fsl=fsl, iql=iql: e.activation(
                            IQ[:, iql:iql + 512],
                            Fs[:, fsl:fsl + 512], AF.Copy, bias=RB, scale=1.0),
                            waits=[q1, q3h[sl]])
                        q2h[sl] = q2
                        q3 = C.emit("v", lambda e, hz=hz, iql=iql:
                            e.tensor_scalar_add(
                            hz, IQ[:, iql:iql + 512], -256.0), waits=[q2])
                        q3h[sl] = q3
                        qts.append(q3)
                    prev_hq = qts[-1]
                for m in range(NT):
                    sb = m % 2
                    wl = y_trdma(bass.AP(U[:, :].tensor, sb * 6144,
                                         [[12288, 128], [128, NI], [1, 128]]),
                                 dw2q[ts(m), :],
                                 waits=[prev_wst[sb], w2st[m] if w2st else None])
                    if sb == 0:
                        last_wl0 = wl
                    else:
                        last_wl = wl
                    mm = None
                    for k in range(NI):
                        mm = C.emit("t", lambda e, m=m, k=k, sb=sb: e.matmul(
                            PA[:, ts(m % 4, 512)],
                            U[:, sb * 6144 + k * 128:sb * 6144 + k * 128 + 128],
                            ZT[:, k * 512:(k + 1) * 512],
                            start=(k == 0), stop=(k == NI - 1)),
                            waits=[wl, qts[NI - 1], bank_rd[m % 4]])
                    prev_wst[sb] = mm
                    last_mm_prev_chunk = mm
                    d1 = C.emit("s", lambda e, m=m: e.activation(Fs[:, 4608:5120],
                                PA[:, ts(m % 4, 512)], AF.Copy, bias=0.0,
                                scale=VS2[:, m:m + 1]), waits=[mm, prev_d2])
                    bank_rd[m % 4] = d1
                    d2 = C.emit("v", lambda e, c=c: e.tensor_mul(Fs[:, 4608:5120],
                                Fs[:, 4608:5120],
                                Fs[:, 2048 + c * 512:2048 + c * 512 + 512]),
                                waits=[d1])
                    prev_d2 = d2
                    d3 = C.emit("v", lambda e, m=m, c=c: e.tensor_add(
                        Fs[:, 5120:5632], Fs[:, 4608:5120],
                        XT[:, m * S + c * 512:m * S + c * 512 + 512]),
                        waits=[d2, prev_ev])
                    prev_ev = C.emit("s", lambda e, m=m, c=c: e.copy(
                        XT[:, m * S + c * 512:m * S + c * 512 + 512],
                        Fs[:, 5120:5632]), waits=[d3])
                if STAGE >= 4:
                    for m4 in range(c * 4, c * 4 + 4):
                        l1ld = g_dma(ZT[:, 22528:24576], dl1w[ts(m4), :],
                                     waits=[l1pld])
                        l1rd = C.emit("v", lambda e: e.reduce_sum(
                            SM[:, 52:53], ZT[:, 22528:24576], AX,
                            apply_absolute_value=True), waits=[l1ld, l1acc])
                        l1pld = l1rd
                        if m4 == 0:
                            l1acc = C.emit("v", lambda e: e.tensor_copy(
                                SM[:, 53:54], SM[:, 52:53]), waits=[l1rd])
                        else:
                            l1acc = C.emit("v", lambda e: e.tensor_add(
                                SM[:, 53:54], SM[:, 53:54], SM[:, 52:53]),
                                waits=[l1rd])
                    l1stats = l1acc
            if KSUB >= 4:
                tok_x2 = ln_feature_major(XT, XT, 2, prev_ev, u_guard=prev_wst[1])
                tok_final = tok_x2
                guard2 = (tok_x2, prev_wst[0], prev_wst[1], last_wl, last_wl0,
                          l1stats)


        if STAGE >= 3 and KSUB3 >= 1:
            prev_mm = guard2
            for g in range(NT):
                sq = U[:, ts(g % 2, 2048)]
                tsq = C.emit("s", lambda e, a=sq, g=g: e.activation(
                    a, XT[:, ts(g, S)], AF.Square, bias=0.0, scale=1.0),
                    waits=[tok_x2, prev_mm])
                for c in range(4):
                    prev_mm = C.emit("t", lambda e, g=g, c=c, sq=sq: e.matmul(
                        PA[0:1, ts(c, 512)], CD[:, :], sq[:, ts(c, 512)],
                        start=(g == 0), stop=(g == NT - 1)), waits=[tsq])
            r3 = C.emit("s", lambda e: e.activation(RWS[0:1, :], PA[0:1, 0:2048],
                        AF.Ln, bias=CV[0:1, 3:4], scale=1.0), waits=[prev_mm])
            r4 = C.emit("s", lambda e: e.activation(RWS[0:1, :], RWS[0:1, :], AF.Exp,
                        bias=0.0, scale=CV[0:1, 4:5]), waits=[r3])
            last = None
            for c in range(4):
                last = C.emit("t", lambda e, c=c: e.matmul(PA[:, ts(c, 512)], C1[:, :],
                              RWS[0:1, ts(c, 512)], start=True, stop=True), waits=[r4])
            tok_rrms = C.emit("s", lambda e: e.copy(Fs[:, 0:2048], PA[:, 0:2048]),
                              waits=[last])
            prev_stage = (prev_mm, tok_rrms) + guard2
            tok_cos = y_dma(Fs[:, 2048:4096], dcos, waits=[tok_x2])
            tok_sin = y_dma(Fs[:, 4096:6144], dsin, waits=[tok_x2])
            vrms_ap = bass.AP(VRMS[:, :].tensor, 0, [[NT, 128], [1, NT], [0, 128]])
            stage3 = bass.AP(U[:, :].tensor, 0, [[12288, 128], [128, 16], [1, 128]])

            prev_ostrip = None
            prev_vt = None
            prev_exp = [None] * 4
            prev_oev = None
            prev_rope = None
            oev_bank = [None] * 4
            bc_prev = None
            blk_free = [prev_stage] * NT
            for g in range(NG if KSUB3 >= 4 else 1):
                qtiles = []
                prev_ev_sub = None
                for sub in range(3):
                    sck = []
                    for k in range(NT):
                        dstb = U[:, k * 128:(k + 1) * 128]
                        if sub < 2:
                            col0 = g * 256 + sub * 128
                            lds = [g_dma(dstb,
                                   bass.AP(dwinT.tensor, col0 + k * 128 * CW,
                                           [[CW, 128], [1, 128]]),
                                   waits=[blk_free[k]])]
                        else:
                            lds = [g_dma(U[:, k * 128:k * 128 + 64],
                                   bass.AP(dwinT.tensor, 2048 + g * 64 + k * 128 * CW,
                                           [[CW, 128], [1, 64]]),
                                   waits=[blk_free[k]]),
                                   g_dma(U[:, k * 128 + 64:(k + 1) * 128],
                                   bass.AP(dwinT.tensor, 2560 + g * 64 + k * 128 * CW,
                                           [[CW, 128], [1, 64]]),
                                   waits=[blk_free[k]])]
                        vrb = bass.AP(VRMS[:, :].tensor, k, [[NT, 128], [0, 128]])
                        sck.append(C.emit("v", lambda e, dstb=dstb, vrb=vrb:
                            e.tensor_tensor(dstb, dstb, vrb, ALU.mult),
                            waits=lds + [tCl]))
                    mm = None
                    for k in range(NT):
                        for c in range(4):
                            mm = C.emit("t", lambda e, c=c, k=k: e.matmul(
                                PA[:, ts(c, 512)], U[:, ts(k)],
                                XT[:, k * S + 512 * c:k * S + 512 * c + 512],
                                start=(k == 0), stop=(k == NT - 1)),
                                waits=[sck[k], tok_x2, prev_rope, prev_ev_sub])
                        blk_free[k] = mm
                    prev_stage = mm
                    dst = U[:, 2048 + sub * 2048:4096 + sub * 2048]
                    ev = C.emit("v", lambda e, dst=dst: e.tensor_mul(
                        dst, PA[:, 0:2048], Fs[:, 0:2048]), waits=[mm, tok_rrms])
                    prev_ev_sub = ev
                    qtiles.append((dst, ev))
                rope_toks = [prev_ev_sub]
                for ri in range(3 if KSUB3 >= 2 else 0):
                    dst, ev = qtiles[ri]
                    nrow = 128 if ri < 2 else 64
                    pm = None
                    for c in range(4):
                        pm = C.emit("t", lambda e, c=c, dst=dst, nrow=nrow: e.matmul(
                            PA[0:nrow, ts(c, 512)], CP[0:nrow, 0:nrow],
                            dst[0:nrow, ts(c, 512)], start=True, stop=True),
                            waits=[ev, tCl, prev_rope, rope_toks[-1]])
                        a1 = C.emit("v", lambda e, c=c, dst=dst, nrow=nrow:
                                    e.tensor_mul(PB[0:nrow, ts(c, 512)],
                                    dst[0:nrow, ts(c, 512)],
                                    Fs[0:nrow, 2048 + c * 512:2048 + c * 512 + 512]),
                                    waits=[ev, tok_cos, prev_oev])
                        a2 = C.emit("v", lambda e, c=c, nrow=nrow: e.tensor_mul(
                            Fs[0:nrow, 0:512] if False else U[0:nrow, 8192 + c * 512:8192 + c * 512 + 512],
                            PA[0:nrow, ts(c, 512)],
                            Fs[0:nrow, 4096 + c * 512:4096 + c * 512 + 512]),
                            waits=[pm, tok_sin])
                        a3 = C.emit("v", lambda e, c=c, dst=dst, nrow=nrow:
                                    e.tensor_add(dst[0:nrow, ts(c, 512)],
                                    PB[0:nrow, ts(c, 512)],
                                    U[0:nrow, 8192 + c * 512:8192 + c * 512 + 512]),
                                    waits=[a1, a2])
                        rope_toks.append(a3)
                    prev_rope = rope_toks[-1]
                kvt, _ = qtiles[2]
                vt0 = 10240
                if KSUB3 >= 3:
                    ttr = y_trdma(bass.AP(U[:, :].tensor, 8192,
                                          [[12288, 128], [128, 16], [1, 128]]),
                                  kvt[:, :], waits=[qtiles[2][1], prev_rope, prev_vt])
                    vts = []
                    for c in range(16):
                        sv = C.emit("s", lambda e, c=c: e.copy(
                            U[:, vt0 + c * VTS:vt0 + c * VTS + 64],
                            U[:, 8192 + c * 128 + 64:8192 + c * 128 + 128]), waits=[ttr])
                        vts.append(sv)
                    on = C.emit("v", lambda e: e.memset(
                        bass.AP(U[:, :].tensor, vt0 + 64, [[12288, 128], [VTS, 16]]), 1.0),
                        waits=[vts[-1], prev_vt])
                    tok_vt = on
                    prev_vt = on
                    kdup = g_dma(kvt[64:128, :], kvt[0:64, :], waits=[ttr, prev_rope])
                for sp in range(4 if KSUB3 >= 3 else 0):
                    q0 = sp * 512
                    nk = sp * 4 + 4
                    mmq = {}
                    av_last = {}
                    for sk in range(nk):
                        for h in range(4):
                            qt = U[:, 2048 + (h // 2) * 2048:4096 + (h // 2) * 2048]
                            hb = 64 * (h % 2)
                            mv = qt[hb:hb + 64, q0:q0 + 512]
                            mmq[h] = C.emit("t", lambda e, h=h, sk=sk, mv=mv,
                                            kvt=kvt, hb=hb:
                                            e.matmul(PA[:, ts(h, 512)],
                                            kvt[hb:hb + 64, ts(sk)], mv,
                                            start=True, stop=True),
                                            waits=[prev_rope, tok_vt, kdup,
                                                   prev_exp[h]])
                        msk = None
                        if sk >= sp * 4:
                            off = sk * 128 - q0
                            for h in range(4):
                                msk = C.emit("v", lambda e, h=h, off=off: e.tensor_add(
                                    PA[:, h * 512 + off:h * 512 + off + 128],
                                    PA[:, h * 512 + off:h * 512 + off + 128],
                                    CM[:, :]), waits=[mmq[h], tCl])
                        for h in range(4):
                            prev_exp[h] = C.emit("s", lambda e, h=h: e.activation(
                                U[:, 8192 + h * 512:8192 + h * 512 + 512],
                                PA[:, ts(h, 512)], AF.Exp, bias=CV[:, 5:6],
                                scale=CV[:, 6:7]),
                                waits=[mmq[h], msk, oev_bank[h]])
                            if sk > sp * 4:
                                off = sk * 128 - q0
                                prev_exp[h] = C.emit("v", lambda e, h=h, off=off:
                                    e.memset(U[:, 8192 + h * 512:
                                               8192 + h * 512 + off], 0.0),
                                    waits=[prev_exp[h]])
                        for h in range(4):
                            av_last[h] = C.emit("t", lambda e, h=h, sk=sk, nk=nk:
                                e.matmul(
                                PB[0:65, ts(h, 512)],
                                U[:, vt0 + sk * VTS:vt0 + sk * VTS + 65],
                                U[:, 8192 + h * 512:8192 + h * 512 + 512],
                                start=(sk == 0), stop=(sk == nk - 1)),
                                waits=[prev_exp[h]] +
                                      ([oev_bank[h]] if sk == 0 else []))
                    otoks = []
                    rr = C.emit("v", lambda e: e.reciprocal(
                        RWS[0:1, 0:2048], PB[64:65, 0:2048]),
                        waits=[av_last[0], av_last[1], av_last[2], av_last[3],
                               bc_prev])
                    for h in range(4):
                        head = 4 * g + h
                        bc = C.emit("t", lambda e, h=h: e.matmul(
                                    PB[64:128, ts(h, 512)],
                                    C1[0:1, 0:64], RWS[0:1, ts(h, 512)],
                                    start=True, stop=True), waits=[rr])
                        bc_prev = bc
                        oslot = U[0:64, 8192 + h * 512:8192 + h * 512 + 512]
                        oe = C.emit("s", lambda e, h=h, oslot=oslot: e.copy(
                            oslot, PB[0:64, ts(h, 512)]), waits=[av_last[h]])
                        zslot = ZT[64 * (head % 2):64 * (head % 2) + 64,
                                   (head // 2) * S + q0:(head // 2) * S + q0 + 512]
                        ov = C.emit("v", lambda e, h=h, zslot=zslot, oslot=oslot:
                                    e.tensor_mul(zslot, oslot,
                                                 PB[64:128, ts(h, 512)]),
                                    waits=[oe, bc])
                        oev_bank[h] = (oe, ov)
                        otoks.extend([oe, ov])
                    prev_oev = tuple(otoks)
            prev_ev3 = None
            ev3_bank = [None, None]
            cp_last = [None, None]
            wst3 = [None, None]
            for m in range(NT if KSUB3 >= 5 else 0):
                sb3 = m % 2
                stage3m = bass.AP(U[:, :].tensor, sb3 * 2048,
                                  [[12288, 128], [128, 16], [1, 128]])
                wl = g_dma(stage3m, bass.AP(dwoutT.tensor, m * 128,
                           [[D, 128], [128 * D, 16], [1, 128]]),
                           waits=[prev_stage, prev_oev, wst3[sb3]])
                PM = PA if sb3 == 0 else PB
                mm = None
                for c in range(4):
                    for k in range(NT):
                        mm = C.emit("t", lambda e, c=c, k=k, PM=PM, sb3=sb3:
                            e.matmul(
                            PM[:, ts(c, 512)],
                            U[:, sb3 * 2048 + k * 128:sb3 * 2048 + k * 128 + 128],
                            ZT[:, k * S + 512 * c:k * S + 512 * c + 512],
                            start=(k == 0), stop=(k == NT - 1)),
                            waits=[wl, prev_oev, ev3_bank[sb3]])
                prev_stage = mm
                wst3[sb3] = mm
                last_v = None
                for c in range(4):
                    v1_ = C.emit("v", lambda e, m=m, c=c, PM=PM, sb3=sb3:
                        e.scalar_tensor_tensor(
                        Fs[:, sb3 * 2048 + c * 512:sb3 * 2048 + c * 512 + 512],
                        XT[:, m * S + c * 512:m * S + c * 512 + 512], 2.0,
                        PM[:, ts(c, 512)], ALU.mult, ALU.add),
                        waits=[mm, cp_last[sb3] if c == 0 else None])
                    prev_ev3 = C.emit("s", lambda e, m=m, c=c, sb3=sb3: e.copy(
                        XT[:, m * S + c * 512:m * S + c * 512 + 512],
                        Fs[:, sb3 * 2048 + c * 512:sb3 * 2048 + c * 512 + 512]),
                        waits=[v1_])
                    last_v = v1_
                ev3_bank[sb3] = last_v
                cp_last[sb3] = prev_ev3
            if KSUB3 >= 5:
                tok_x3 = ln_feature_major(XT, XT, 2, prev_ev3, u_guard=prev_stage)
                tok_final = tok_x3

        if STAGE >= 4:
            gs = C.emit("t", lambda e: e.matmul(PA[0:1, 0:1], CK[:, :],
                        SM[:, 53:54], start=True, stop=True),
                        waits=[l1stats, tCl, tok_x3])
            g1_ = C.emit("s", lambda e: e.mul(RWS[0:1, 0:1], PA[0:1, 0:1],
                         1.0 / (D * D)), waits=[gs])
            g2_ = C.emit("v", lambda e: e.tensor_scalar_add(RWS[0:1, 1:2],
                         RWS[0:1, 0:1], L1_EPS), waits=[g1_])
            g3_ = C.emit("v", lambda e: e.reciprocal(RWS[0:1, 2:3], RWS[0:1, 1:2]),
                         waits=[g2_])
            bc1 = C.emit("t", lambda e: e.matmul(PB[:, 0:1], C1[0:1, :],
                         RWS[0:1, 2:3], start=True, stop=True), waits=[g3_])
            bc2 = C.emit("t", lambda e: e.matmul(PB[:, 512:513], C1[0:1, :],
                         RWS[0:1, 0:1], start=True, stop=True), waits=[g3_])
            sv1 = C.emit("s", lambda e: e.copy(VSL[:, 0:1], PB[:, 0:1]), waits=[bc1])
            sv2 = C.emit("s", lambda e: e.copy(VSL[:, 1:2], PB[:, 512:513]), waits=[bc2])
            prev_rq4 = [None, None]
            prev_tr4 = [None, None]
            prev_t1 = None
            prev_z4 = None
            prev_mm4 = [None, None]
            psv_bank = [None, None]
            prev_cp4 = [None, None]
            for m in range(NT):
                sb4 = m % 2
                nat = U[:, sb4 * 2048:sb4 * 2048 + 2048]
                ld = g_dma(nat, dl1w[ts(m), :],
                           waits=[prev_rq4[sb4], prev_tr4[sb4], sv1, sv2])
                rq = C.emit("s", lambda e, nat=nat: e.activation(IQ[:, :], nat,
                            AF.Copy, bias=RB, scale=VSL[:, 0:1]),
                            waits=[ld, prev_t1])
                t1_ = C.emit("v", lambda e: e.tensor_scalar(Fs[:, 4096:6144],
                             IQ[:, :], -(RB), -1.0, ALU.add, ALU.max), waits=[rq])
                prev_t1 = t1_
                t2_ = C.emit("v", lambda e, nat=nat: e.tensor_scalar(
                    nat, Fs[:, 4096:6144], 1.0, 1.0, ALU.min, ALU.mult),
                    waits=[t1_])
                prev_rq4[sb4] = t2_
                so4 = 4096 + sb4 * 2048
                ot = bass.AP(U[:, :].tensor, so4, [[12288, 128], [128, 16], [1, 128]])
                tr = y_trdma(ot, nat, waits=[t2_, prev_mm4[sb4]])
                prev_tr4[sb4] = tr
                PM = PA if sb4 == 0 else PB
                mm = None
                for c in range(4):
                    for k in range(NT):
                        mm = C.emit("t", lambda e, c=c, k=k, PM=PM, so4=so4:
                            e.matmul(
                            PM[:, ts(c, 512)],
                            U[:, so4 + k * 128:so4 + k * 128 + 128],
                            XT[:, k * S + 512 * c:k * S + 512 * c + 512],
                            start=(k == 0), stop=(k == NT - 1)),
                            waits=[tr, psv_bank[sb4]])
                prev_mm4[sb4] = mm
                for c in range(4):
                    sv = C.emit("s", lambda e, c=c, PM=PM, sb4=sb4: e.activation(
                        Fs[:, sb4 * 2048 + c * 512:sb4 * 2048 + c * 512 + 512],
                        PM[:, ts(c, 512)], AF.Copy, bias=0.0, scale=VSL[:, 1:2]),
                        waits=[mm, prev_cp4[sb4] if c == 0 else None])
                    prev_z4 = C.emit("v", lambda e, m=m, c=c, sb4=sb4:
                        e.scalar_tensor_tensor(
                        ZT[:, m * S + c * 512:m * S + c * 512 + 512],
                        Fs[:, sb4 * 2048 + c * 512:sb4 * 2048 + c * 512 + 512],
                        VL1B[:, m:m + 1],
                        XT[:, m * S + c * 512:m * S + c * 512 + 512],
                        ALU.add, ALU.add), waits=[sv])
                    psv_bank[sb4] = sv
                prev_cp4[sb4] = prev_z4
            tok_x4 = ln_feature_major(ZT, XT, 2, prev_z4,
                                      u_guard=(prev_tr4[0], prev_tr4[1]))
            tok_final = tok_x4

        out_done = []
        prev_cv = [None, None]
        for j in range(16 if not KPROBE else 0):
            sb = j % 2
            g1 = U[:, 4096 * sb:4096 * sb + 2048]
            g2 = U[:, 4096 * sb + 2048:4096 * sb + 4096]
            it_ap = bass.AP(XT[:, :].tensor, j * 128,
                            [[NT * S, 128], [S, NT], [1, 128]])
            td = y_dma(g1[:, :], it_ap, waits=[tok_final, prev_cv[sb]])
            tt = y_trdma(bass.AP(U[:, :].tensor, 4096 * sb + 2048,
                                 [[12288, 128], [128, 16], [1, 128]]),
                         g1[:, :], waits=[td])
            cv = C.emit("s", lambda e, g2=g2, sb=sb: e.copy(
                Fs[:, ts(sb, 2048)], g2[:, :]),
                waits=[tt] + ([out_done[j - 2]] if j >= 2 else []))
            prev_cv[sb] = cv
            st = y_dma(dout[ts(j), :], Fs[:, ts(sb, 2048)], waits=[cv], sem="do")
            out_done.append(st)

        final_o = (C.cnt["o0"], C.cnt["o1"])

        def mk(ename):
            def body(eng):
                for fn, waits, inc, n in C.ops[ename]:
                    for (snm, v) in waits:
                        eng.wait_ge(sems[snm], v)
                    ins = fn(eng)
                    if n:
                        ins.then_inc(sems[inc], n)
                if ename == "y":
                    eng.wait_ge(sems["o0"], final_o[0])
                    eng.wait_ge(sems["o1"], final_o[1])
            return body

        block.tensor(mk("t"))
        block.vector(mk("v"))
        block.scalar(mk("s"))
        block.gpsimd(mk("g"))
        block.sync(mk("y"))

    return nc


def _rope_tables():
    j = np.arange(0, DQK, 2, dtype=np.float32)
    inv = (np.float32(1.0) / np.power(np.float32(10000.0),
           (j / np.float32(DQK)).astype(np.float32))).astype(np.float32)
    ang = (np.arange(S, dtype=np.float32)[:, None] * inv[None, :]).astype(np.float32)
    c = np.cos(ang).astype(np.float32)
    s = np.sin(ang).astype(np.float32)
    rows = np.arange(128) % 32
    return np.ascontiguousarray(c[:, rows].T), np.ascontiguousarray(s[:, rows].T)


def _vec_tiles(v):
    return np.ascontiguousarray(np.asarray(v, dtype=np.float32).reshape(NT, 128).T)


def make_inputs(L0_attn_x, L0_attn_Wa, L0_attn_Wx, L0_attn_ba, L0_attn_bx,
                L0_attn_Lam, L0_ffn_w_gate_up, L0_ffn_w_down, L1_attn_w_in,
                L1_attn_w_out, L1_attn_rms_w, L1_ffn_weight, L1_ffn_bias):
    cosT, sinT = _rope_tables()
    P = np.zeros((128, 128), dtype=np.float32)
    for h in range(2):
        for jj in range(32):
            P[h * 64 + jj + 32, h * 64 + jj] = -1.0
            P[h * 64 + jj, h * 64 + jj + 32] = 1.0
    mask = np.where(np.arange(128)[:, None] > np.arange(128)[None, :],
                    np.float32(-1e5), np.float32(0.0)).astype(np.float32)
    shared = dict(
        waT=np.ascontiguousarray(np.asarray(L0_attn_Wa, dtype=np.float32).T),
        wxT=np.ascontiguousarray(np.asarray(L0_attn_Wx, dtype=np.float32).T),
        ba2=_vec_tiles(L0_attn_ba), bx2=_vec_tiles(L0_attn_bx),
        lam=_vec_tiles(L0_attn_Lam),
        w1=np.ascontiguousarray(L0_ffn_w_gate_up, dtype=np.float32),
        w2=np.ascontiguousarray(L0_ffn_w_down, dtype=np.float32),
        winT=np.ascontiguousarray(np.asarray(L1_attn_w_in, dtype=np.float32).T),
        woutT=np.ascontiguousarray(np.asarray(L1_attn_w_out, dtype=np.float32).T),
        rms2=_vec_tiles(L1_attn_rms_w),
        l1w=np.ascontiguousarray(L1_ffn_weight, dtype=np.float32),
        l1b2=_vec_tiles(L1_ffn_bias),
        identb=np.eye(128, dtype=BF), identf=np.eye(128, dtype=np.float32),
        prope=P.astype(BF), mask=mask, cosT=cosT, sinT=sinT,
        ones1=np.ones((1, 128), dtype=np.float32),
        onesd=np.full((128, 1), 1.0 / D, dtype=BF),
        onesk=np.ones((128, 1), dtype=np.float32),
        cvec=np.tile(np.array([LOG_HALF, RB, LN_EPS, RMS_EPS, -0.5, -20.0,
                               0.125, 0.0], dtype=np.float32), (128, 1)),
    )
    xs = np.asarray(L0_attn_x, dtype=np.float32)
    return [dict(shared, x=np.ascontiguousarray(xs[c])) for c in range(NCORE)]


def kernel(**inputs):
    nc = build_program()
    in_maps = make_inputs(**inputs)
    res = run_bass_kernel_spmd(nc, in_maps, core_ids=list(range(NCORE)))
    return np.stack([np.asarray(r["out"], dtype=np.float32) for r in res.results])


# version 13 (final)
# speedup vs baseline: 1.0894x; 1.0476x over previous
import os
import numpy as np
import ml_dtypes
import concourse.bass as bass
import concourse.mybir as mybir
from concourse.bass_utils import run_bass_kernel_spmd

F32 = mybir.dt.float32
BF16 = mybir.dt.bfloat16
I32 = mybir.dt.int32
AF = mybir.ActivationFunctionType
ALU = mybir.AluOpType
BF = ml_dtypes.bfloat16
AX = mybir.AxisListType.X

D = 2048
S = 2048
NT = D // 128
IFF = 5632
NI = IFF // 128
CW = 3072
NG = 8
DQK = 64
STAGE = int(os.environ.get("KSTAGE", "5"))
KSUB = int(os.environ.get("KSUB", "9"))
KSUB3 = int(os.environ.get("KSUB3", "9"))
KPROBE = int(os.environ.get("KPROBE", "0"))
NCORE = int(os.environ.get("KCORES", "8"))

LN_EPS = 1e-5
RMS_EPS = 1e-6
QMAX = 127.0
QEPS = 1e-8
L1_EPS = 1e-10
RB = 256.0
LOG_HALF = float(np.log(0.5))


NRR = 6
VTS = 96


class Ctx:
    ENG = ("t", "v", "s", "g", "y")

    def __init__(self):
        self.ops = {e: [] for e in self.ENG}
        self.semnames = (["t", "v", "s", "g"]
                         + [f"y{i}" for i in range(NRR)]
                         + [f"g{i}" for i in range(NRR)]
                         + ["o0", "o1"])
        self.cnt = {n: 0 for n in self.semnames}
        self.hw = {}
        self.rr = {"y": 0, "g": 0, "o": 0}
        self.seen = {e: {} for e in self.ENG}

    def emit(self, eng, fn, waits=(), inc=None, n=1):
        real = []
        flat = []
        for w in waits:
            if w is None:
                continue
            if isinstance(w, tuple) and w and isinstance(w[0], tuple):
                flat.extend(w)
            else:
                flat.append(w)
        seen = {}
        for w in flat:
            s, v = w
            if v <= 0:
                continue
            if seen.get(s, 0) < v:
                seen[s] = v
        eseen = self.seen[eng]
        for s, v in sorted(seen.items()):
            if v <= eseen.get(s, 0):
                continue
            eseen[s] = v
            real.append((s, v))
        if inc is None:
            inc, n = eng, 1
        self.cnt[inc] += n
        tok = (inc, self.cnt[inc])
        self.ops[eng].append((fn, tuple(real), inc, n))
        return tok

    def dma(self, eng, fn, waits=(), queue=None, nsem=None):
        q = queue or eng
        nsem = nsem if nsem is not None else (NRR if q in ("y", "g") else 2)
        sem = f"{q}{self.rr[q] % nsem}"
        self.rr[q] += 1
        self_wait = (sem, self.cnt[sem])
        return self.emit(eng, fn, waits=list(waits) + [self_wait],
                         inc=sem, n=16)


def ts(i, n=128):
    return slice(i * n, (i + 1) * n)


def build_program():
    nc = bass.Bass("TRN2", target_bir_lowering=False)

    dx = nc.dram_tensor("x", [S, D], F32, kind="ExternalInput").ap()
    dwaT = nc.dram_tensor("waT", [D, D], F32, kind="ExternalInput").ap()
    dwxT = nc.dram_tensor("wxT", [D, D], F32, kind="ExternalInput").ap()
    dba = nc.dram_tensor("ba2", [128, NT], F32, kind="ExternalInput").ap()
    dbx = nc.dram_tensor("bx2", [128, NT], F32, kind="ExternalInput").ap()
    dlam = nc.dram_tensor("lam", [128, NT], F32, kind="ExternalInput").ap()
    dw1 = nc.dram_tensor("w1", [2 * IFF, D], F32, kind="ExternalInput").ap()
    dw2 = nc.dram_tensor("w2", [D, IFF], F32, kind="ExternalInput").ap()
    dwinT = nc.dram_tensor("winT", [D, CW], F32, kind="ExternalInput").ap()
    dwoutT = nc.dram_tensor("woutT", [D, D], F32, kind="ExternalInput").ap()
    drms = nc.dram_tensor("rms2", [128, NT], F32, kind="ExternalInput").ap()
    dl1w = nc.dram_tensor("l1w", [D, D], F32, kind="ExternalInput").ap()
    dl1b = nc.dram_tensor("l1b2", [128, NT], F32, kind="ExternalInput").ap()
    dident = nc.dram_tensor("identb", [128, 128], BF16, kind="ExternalInput").ap()
    didf = nc.dram_tensor("identf", [128, 128], F32, kind="ExternalInput").ap()
    dprope = nc.dram_tensor("prope", [128, 128], BF16, kind="ExternalInput").ap()
    dmask = nc.dram_tensor("mask", [128, 128], F32, kind="ExternalInput").ap()
    dcos = nc.dram_tensor("cosT", [128, S], F32, kind="ExternalInput").ap()
    dsin = nc.dram_tensor("sinT", [128, S], F32, kind="ExternalInput").ap()
    dones1 = nc.dram_tensor("ones1", [1, 128], F32, kind="ExternalInput").ap()
    donesd = nc.dram_tensor("onesd", [128, 1], BF16, kind="ExternalInput").ap()
    donesk = nc.dram_tensor("onesk", [128, 1], F32, kind="ExternalInput").ap()
    dcvec = nc.dram_tensor("cvec", [128, 8], F32, kind="ExternalInput").ap()

    dout = nc.dram_tensor("out", [S, D], F32, kind="ExternalOutput").ap()
    dh = nc.dram_tensor("h_scr", [IFF, S], BF16).ap()
    dw2q = nc.dram_tensor("w2q_scr", [D, IFF], BF16).ap()

    from contextlib import ExitStack
    es = ExitStack()
    with es:
        XT = es.enter_context(nc.sbuf_tensor([128, NT * S], BF16))
        ZT = es.enter_context(nc.sbuf_tensor([128, NT * S], BF16))
        U = es.enter_context(nc.sbuf_tensor([128, 12288], BF16))
        Fs = es.enter_context(nc.sbuf_tensor([128, 3 * 2048], F32))
        IQ = es.enter_context(nc.sbuf_tensor([128, 2048], I32))
        SM = es.enter_context(nc.sbuf_tensor([128, 64], F32))
        SRA = es.enter_context(nc.sbuf_tensor([16, 128], F32))
        SRB = es.enter_context(nc.sbuf_tensor([16, 128], F32))
        RWS = es.enter_context(nc.sbuf_tensor([1, 2048], F32))
        RWS2 = es.enter_context(nc.sbuf_tensor([1, 2048], F32))
        CID = es.enter_context(nc.sbuf_tensor([128, 128], BF16))
        CIDF = es.enter_context(nc.sbuf_tensor([128, 128], F32))
        CP = es.enter_context(nc.sbuf_tensor([128, 128], BF16))
        CM = es.enter_context(nc.sbuf_tensor([128, 128], F32))
        C1 = es.enter_context(nc.sbuf_tensor([1, 128], F32))
        CD = es.enter_context(nc.sbuf_tensor([128, 1], BF16))
        CK = es.enter_context(nc.sbuf_tensor([128, 1], F32))
        CV = es.enter_context(nc.sbuf_tensor([128, 8], F32))
        VBA2 = es.enter_context(nc.sbuf_tensor([128, NT], F32))
        VBX2 = es.enter_context(nc.sbuf_tensor([128, NT], F32))
        VLAM = es.enter_context(nc.sbuf_tensor([128, NT], F32))
        VC2 = es.enter_context(nc.sbuf_tensor([128, NT], F32))
        VRMS = es.enter_context(nc.sbuf_tensor([128, NT], F32))
        VL1B = es.enter_context(nc.sbuf_tensor([128, NT], F32))
        VS1 = es.enter_context(nc.sbuf_tensor([128, 2 * NI], F32))
        VSC = es.enter_context(nc.sbuf_tensor([128, NI], F32))
        VS2 = es.enter_context(nc.sbuf_tensor([128, NT], F32))
        VSL = es.enter_context(nc.sbuf_tensor([128, 2], F32))

        PA = es.enter_context(nc.psum_tensor([128, 2048], F32))
        PB = es.enter_context(nc.psum_tensor([128, 2048], F32))

        sems = {}
        for nme in Ctx().semnames:
            sems[nme] = es.enter_context(nc.semaphore("sem_" + nme))
        block = es.enter_context(nc.Block())

        C = Ctx()

        def y_dma(dst, src, waits=(), sem=None):
            return C.dma("y", lambda e, d=dst, s=src: e.dma_start(d, s), waits,
                         queue="o" if sem == "do" else "y")

        def g_dma(dst, src, waits=()):
            return C.dma("g", lambda e, d=dst, s=src: e.dma_start(d, s), waits)

        def y_trdma(dst, src, waits=()):
            return C.dma("y", lambda e, d=dst, s=src: e.dma_start_transpose(d, s),
                         waits)

        _ctoks = []
        for dst, src in ((CID[:, :], dident), (CIDF[:, :], didf), (CP[:, :], dprope),
                         (CM[:, :], dmask),
                         (C1[:, :], dones1), (CD[:, :], donesd),
                         (CK[:, :], donesk), (CV[:, :], dcvec), (VBA2[:, :], dba), (VBX2[:, :], dbx),
                         (VLAM[:, :], dlam), (VRMS[:, :], drms), (VL1B[:, :], dl1b)):
            _ctoks.append(y_dma(dst, src))
        tCl = tuple(_ctoks)

        t1 = C.emit("s", lambda e: e.activation(VC2[:, :], VLAM[:, :], AF.Exp,
                    bias=0.0, scale=-1.0), waits=[tCl])
        t2 = C.emit("v", lambda e: e.tensor_scalar_add(VC2[:, :], VC2[:, :], 1.0), waits=[t1])
        t3 = C.emit("s", lambda e: e.activation(VC2[:, :], VC2[:, :], AF.Ln,
                    bias=0.0, scale=1.0), waits=[t2])
        tok_c2 = C.emit("s", lambda e: e.mul(VC2[:, :], VC2[:, :], -4.0), waits=[t3])
        t5 = C.emit("s", lambda e: e.mul(VBA2[:, :], VBA2[:, :], 0.5), waits=[tCl])
        tok_b2 = C.emit("s", lambda e: e.mul(VBX2[:, :], VBX2[:, :], 0.5), waits=[t5])

        stage_tok = [None, None]
        for j in range(16):
            sb = j % 2
            buf = U[:, 4096 * sb:4096 * sb + 2048]
            buf2 = U[:, 4096 * sb + 2048:4096 * sb + 4096]
            tl = g_dma(buf, dx[ts(j), :], waits=[stage_tok[sb]])
            tt = y_trdma(bass.AP(U[:, :].tensor, 4096 * sb + 2048,
                                 [[12288, 128], [128, 16], [1, 128]]),
                         buf[:, :], waits=[tl])
            out_ap = bass.AP(XT[:, :].tensor, j * 128,
                             [[NT * S, 128], [S, NT], [1, 128]])
            tt2 = y_dma(out_ap, buf2[:, :], waits=[tt])
            stage_tok[sb] = tt2
        tok_x = stage_tok[1]

        def ln_feature_major(src_big, dst_big, eps_col, tok_ready, u_guard=None):
            prev_mm = u_guard
            for g in range(NT):
                zg = src_big[:, ts(g, S)]
                sq = U[:, ts(g % 2, 2048)]
                tsq = C.emit("s", lambda e, a=sq, b=zg: e.activation(
                    a, b, AF.Square, bias=0.0, scale=1.0),
                    waits=[tok_ready, prev_mm])
                for c in range(4):
                    C.emit("t", lambda e, g=g, c=c, zg=zg: e.matmul(
                        PA[0:1, ts(c, 512)], CD[:, :], zg[:, ts(c, 512)],
                        start=(g == 0), stop=(g == NT - 1)),
                        waits=[tok_ready, tCl])
                    prev_mm = C.emit("t", lambda e, g=g, c=c, sq=sq: e.matmul(
                        PB[0:1, ts(c, 512)], CD[:, :], sq[:, ts(c, 512)],
                        start=(g == 0), stop=(g == NT - 1)), waits=[tsq])
            r1 = C.emit("s", lambda e: e.activation(RWS[0:1, :], PA[0:1, 0:2048],
                        AF.Square, bias=0.0, scale=1.0), waits=[prev_mm])
            r2 = C.emit("v", lambda e: e.scalar_tensor_tensor(RWS[0:1, :],
                        PB[0:1, 0:2048], 1.0, RWS[0:1, :], ALU.mult, ALU.subtract),
                        waits=[r1])
            r3 = C.emit("s", lambda e: e.activation(RWS[0:1, :], RWS[0:1, :], AF.Ln,
                        bias=CV[0:1, eps_col:eps_col + 1], scale=1.0), waits=[r2])
            r4 = C.emit("s", lambda e: e.activation(RWS[0:1, :], RWS[0:1, :], AF.Exp,
                        bias=0.0, scale=CV[0:1, 4:5]), waits=[r3])
            b2 = C.emit("s", lambda e: e.copy(RWS2[0:1, :], PA[0:1, 0:2048]), waits=[r4])
            last = None
            for c in range(4):
                C.emit("t", lambda e, c=c: e.matmul(PA[:, ts(c, 512)], C1[:, :],
                       RWS[0:1, ts(c, 512)], start=True, stop=True), waits=[b2])
                last = C.emit("t", lambda e, c=c: e.matmul(PB[:, ts(c, 512)], C1[:, :],
                              RWS2[0:1, ts(c, 512)], start=True, stop=True), waits=[b2])
            v1 = C.emit("s", lambda e: e.copy(Fs[:, 0:2048], PA[:, 0:2048]), waits=[last])
            v2 = C.emit("s", lambda e: e.copy(Fs[:, 2048:4096], PB[:, 0:2048]), waits=[v1])
            out_tok = None
            C.ln_tile_toks = []
            for g in range(NT):
                zg = src_big[:, ts(g, S)]
                dg = dst_big[:, ts(g, S)]
                a1 = C.emit("v", lambda e, zg=zg: e.tensor_sub(
                    Fs[:, 4096:6144], zg, Fs[:, 2048:4096]), waits=[v2, out_tok])
                out_tok = C.emit("v", lambda e, dg=dg: e.tensor_mul(
                    dg, Fs[:, 4096:6144], Fs[:, 0:2048]), waits=[a1])
                C.ln_tile_toks.append(out_tok)
            return out_tok

        def max_stat_bcast(tok_acc):
            red = None
            for c in range(16):
                b = (c % 4) * 512
                tt = C.emit("t", lambda e, b=b, c=c: e.transpose(
                    PA[:, b:b + 128],
                    Fs[:, 4096 + c * 128:4096 + (c + 1) * 128], CIDF[:, :]),
                    waits=[tok_acc, red, tCl])
                red = C.emit("v", lambda e, b=b, c=c: e.reduce_max(
                    SM[:, c:c + 1], PA[:, b:b + 128], AX), waits=[tt])
            q1 = C.emit("v", lambda e: e.tensor_scalar(SM[:, 16:32], SM[:, 0:16],
                        1.0 / QMAX, QEPS, ALU.mult, ALU.max), waits=[red])
            q2 = C.emit("v", lambda e: e.reciprocal(SM[:, 32:48], SM[:, 16:32]), waits=[q1])
            q3a = C.emit("t", lambda e: e.transpose(PB[0:16, 0:128], SM[:, 16:32],
                         CIDF[:, :]), waits=[q2])
            q3b = C.emit("t", lambda e: e.transpose(PB[0:16, 512:640], SM[:, 32:48],
                         CIDF[:, :]), waits=[q2])
            q4a = C.emit("s", lambda e: e.copy(SRA[:, :], PB[0:16, 0:128]), waits=[q3a])
            q4b = C.emit("s", lambda e: e.copy(SRB[:, :], PB[0:16, 512:640]), waits=[q3b])
            c1 = y_dma(RWS[0:1, :], SRB[:, :], waits=[q4b])
            c2 = y_dma(RWS2[0:1, :], SRA[:, :], waits=[q4a])
            last = None
            for c in range(4):
                C.emit("t", lambda e, c=c: e.matmul(PA[:, ts(c, 512)], C1[:, :],
                       RWS[0:1, ts(c, 512)], start=True, stop=True), waits=[c1, c2])
                last = C.emit("t", lambda e, c=c: e.matmul(PB[:, ts(c, 512)], C1[:, :],
                              RWS2[0:1, ts(c, 512)], start=True, stop=True), waits=[c1, c2])
            b1 = C.emit("s", lambda e: e.copy(Fs[:, 0:2048], PA[:, 0:2048]), waits=[last])
            b2 = C.emit("s", lambda e: e.copy(Fs[:, 2048:4096], PB[:, 0:2048]), waits=[b1])
            return b2

        tok_final = tok_x

        HW2 = IFF // 2
        W2A, W2B = 22528, 22528 + HW2
        w2s = dict(pw=None, ptr=None, piv2=None, wq=None)
        w2st = []
        RUN_W2 = STAGE >= 2 and KSUB >= 3

        def emit_w2quant(m):
            ldA = g_dma(ZT[:, W2A:W2A + HW2], dw2[ts(m), 0:HW2],
                        waits=[w2s["ptr"]])
            ldB = g_dma(ZT[:, W2B:W2B + HW2], dw2[ts(m), HW2:IFF],
                        waits=[w2s["ptr"]])
            rdA = C.emit("v", lambda e: e.reduce_max(SM[:, 48:49],
                         ZT[:, W2A:W2A + HW2],
                         AX, apply_absolute_value=True), waits=[ldA, w2s["piv2"]])
            rdB = C.emit("v", lambda e: e.reduce_max(SM[:, 49:50],
                         ZT[:, W2B:W2B + HW2],
                         AX, apply_absolute_value=True), waits=[ldB, rdA])
            mx = C.emit("v", lambda e: e.tensor_max(SM[:, 48:49],
                        SM[:, 48:49], SM[:, 49:50]), waits=[rdB])
            sc = C.emit("v", lambda e, m=m: e.tensor_scalar(
                VS2[:, m:m + 1], SM[:, 48:49], 1.0 / QMAX, QEPS,
                ALU.mult, ALU.max), waits=[mx])
            iv = C.emit("v", lambda e, m=m: e.reciprocal(SM[:, 50:51],
                        VS2[:, m:m + 1]), waits=[sc, w2s["wq"]])
            w2s["piv2"] = iv
            half_toks = []
            for cc in range(4):
                lo = W2A + cc * 1408
                rq = C.emit("s", lambda e, lo=lo: e.activation(
                    IQ[:, 0:1408], ZT[:, lo:lo + 1408], AF.Copy, bias=RB,
                    scale=SM[:, 50:51]),
                    waits=[iv] + ([half_toks[-1]] if half_toks else [w2s["wq"]]))
                wq = C.emit("v", lambda e, lo=lo: e.tensor_scalar_add(
                    ZT[:, lo:lo + 1408], IQ[:, 0:1408], -256.0), waits=[rq])
                half_toks.append(wq)
            w2s["wq"] = half_toks[-1]
            stA = y_dma(dw2q[ts(m), 0:HW2], ZT[:, W2A:W2A + HW2],
                        waits=[half_toks[1], w2s["ptr"]])
            ptr = y_dma(dw2q[ts(m), HW2:IFF], ZT[:, W2B:W2B + HW2],
                        waits=[half_toks[3], stA])
            w2s["ptr"] = ptr
            w2st.append(ptr)

        if RUN_W2:
            for m in range(4):
                emit_w2quant(m)

        if STAGE >= 1:
            prev_ra = None
            prev_rb = None
            prev_z = None
            prev_s3 = None
            prev_v3 = None
            prev_v5 = None
            wtok = [stage_tok[0], stage_tok[1]]
            for g in range(NT):
                sb = g % 2
                base = 4096 * sb
                wa_st = U[:, base:base + 2048]
                wx_st = U[:, base + 2048:base + 4096]
                in_a = bass.AP(dwaT.tensor, g * 128, [[D, 128], [128 * D, 16], [1, 128]])
                in_x = bass.AP(dwxT.tensor, g * 128, [[D, 128], [128 * D, 16], [1, 128]])
                oa = bass.AP(U[:, :].tensor, base, [[12288, 128], [128, 16], [1, 128]])
                ox = bass.AP(U[:, :].tensor, base + 2048, [[12288, 128], [128, 16], [1, 128]])
                ta = g_dma(oa, in_a, waits=[wtok[sb]])
                txw = g_dma(ox, in_x, waits=[wtok[sb]])
                mm = None
                for c in range(4):
                    for k in range(NT):
                        mm = C.emit("t", lambda e, c=c, k=k, wa_st=wa_st: e.matmul(
                            PA[:, ts(c, 512)], wa_st[:, ts(k)],
                            XT[:, k * S + 512 * c:k * S + 512 * c + 512],
                            start=(k == 0), stop=(k == NT - 1)),
                            waits=[ta, tok_x, prev_ra])
                mm2 = None
                for c in range(4):
                    for k in range(NT):
                        mm2 = C.emit("t", lambda e, c=c, k=k, wx_st=wx_st: e.matmul(
                            PB[:, ts(c, 512)], wx_st[:, ts(k)],
                            XT[:, k * S + 512 * c:k * S + 512 * c + 512],
                            start=(k == 0), stop=(k == NT - 1)),
                            waits=[txw, prev_rb])
                wtok[sb] = mm2
                rt = U[:, 8192:10240]
                it = U[:, 10240:12288]
                prev_ra = C.emit("s", lambda e, g=g, rt=rt: e.activation(
                    rt, PA[:, 0:2048], AF.Tanh, bias=VBA2[:, g:g + 1], scale=0.5),
                    waits=[mm, tok_b2, prev_s3])
                prev_rb = C.emit("s", lambda e, g=g, it=it: e.activation(
                    it, PB[:, 0:2048], AF.Tanh, bias=VBX2[:, g:g + 1], scale=0.5),
                    waits=[mm2, prev_v3])
                s3 = C.emit("s", lambda e, g=g, rt=rt: e.activation(
                    Fs[:, 0:2048], rt, AF.Exp, bias=VC2[:, g:g + 1],
                    scale=VC2[:, g:g + 1]), waits=[prev_ra, tok_c2, prev_v5])
                prev_s3 = s3
                v1 = C.emit("v", lambda e: e.tensor_mul(Fs[:, 2048:4096],
                            Fs[:, 0:2048], Fs[:, 0:2048]), waits=[s3])
                v2 = C.emit("v", lambda e: e.tensor_scalar(Fs[:, 2048:4096],
                            Fs[:, 2048:4096], -1.0, 1.0, ALU.mult, ALU.add), waits=[v1])
                s4 = C.emit("s", lambda e: e.activation(Fs[:, 2048:4096],
                            Fs[:, 2048:4096], AF.Ln, bias=0.0, scale=1.0), waits=[v2])
                s5 = C.emit("s", lambda e: e.activation(Fs[:, 2048:4096],
                            Fs[:, 2048:4096], AF.Exp, bias=CV[:, 0:1], scale=0.5), waits=[s4])
                v3 = C.emit("v", lambda e, g=g, it=it: e.scalar_tensor_tensor(
                    Fs[:, 4096:6144], it, 1.0, XT[:, ts(g, S)], ALU.add, ALU.mult),
                    waits=[prev_rb, tok_x])
                prev_v3 = v3
                v4 = C.emit("v", lambda e: e.tensor_mul(Fs[:, 2048:4096],
                            Fs[:, 2048:4096], Fs[:, 4096:6144]), waits=[v3, s5])
                v5 = C.emit("v", lambda e: e.tensor_tensor_scan(
                    Fs[:, 4096:6144], Fs[:, 0:2048], Fs[:, 2048:4096], 0.0,
                    ALU.mult, ALU.add), waits=[v4])
                prev_v5 = v5
                if RUN_W2 and g < 12:
                    emit_w2quant(g + 4)
                zguard = w2st[-1] if (RUN_W2 and g in (11, 12, 13)) else None
                prev_z = C.emit("v", lambda e, g=g: e.tensor_add(
                    ZT[:, ts(g, S)], Fs[:, 4096:6144], XT[:, ts(g, S)]),
                    waits=[v5, zguard])
            tok_x1 = ln_feature_major(ZT, XT, 2, prev_z, u_guard=wtok[1])
            tok_final = tok_x1

        if STAGE >= 2:
            acc = None
            sprev = None
            for g in range(NT):
                sprev = C.emit("s", lambda e, g=g: e.activation(U[:, 0:2048],
                               XT[:, ts(g, S)], AF.Abs, bias=0.0, scale=1.0),
                               waits=[tok_x1, acc])
                if g == 0:
                    acc = C.emit("v", lambda e: e.tensor_copy(Fs[:, 4096:6144],
                                 U[:, 0:2048]), waits=[sprev])
                else:
                    acc = C.emit("v", lambda e: e.tensor_max(Fs[:, 4096:6144],
                                 Fs[:, 4096:6144], U[:, 0:2048]), waits=[sprev])
            tok_bc = max_stat_bcast(acc)
            if KSUB == 0:
                tok_final = tok_bc
            tq = None
            prev_iqr = w2s["wq"]
            for g in range(NT):
                w1_ = C.emit("v", lambda e, g=g: e.tensor_mul(
                    Fs[:, 4096:6144], XT[:, ts(g, S)], Fs[:, 0:2048]),
                    waits=[tok_bc, tq])
                w2_ = C.emit("s", lambda e: e.activation(IQ[:, :], Fs[:, 4096:6144],
                             AF.Copy, bias=RB, scale=1.0), waits=[w1_, prev_iqr])
                w3_ = C.emit("v", lambda e: e.tensor_scalar_add(Fs[:, 4096:6144],
                             IQ[:, :], -256.0), waits=[w2_])
                prev_iqr = w3_
                tq = C.emit("v", lambda e, g=g: e.tensor_mul(
                    ZT[:, ts(g, S)], Fs[:, 4096:6144], Fs[:, 2048:4096]), waits=[w3_])
            tok_xq = tq
            if KSUB == 1:
                tok_final = tok_xq

            RUN_GU = KSUB >= 2
            hacc = None
            prev_e1 = None
            prev_e2 = None
            prev_e3 = None
            prev_rq = {"g": None, "u": None}
            prev_wq = w2s["wq"]
            h_done = None
            st_free = None
            prev_iv = w2s["piv2"]
            prev_trt = {"g": None, "u": None}
            e1_hist = [None, None]
            gs = dict(prev_rq=prev_rq, prev_wq=prev_wq, prev_iv=prev_iv,
                      prev_trt=prev_trt, st_free=None)

            def emit_w1quant(i):
                wgl = g_dma(U[:, 0:2048], dw1[ts(i), :],
                            waits=[gs["prev_rq"]["g"], gs["prev_trt"]["g"],
                                   tok_xq if i == 0 else None])
                wul = g_dma(U[:, 2048:4096], dw1[ts(NI + i), :],
                            waits=[gs["prev_rq"]["u"], gs["prev_trt"]["u"]])
                trt = {}
                for nm, off, ld in (("g", 0, wgl), ("u", 2048, wul)):
                    wn = U[:, off:off + 2048]
                    col = i if nm == "g" else NI + i
                    rd = C.emit("v", lambda e, wn=wn: e.reduce_max(
                        SM[:, 48:49], wn, AX, apply_absolute_value=True),
                        waits=[ld, gs["prev_iv"]])
                    sc = C.emit("v", lambda e, col=col: e.tensor_scalar(
                        VS1[:, col:col + 1], SM[:, 48:49], 1.0 / QMAX, QEPS,
                        ALU.mult, ALU.max), waits=[rd])
                    iv = C.emit("v", lambda e, col=col: e.reciprocal(
                        SM[:, 50:51], VS1[:, col:col + 1]),
                        waits=[sc, gs["prev_rq"][nm]])
                    gs["prev_iv"] = iv
                    rq = C.emit("s", lambda e, wn=wn: e.activation(IQ[:, :], wn,
                                AF.Copy, bias=RB, scale=SM[:, 50:51]),
                                waits=[iv, gs["prev_wq"]])
                    gs["prev_rq"][nm] = rq
                    wq = C.emit("v", lambda e, wn=wn: e.tensor_scalar_add(
                        wn, IQ[:, :], -256.0), waits=[rq])
                    gs["prev_wq"] = wq
                    so_ = 4096 + (0 if nm == "g" else 2048)
                    ot = bass.AP(U[:, :].tensor, so_, [[12288, 128], [128, 16], [1, 128]])
                    trt[nm] = y_trdma(ot, wn, waits=[wq, gs["st_free"]])
                    gs["prev_trt"][nm] = trt[nm]
                cs = C.emit("v", lambda e, i=i: e.tensor_mul(VSC[:, i:i + 1],
                            VS1[:, i:i + 1], VS1[:, NI + i:NI + i + 1]),
                            waits=[gs["prev_rq"]["u"], gs["prev_rq"]["g"]])
                cs2 = C.emit("v", lambda e, i=i: e.tensor_scalar_mul(
                    VSC[:, i:i + 1], VSC[:, i:i + 1], 0.5), waits=[cs])
                tcol = 54 + (i % 2)
                tsc = C.emit("v", lambda e, i=i, tcol=tcol: e.tensor_scalar_mul(
                    SM[:, tcol:tcol + 1], VS1[:, i:i + 1], 0.5),
                    waits=[cs2, e1_hist[i % 2]])
                trt["tsc"] = tsc
                trt["cs2"] = cs2
                return trt

            trt_cur = emit_w1quant(0) if RUN_GU else None
            for i in range(NI if RUN_GU else 0):
                trt = trt_cur
                mmg = None
                for c in range(4):
                    for k in range(NT):
                        mmg = C.emit("t", lambda e, c=c, k=k: e.matmul(
                            PA[:, ts(c, 512)],
                            U[:, 4096 + k * 128:4096 + k * 128 + 128],
                            ZT[:, k * S + 512 * c:k * S + 512 * c + 512],
                            start=(k == 0), stop=(k == NT - 1)),
                            waits=[trt["g"], tok_xq, prev_e2])
                mmu = None
                for c in range(4):
                    for k in range(NT):
                        mmu = C.emit("t", lambda e, c=c, k=k: e.matmul(
                            PB[:, ts(c, 512)],
                            U[:, 6144 + k * 128:6144 + k * 128 + 128],
                            ZT[:, k * S + 512 * c:k * S + 512 * c + 512],
                            start=(k == 0), stop=(k == NT - 1)),
                            waits=[trt["u"], prev_e3])
                gs["st_free"] = mmu
                if i + 1 < NI:
                    trt_cur = emit_w1quant(i + 1)
                tcol = 54 + (i % 2)
                e1 = C.emit("s", lambda e, tcol=tcol: e.activation(
                            U[:, 8192:10240], PA[:, 0:2048], AF.Tanh, bias=0.0,
                            scale=SM[:, tcol:tcol + 1]),
                            waits=[mmg, trt["tsc"], h_done])
                e1_hist[i % 2] = e1
                prev_e1 = e1
                e2 = C.emit("v", lambda e: e.scalar_tensor_tensor(Fs[:, 0:2048],
                            U[:, 8192:10240], 1.0, PA[:, 0:2048], ALU.add, ALU.mult),
                            waits=[e1])
                prev_e2 = e2
                e3 = C.emit("v", lambda e: e.tensor_mul(Fs[:, 0:2048], Fs[:, 0:2048],
                            PB[:, 0:2048]), waits=[e2, mmu])
                prev_e3 = e3
                e4 = C.emit("v", lambda e, i=i: e.tensor_scalar_mul(
                    U[:, 10240:12288], Fs[:, 0:2048], VSC[:, i:i + 1]),
                    waits=[e3, trt["cs2"], h_done])
                sab = C.emit("s", lambda e: e.activation(Fs[:, 2048:4096],
                             U[:, 10240:12288], AF.Abs, bias=0.0, scale=1.0), waits=[e4])
                if i == 0:
                    hacc = C.emit("v", lambda e: e.tensor_copy(Fs[:, 4096:6144],
                                  Fs[:, 2048:4096]), waits=[sab])
                else:
                    hacc = C.emit("v", lambda e: e.tensor_max(Fs[:, 4096:6144],
                                  Fs[:, 4096:6144], Fs[:, 2048:4096]), waits=[sab])
                h_done = y_dma(dh[ts(i), :], U[:, 10240:12288], waits=[e4])
            prev_wq = gs["prev_wq"]
            tok_hbc = max_stat_bcast(hacc) if RUN_GU else None
            if KSUB == 2:
                tok_final = tok_hbc

            qts0 = []
            prev_hq = tok_xq
            q2h = [None, None]
            q3h = [prev_wq, prev_wq]
            HFS = (4096, 5632)
            HIQ = (1536, 1024)
            if KSUB >= 4:
                for i in range(NI):
                    sl = i % 2
                    fsl, iql = HFS[sl], HIQ[sl]
                    hz = ZT[:, i * 512:(i + 1) * 512]
                    hl = g_dma(hz, dh[ts(i), 0:512],
                               waits=[tok_hbc, h_done])
                    q1 = C.emit("v", lambda e, hz=hz, fsl=fsl: e.tensor_mul(
                        Fs[:, fsl:fsl + 512], hz, Fs[:, 0:512]),
                        waits=[hl, q2h[sl]])
                    q2 = C.emit("s", lambda e, fsl=fsl, iql=iql: e.activation(
                        IQ[:, iql:iql + 512],
                        Fs[:, fsl:fsl + 512], AF.Copy, bias=RB, scale=1.0),
                        waits=[q1, q3h[sl]])
                    q2h[sl] = q2
                    q3 = C.emit("v", lambda e, hz=hz, iql=iql: e.tensor_scalar_add(
                        hz, IQ[:, iql:iql + 512], -256.0), waits=[q2])
                    q3h[sl] = q3
                    qts0.append(q3)
                prev_hq = qts0[-1]

            l1pld = w2st[-1] if w2st else None
            l1acc = None
            l1stats = None
            prev_ev = None
            prev_wst = [(gs["prev_trt"]["g"], gs["prev_trt"]["u"], tok_hbc),
                        (h_done, tok_hbc)]
            last_mm_prev_chunk = None
            bank_rd = [None] * 4
            prev_d2 = None
            for c in range(4 if KSUB >= 4 else 0):
                if c == 0:
                    qts = qts0
                else:
                    qts = []
                    for i in range(NI):
                        sl = i % 2
                        fsl, iql = HFS[sl], HIQ[sl]
                        hz = ZT[:, i * 512:(i + 1) * 512]
                        hl = g_dma(hz, dh[ts(i), c * 512:c * 512 + 512],
                                   waits=[h_done, last_mm_prev_chunk])
                        q1 = C.emit("v", lambda e, hz=hz, c=c, fsl=fsl:
                            e.tensor_mul(
                            Fs[:, fsl:fsl + 512], hz,
                            Fs[:, c * 512:c * 512 + 512]),
                            waits=[hl, q2h[sl]])
                        q2 = C.emit("s", lambda e, fsl=fsl, iql=iql: e.activation(
                            IQ[:, iql:iql + 512],
                            Fs[:, fsl:fsl + 512], AF.Copy, bias=RB, scale=1.0),
                            waits=[q1, q3h[sl]])
                        q2h[sl] = q2
                        q3 = C.emit("v", lambda e, hz=hz, iql=iql:
                            e.tensor_scalar_add(
                            hz, IQ[:, iql:iql + 512], -256.0), waits=[q2])
                        q3h[sl] = q3
                        qts.append(q3)
                    prev_hq = qts[-1]
                for m in range(NT):
                    sb = m % 2
                    wl = y_trdma(bass.AP(U[:, :].tensor, sb * 6144,
                                         [[12288, 128], [128, NI], [1, 128]]),
                                 dw2q[ts(m), :],
                                 waits=[prev_wst[sb], w2st[m] if w2st else None])
                    if sb == 0:
                        last_wl0 = wl
                    else:
                        last_wl = wl
                    mm = None
                    for k in range(NI):
                        mm = C.emit("t", lambda e, m=m, k=k, sb=sb: e.matmul(
                            PA[:, ts(m % 4, 512)],
                            U[:, sb * 6144 + k * 128:sb * 6144 + k * 128 + 128],
                            ZT[:, k * 512:(k + 1) * 512],
                            start=(k == 0), stop=(k == NI - 1)),
                            waits=[wl, qts[NI - 1], bank_rd[m % 4]])
                    prev_wst[sb] = mm
                    last_mm_prev_chunk = mm
                    d1 = C.emit("s", lambda e, m=m: e.activation(Fs[:, 4608:5120],
                                PA[:, ts(m % 4, 512)], AF.Copy, bias=0.0,
                                scale=VS2[:, m:m + 1]), waits=[mm, prev_d2])
                    bank_rd[m % 4] = d1
                    d2 = C.emit("v", lambda e, c=c: e.tensor_mul(Fs[:, 4608:5120],
                                Fs[:, 4608:5120],
                                Fs[:, 2048 + c * 512:2048 + c * 512 + 512]),
                                waits=[d1])
                    prev_d2 = d2
                    d3 = C.emit("v", lambda e, m=m, c=c: e.tensor_add(
                        Fs[:, 5120:5632], Fs[:, 4608:5120],
                        XT[:, m * S + c * 512:m * S + c * 512 + 512]),
                        waits=[d2, prev_ev])
                    prev_ev = C.emit("s", lambda e, m=m, c=c: e.copy(
                        XT[:, m * S + c * 512:m * S + c * 512 + 512],
                        Fs[:, 5120:5632]), waits=[d3])
                if STAGE >= 4:
                    for m4 in range(c * 4, c * 4 + 4):
                        l1ld = g_dma(ZT[:, 22528:24576], dl1w[ts(m4), :],
                                     waits=[l1pld])
                        l1rd = C.emit("v", lambda e: e.reduce_sum(
                            SM[:, 52:53], ZT[:, 22528:24576], AX,
                            apply_absolute_value=True), waits=[l1ld, l1acc])
                        l1pld = l1rd
                        if m4 == 0:
                            l1acc = C.emit("v", lambda e: e.tensor_copy(
                                SM[:, 53:54], SM[:, 52:53]), waits=[l1rd])
                        else:
                            l1acc = C.emit("v", lambda e: e.tensor_add(
                                SM[:, 53:54], SM[:, 53:54], SM[:, 52:53]),
                                waits=[l1rd])
                    l1stats = l1acc
            if KSUB >= 4:
                tok_x2 = ln_feature_major(XT, XT, 2, prev_ev, u_guard=prev_wst[1])
                tok_final = tok_x2
                guard2 = (tok_x2, prev_wst[0], prev_wst[1], last_wl, last_wl0,
                          l1stats)


        if STAGE >= 3 and KSUB3 >= 1:
            prev_mm = guard2
            for g in range(NT):
                sq = U[:, ts(g % 2, 2048)]
                tsq = C.emit("s", lambda e, a=sq, g=g: e.activation(
                    a, XT[:, ts(g, S)], AF.Square, bias=0.0, scale=1.0),
                    waits=[tok_x2, prev_mm])
                for c in range(4):
                    prev_mm = C.emit("t", lambda e, g=g, c=c, sq=sq: e.matmul(
                        PA[0:1, ts(c, 512)], CD[:, :], sq[:, ts(c, 512)],
                        start=(g == 0), stop=(g == NT - 1)), waits=[tsq])
            r3 = C.emit("s", lambda e: e.activation(RWS[0:1, :], PA[0:1, 0:2048],
                        AF.Ln, bias=CV[0:1, 3:4], scale=1.0), waits=[prev_mm])
            r4 = C.emit("s", lambda e: e.activation(RWS[0:1, :], RWS[0:1, :], AF.Exp,
                        bias=0.0, scale=CV[0:1, 4:5]), waits=[r3])
            last = None
            for c in range(4):
                last = C.emit("t", lambda e, c=c: e.matmul(PA[:, ts(c, 512)], C1[:, :],
                              RWS[0:1, ts(c, 512)], start=True, stop=True), waits=[r4])
            tok_rrms = C.emit("s", lambda e: e.copy(Fs[:, 0:2048], PA[:, 0:2048]),
                              waits=[last])
            prev_stage = (prev_mm, tok_rrms) + guard2
            tok_cos = y_dma(Fs[:, 2048:4096], dcos, waits=[tok_x2])
            tok_sin = y_dma(Fs[:, 4096:6144], dsin, waits=[tok_x2])
            vrms_ap = bass.AP(VRMS[:, :].tensor, 0, [[NT, 128], [1, NT], [0, 128]])
            stage3 = bass.AP(U[:, :].tensor, 0, [[12288, 128], [128, 16], [1, 128]])

            prev_ostrip = None
            prev_vt = None
            prev_exp = [None] * 4
            prev_oev = None
            prev_rope = None
            oev_bank = [None] * 4
            bc_prev = None
            blk_free = [prev_stage] * NT
            for g in range(NG if KSUB3 >= 4 else 1):
                qtiles = []
                prev_ev_sub = None
                for sub in range(3):
                    sck = []
                    for k in range(NT):
                        dstb = U[:, k * 128:(k + 1) * 128]
                        if sub < 2:
                            col0 = g * 256 + sub * 128
                            lds = [g_dma(dstb,
                                   bass.AP(dwinT.tensor, col0 + k * 128 * CW,
                                           [[CW, 128], [1, 128]]),
                                   waits=[blk_free[k]])]
                        else:
                            lds = [g_dma(U[:, k * 128:k * 128 + 64],
                                   bass.AP(dwinT.tensor, 2048 + g * 64 + k * 128 * CW,
                                           [[CW, 128], [1, 64]]),
                                   waits=[blk_free[k]]),
                                   g_dma(U[:, k * 128 + 64:(k + 1) * 128],
                                   bass.AP(dwinT.tensor, 2560 + g * 64 + k * 128 * CW,
                                           [[CW, 128], [1, 64]]),
                                   waits=[blk_free[k]])]
                        vrb = bass.AP(VRMS[:, :].tensor, k, [[NT, 128], [0, 128]])
                        sck.append(C.emit("v", lambda e, dstb=dstb, vrb=vrb:
                            e.tensor_tensor(dstb, dstb, vrb, ALU.mult),
                            waits=lds + [tCl]))
                    mm = None
                    for k in range(NT):
                        for c in range(4):
                            mm = C.emit("t", lambda e, c=c, k=k: e.matmul(
                                PA[:, ts(c, 512)], U[:, ts(k)],
                                XT[:, k * S + 512 * c:k * S + 512 * c + 512],
                                start=(k == 0), stop=(k == NT - 1)),
                                waits=[sck[k], tok_x2, prev_rope, prev_ev_sub])
                        blk_free[k] = mm
                    prev_stage = mm
                    dst = U[:, 2048 + sub * 2048:4096 + sub * 2048]
                    ev = C.emit("v", lambda e, dst=dst: e.tensor_mul(
                        dst, PA[:, 0:2048], Fs[:, 0:2048]), waits=[mm, tok_rrms])
                    prev_ev_sub = ev
                    qtiles.append((dst, ev))
                rope_toks = [prev_ev_sub]
                for ri in range(3 if KSUB3 >= 2 else 0):
                    dst, ev = qtiles[ri]
                    nrow = 128 if ri < 2 else 64
                    pm = None
                    for c in range(4):
                        pm = C.emit("t", lambda e, c=c, dst=dst, nrow=nrow: e.matmul(
                            PA[0:nrow, ts(c, 512)], CP[0:nrow, 0:nrow],
                            dst[0:nrow, ts(c, 512)], start=True, stop=True),
                            waits=[ev, tCl, prev_rope, rope_toks[-1]])
                        a1 = C.emit("v", lambda e, c=c, dst=dst, nrow=nrow:
                                    e.tensor_mul(PB[0:nrow, ts(c, 512)],
                                    dst[0:nrow, ts(c, 512)],
                                    Fs[0:nrow, 2048 + c * 512:2048 + c * 512 + 512]),
                                    waits=[ev, tok_cos, prev_oev])
                        a2 = C.emit("v", lambda e, c=c, nrow=nrow: e.tensor_mul(
                            Fs[0:nrow, 0:512] if False else U[0:nrow, 8192 + c * 512:8192 + c * 512 + 512],
                            PA[0:nrow, ts(c, 512)],
                            Fs[0:nrow, 4096 + c * 512:4096 + c * 512 + 512]),
                            waits=[pm, tok_sin])
                        a3 = C.emit("v", lambda e, c=c, dst=dst, nrow=nrow:
                                    e.tensor_add(dst[0:nrow, ts(c, 512)],
                                    PB[0:nrow, ts(c, 512)],
                                    U[0:nrow, 8192 + c * 512:8192 + c * 512 + 512]),
                                    waits=[a1, a2])
                        rope_toks.append(a3)
                    prev_rope = rope_toks[-1]
                kvt, _ = qtiles[2]
                vt0 = 10240
                if KSUB3 >= 3:
                    ttr = y_trdma(bass.AP(U[:, :].tensor, 8192,
                                          [[12288, 128], [128, 16], [1, 128]]),
                                  kvt[:, :], waits=[qtiles[2][1], prev_rope, prev_vt])
                    vts = []
                    for c in range(16):
                        sv = C.emit("s", lambda e, c=c: e.copy(
                            U[:, vt0 + c * VTS:vt0 + c * VTS + 64],
                            U[:, 8192 + c * 128 + 64:8192 + c * 128 + 128]), waits=[ttr])
                        vts.append(sv)
                    on = C.emit("v", lambda e: e.memset(
                        bass.AP(U[:, :].tensor, vt0 + 64, [[12288, 128], [VTS, 16]]), 1.0),
                        waits=[vts[-1], prev_vt])
                    tok_vt = on
                    prev_vt = on
                    kdup = g_dma(kvt[64:128, :], kvt[0:64, :], waits=[ttr, prev_rope])
                for sp in range(4 if KSUB3 >= 3 else 0):
                    q0 = sp * 512
                    nk = sp * 4 + 4
                    mmq = {}
                    av_last = {}
                    for sk in range(nk):
                        for h in range(4):
                            qt = U[:, 2048 + (h // 2) * 2048:4096 + (h // 2) * 2048]
                            hb = 64 * (h % 2)
                            mv = qt[hb:hb + 64, q0:q0 + 512]
                            mmq[h] = C.emit("t", lambda e, h=h, sk=sk, mv=mv,
                                            kvt=kvt, hb=hb:
                                            e.matmul(PA[:, ts(h, 512)],
                                            kvt[hb:hb + 64, ts(sk)], mv,
                                            start=True, stop=True),
                                            waits=[prev_rope, tok_vt, kdup,
                                                   prev_exp[h]])
                        msk = None
                        if sk >= sp * 4:
                            off = sk * 128 - q0
                            for h in range(4):
                                msk = C.emit("v", lambda e, h=h, off=off: e.tensor_add(
                                    PA[:, h * 512 + off:h * 512 + off + 128],
                                    PA[:, h * 512 + off:h * 512 + off + 128],
                                    CM[:, :]), waits=[mmq[h], tCl])
                        for h in range(4):
                            prev_exp[h] = C.emit("s", lambda e, h=h: e.activation(
                                U[:, 8192 + h * 512:8192 + h * 512 + 512],
                                PA[:, ts(h, 512)], AF.Exp, bias=CV[:, 5:6],
                                scale=CV[:, 6:7]),
                                waits=[mmq[h], msk, oev_bank[h]])
                            if sk > sp * 4:
                                off = sk * 128 - q0
                                prev_exp[h] = C.emit("v", lambda e, h=h, off=off:
                                    e.memset(U[:, 8192 + h * 512:
                                               8192 + h * 512 + off], 0.0),
                                    waits=[prev_exp[h]])
                        for h in range(4):
                            av_last[h] = C.emit("t", lambda e, h=h, sk=sk, nk=nk:
                                e.matmul(
                                PB[0:65, ts(h, 512)],
                                U[:, vt0 + sk * VTS:vt0 + sk * VTS + 65],
                                U[:, 8192 + h * 512:8192 + h * 512 + 512],
                                start=(sk == 0), stop=(sk == nk - 1)),
                                waits=[prev_exp[h]] +
                                      ([oev_bank[h]] if sk == 0 else []))
                    otoks = []
                    rr = C.emit("v", lambda e: e.reciprocal(
                        RWS[0:1, 0:2048], PB[64:65, 0:2048]),
                        waits=[av_last[0], av_last[1], av_last[2], av_last[3],
                               bc_prev])
                    for h in range(4):
                        head = 4 * g + h
                        bc = C.emit("t", lambda e, h=h: e.matmul(
                                    PB[64:128, ts(h, 512)],
                                    C1[0:1, 0:64], RWS[0:1, ts(h, 512)],
                                    start=True, stop=True), waits=[rr])
                        bc_prev = bc
                        oslot = U[0:64, 8192 + h * 512:8192 + h * 512 + 512]
                        oe = C.emit("s", lambda e, h=h, oslot=oslot: e.copy(
                            oslot, PB[0:64, ts(h, 512)]), waits=[av_last[h]])
                        zslot = ZT[64 * (head % 2):64 * (head % 2) + 64,
                                   (head // 2) * S + q0:(head // 2) * S + q0 + 512]
                        ov = C.emit("v", lambda e, h=h, zslot=zslot, oslot=oslot:
                                    e.tensor_mul(zslot, oslot,
                                                 PB[64:128, ts(h, 512)]),
                                    waits=[oe, bc])
                        oev_bank[h] = (oe, ov)
                        otoks.extend([oe, ov])
                    prev_oev = tuple(otoks)
            prev_ev3 = None
            ev3_bank = [None, None]
            cp_last = [None, None]
            wst3 = [None, None]
            for m in range(NT if KSUB3 >= 5 else 0):
                sb3 = m % 2
                stage3m = bass.AP(U[:, :].tensor, sb3 * 2048,
                                  [[12288, 128], [128, 16], [1, 128]])
                wl = g_dma(stage3m, bass.AP(dwoutT.tensor, m * 128,
                           [[D, 128], [128 * D, 16], [1, 128]]),
                           waits=[prev_stage, prev_oev, wst3[sb3]])
                PM = PA if sb3 == 0 else PB
                mm = None
                for c in range(4):
                    for k in range(NT):
                        mm = C.emit("t", lambda e, c=c, k=k, PM=PM, sb3=sb3:
                            e.matmul(
                            PM[:, ts(c, 512)],
                            U[:, sb3 * 2048 + k * 128:sb3 * 2048 + k * 128 + 128],
                            ZT[:, k * S + 512 * c:k * S + 512 * c + 512],
                            start=(k == 0), stop=(k == NT - 1)),
                            waits=[wl, prev_oev, ev3_bank[sb3]])
                prev_stage = mm
                wst3[sb3] = mm
                last_v = None
                for c in range(4):
                    v1_ = C.emit("v", lambda e, m=m, c=c, PM=PM, sb3=sb3:
                        e.scalar_tensor_tensor(
                        Fs[:, sb3 * 2048 + c * 512:sb3 * 2048 + c * 512 + 512],
                        XT[:, m * S + c * 512:m * S + c * 512 + 512], 2.0,
                        PM[:, ts(c, 512)], ALU.mult, ALU.add),
                        waits=[mm, cp_last[sb3] if c == 0 else None])
                    prev_ev3 = C.emit("s", lambda e, m=m, c=c, sb3=sb3: e.copy(
                        XT[:, m * S + c * 512:m * S + c * 512 + 512],
                        Fs[:, sb3 * 2048 + c * 512:sb3 * 2048 + c * 512 + 512]),
                        waits=[v1_])
                    last_v = v1_
                ev3_bank[sb3] = last_v
                cp_last[sb3] = prev_ev3
            if KSUB3 >= 5:
                tok_x3 = ln_feature_major(XT, XT, 2, prev_ev3, u_guard=prev_stage)
                tok_final = tok_x3

        if STAGE >= 4:
            gs = C.emit("t", lambda e: e.matmul(PA[0:1, 0:1], CK[:, :],
                        SM[:, 53:54], start=True, stop=True),
                        waits=[l1stats, tCl, tok_x3])
            g1_ = C.emit("s", lambda e: e.mul(RWS[0:1, 0:1], PA[0:1, 0:1],
                         1.0 / (D * D)), waits=[gs])
            g2_ = C.emit("v", lambda e: e.tensor_scalar_add(RWS[0:1, 1:2],
                         RWS[0:1, 0:1], L1_EPS), waits=[g1_])
            g3_ = C.emit("v", lambda e: e.reciprocal(RWS[0:1, 2:3], RWS[0:1, 1:2]),
                         waits=[g2_])
            bc1 = C.emit("t", lambda e: e.matmul(PB[:, 0:1], C1[0:1, :],
                         RWS[0:1, 2:3], start=True, stop=True), waits=[g3_])
            bc2 = C.emit("t", lambda e: e.matmul(PB[:, 512:513], C1[0:1, :],
                         RWS[0:1, 0:1], start=True, stop=True), waits=[g3_])
            sv1 = C.emit("s", lambda e: e.copy(VSL[:, 0:1], PB[:, 0:1]), waits=[bc1])
            sv2 = C.emit("s", lambda e: e.copy(VSL[:, 1:2], PB[:, 512:513]), waits=[bc2])
            prev_rq4 = [None, None]
            prev_tr4 = [None, None]
            prev_t1 = None
            prev_z4 = None
            prev_mm4 = [None, None]
            psv_bank = [None, None]
            prev_cp4 = [None, None]
            for m in range(NT):
                sb4 = m % 2
                nat = U[:, sb4 * 2048:sb4 * 2048 + 2048]
                ld = g_dma(nat, dl1w[ts(m), :],
                           waits=[prev_rq4[sb4], prev_tr4[sb4], sv1, sv2])
                rq = C.emit("s", lambda e, nat=nat: e.activation(IQ[:, :], nat,
                            AF.Copy, bias=RB, scale=VSL[:, 0:1]),
                            waits=[ld, prev_t1])
                t1_ = C.emit("v", lambda e: e.tensor_scalar(Fs[:, 4096:6144],
                             IQ[:, :], -(RB), -1.0, ALU.add, ALU.max), waits=[rq])
                prev_t1 = t1_
                t2_ = C.emit("v", lambda e, nat=nat: e.tensor_scalar(
                    nat, Fs[:, 4096:6144], 1.0, 1.0, ALU.min, ALU.mult),
                    waits=[t1_])
                prev_rq4[sb4] = t2_
                so4 = 4096 + sb4 * 2048
                ot = bass.AP(U[:, :].tensor, so4, [[12288, 128], [128, 16], [1, 128]])
                tr = y_trdma(ot, nat, waits=[t2_, prev_mm4[sb4]])
                prev_tr4[sb4] = tr
                PM = PA if sb4 == 0 else PB
                mm = None
                for c in range(4):
                    for k in range(NT):
                        mm = C.emit("t", lambda e, c=c, k=k, PM=PM, so4=so4:
                            e.matmul(
                            PM[:, ts(c, 512)],
                            U[:, so4 + k * 128:so4 + k * 128 + 128],
                            XT[:, k * S + 512 * c:k * S + 512 * c + 512],
                            start=(k == 0), stop=(k == NT - 1)),
                            waits=[tr, psv_bank[sb4]])
                prev_mm4[sb4] = mm
                for c in range(4):
                    sv = C.emit("s", lambda e, c=c, PM=PM, sb4=sb4: e.activation(
                        Fs[:, sb4 * 2048 + c * 512:sb4 * 2048 + c * 512 + 512],
                        PM[:, ts(c, 512)], AF.Copy, bias=0.0, scale=VSL[:, 1:2]),
                        waits=[mm, prev_cp4[sb4] if c == 0 else None])
                    prev_z4 = C.emit("v", lambda e, m=m, c=c, sb4=sb4:
                        e.scalar_tensor_tensor(
                        ZT[:, m * S + c * 512:m * S + c * 512 + 512],
                        Fs[:, sb4 * 2048 + c * 512:sb4 * 2048 + c * 512 + 512],
                        VL1B[:, m:m + 1],
                        XT[:, m * S + c * 512:m * S + c * 512 + 512],
                        ALU.add, ALU.add), waits=[sv])
                    psv_bank[sb4] = sv
                prev_cp4[sb4] = prev_z4
            tok_x4 = ln_feature_major(ZT, XT, 2, prev_z4,
                                      u_guard=(prev_tr4[0], prev_tr4[1]))
            tok_final = tok_x4

        out_done = []
        prev_tt = [None, None]
        ln_toks = getattr(C, "ln_tile_toks", [None] * NT)
        if len(ln_toks) < NT:
            ln_toks = [tok_final] * NT
        ug4 = (prev_tr4[0], prev_tr4[1]) if STAGE >= 4 else tok_final
        for j in range(16 if not KPROBE else 0):
            sb = j % 2
            stg = U[:, sb * 2048:sb * 2048 + 2048]
            tt = y_trdma(bass.AP(U[:, :].tensor, sb * 2048,
                                 [[12288, 128], [128, 16], [1, 128]]),
                         XT[:, ts(j, S)],
                         waits=[ln_toks[j], prev_tt[sb],
                                ug4 if j < 2 else None])
            dst_ap = bass.AP(dout.tensor, j * 128,
                             [[D, 128], [128 * D, 16], [1, 128]])
            st = C.dma("g", lambda e, d=dst_ap, s=stg: e.dma_start(d, s[:, :]),
                       waits=[tt], queue="o")
            prev_tt[sb] = st
            out_done.append(st)

        final_o = (C.cnt["o0"], C.cnt["o1"])

        def mk(ename):
            def body(eng):
                for fn, waits, inc, n in C.ops[ename]:
                    for (snm, v) in waits:
                        eng.wait_ge(sems[snm], v)
                    ins = fn(eng)
                    if n:
                        ins.then_inc(sems[inc], n)
                if ename == "y":
                    eng.wait_ge(sems["o0"], final_o[0])
                    eng.wait_ge(sems["o1"], final_o[1])
            return body

        block.tensor(mk("t"))
        block.vector(mk("v"))
        block.scalar(mk("s"))
        block.gpsimd(mk("g"))
        block.sync(mk("y"))

    return nc


def _rope_tables():
    j = np.arange(0, DQK, 2, dtype=np.float32)
    inv = (np.float32(1.0) / np.power(np.float32(10000.0),
           (j / np.float32(DQK)).astype(np.float32))).astype(np.float32)
    ang = (np.arange(S, dtype=np.float32)[:, None] * inv[None, :]).astype(np.float32)
    c = np.cos(ang).astype(np.float32)
    s = np.sin(ang).astype(np.float32)
    rows = np.arange(128) % 32
    return np.ascontiguousarray(c[:, rows].T), np.ascontiguousarray(s[:, rows].T)


def _vec_tiles(v):
    return np.ascontiguousarray(np.asarray(v, dtype=np.float32).reshape(NT, 128).T)


def make_inputs(L0_attn_x, L0_attn_Wa, L0_attn_Wx, L0_attn_ba, L0_attn_bx,
                L0_attn_Lam, L0_ffn_w_gate_up, L0_ffn_w_down, L1_attn_w_in,
                L1_attn_w_out, L1_attn_rms_w, L1_ffn_weight, L1_ffn_bias):
    cosT, sinT = _rope_tables()
    P = np.zeros((128, 128), dtype=np.float32)
    for h in range(2):
        for jj in range(32):
            P[h * 64 + jj + 32, h * 64 + jj] = -1.0
            P[h * 64 + jj, h * 64 + jj + 32] = 1.0
    mask = np.where(np.arange(128)[:, None] > np.arange(128)[None, :],
                    np.float32(-1e5), np.float32(0.0)).astype(np.float32)
    shared = dict(
        waT=np.ascontiguousarray(np.asarray(L0_attn_Wa, dtype=np.float32).T),
        wxT=np.ascontiguousarray(np.asarray(L0_attn_Wx, dtype=np.float32).T),
        ba2=_vec_tiles(L0_attn_ba), bx2=_vec_tiles(L0_attn_bx),
        lam=_vec_tiles(L0_attn_Lam),
        w1=np.ascontiguousarray(L0_ffn_w_gate_up, dtype=np.float32),
        w2=np.ascontiguousarray(L0_ffn_w_down, dtype=np.float32),
        winT=np.ascontiguousarray(np.asarray(L1_attn_w_in, dtype=np.float32).T),
        woutT=np.ascontiguousarray(np.asarray(L1_attn_w_out, dtype=np.float32).T),
        rms2=_vec_tiles(L1_attn_rms_w),
        l1w=np.ascontiguousarray(L1_ffn_weight, dtype=np.float32),
        l1b2=_vec_tiles(L1_ffn_bias),
        identb=np.eye(128, dtype=BF), identf=np.eye(128, dtype=np.float32),
        prope=P.astype(BF), mask=mask, cosT=cosT, sinT=sinT,
        ones1=np.ones((1, 128), dtype=np.float32),
        onesd=np.full((128, 1), 1.0 / D, dtype=BF),
        onesk=np.ones((128, 1), dtype=np.float32),
        cvec=np.tile(np.array([LOG_HALF, RB, LN_EPS, RMS_EPS, -0.5, -20.0,
                               0.125, 0.0], dtype=np.float32), (128, 1)),
    )
    xs = np.asarray(L0_attn_x, dtype=np.float32)
    return [dict(shared, x=np.ascontiguousarray(xs[c])) for c in range(NCORE)]


def kernel(**inputs):
    nc = build_program()
    in_maps = make_inputs(**inputs)
    res = run_bass_kernel_spmd(nc, in_maps, core_ids=list(range(NCORE)))
    return np.stack([np.asarray(r["out"], dtype=np.float32) for r in res.results])
